# version 28
# baseline (speedup 1.0000x reference)
import collections
import math
from contextlib import ExitStack

import numpy as np
import ml_dtypes
import concourse.bass as bass
import concourse.mybir as mybir
from concourse.bass_utils import run_bass_kernel_spmd

F32 = mybir.dt.float32
BF16 = mybir.dt.bfloat16
AF = mybir.ActivationFunctionType
ALU = mybir.AluOpType
AX = mybir.AxisListType

N_CORES = 8
P = 128
SBMAP = []


class Cfg:
    def __init__(self, S=2048, D=2048, HD=128, NQ=16, NKV=4, HF=5632, NB=2,
                 MAX_POS=4096, THETA=10000.0):
        self.S, self.D, self.HD, self.NQ, self.NKV, self.HF, self.NB = S, D, HD, NQ, NKV, HF, NB
        self.MAX_POS, self.THETA = MAX_POS, THETA
        self.KV = NKV * HD
        self.GRP = NQ // NKV
        self.TT = S // P
        self.HTT = self.TT // 2
        self.SH = S // 2
        self.DC = D // P
        self.HC = HF // P
        self.QGT = min(4, self.TT)
        self.NG = self.TT // self.QGT
        self.QGW = self.QGT * P
        self.DQ = max(1, D // 512)
        self.CGW = 256
        self.CGN = (D + 2 * self.KV) // self.CGW
        self.QCG = D // self.CGW
        self.KCG = self.KV // self.CGW
        self.NH = self.CGW // HD
        self.SCALE = 1.0 / math.sqrt(HD)
        self.CSHIFT = 30.0
        self.EPS = 1e-5
        self.QEPS = 1e-8
        self.GCH = 512
        assert self.KV % self.CGW == 0 and D % 512 == 0 and self.HTT <= 8
        assert self.SH % 512 == 0 and self.TT % self.QGT == 0
        assert self.D % self.GCH == 0
        assert (self.HF * self.D) % (128 * self.GCH) == 0


def host_constants(cfg):
    HD = cfg.HD
    k = np.arange(0, HD, 2)[:HD // 2].astype(np.float32)
    inv_freq = 1.0 / cfg.THETA ** (k / HD)
    positions = np.arange(cfg.MAX_POS, dtype=np.float32)
    angles = positions[:, None] * inv_freq[None, :]
    angles = np.concatenate([angles, angles], axis=-1)
    cos = np.cos(angles)[:cfg.S].astype(np.float32)
    sin = np.sin(angles)[:cfg.S].astype(np.float32)
    ssin = np.concatenate([-sin[:, :HD // 2], sin[:, HD // 2:]], axis=1)
    cos = np.ascontiguousarray(cos.T)
    ssin = np.ascontiguousarray(ssin.T)
    ident = np.eye(P, dtype=ml_dtypes.bfloat16)
    QGT = cfg.QGT
    maskv = np.zeros((QGT, P, QGT * P), dtype=np.float32)
    for i in range(QGT):
        for j in range(QGT):
            blk = maskv[i, :, j * P:(j + 1) * P]
            if j < i:
                blk[:] = -1e30
            elif j == i:
                pp, cc = np.meshgrid(np.arange(P), np.arange(P), indexing="ij")
                blk[pp > cc] = -1e30
    return {"cos_c": cos, "ssin_c": ssin, "ident_c": ident,
            "maskv_c": maskv.astype(ml_dtypes.bfloat16)}


ENGINES = ("sync", "gpsimd", "vector", "scalar", "tensor")


class B:

    _SELF = {"vector": "V", "scalar": "A", "gpsimd": "G"}

    def __init__(self, nc, stack):
        self.nc = nc
        self.stack = stack
        self.st = {e: [] for e in ENGINES}
        self.sem = {}
        self.hw = {e: {} for e in ENGINES}
        for name in ("T", "V", "A", "G"):
            h = stack.enter_context(nc.semaphore(name))
            self.sem[name] = [h, 0]
        self.ln_guard = [None, None]
        self.ln_i = 0

    def emit(self, eng, fn, inc=None, k=1):
        nm = self._SELF.get(eng)
        if nm is not None:
            s = self.sem[nm]
            self.wait(eng, (nm, s[1]))
            s[1] += 1
            tgt = s[1]
            self.st[eng].append(lambda e, fn=fn, h=s[0]: fn(e).then_inc(h, 1))
            return (nm, tgt)
        if inc is not None:
            s = self.sem[inc]
            s[1] += k
            tgt = s[1]
            self.st[eng].append(lambda e, fn=fn, h=s[0], k=k: fn(e).then_inc(h, k))
            return (inc, tgt)
        self.st[eng].append(fn)
        return None

    def dma(self, eng, out_ap, in_ap, ch="d0", **kw):
        nm = "D" + ch
        if nm not in self.sem:
            h = self.stack.enter_context(self.nc.semaphore(nm))
            self.sem[nm] = [h, 0]
        s = self.sem[nm]
        self.wait(eng, (nm, s[1]))
        s[1] += 16
        tgt = s[1]
        self.st[eng].append(
            lambda e, o=out_ap, i=in_ap, h=s[0], kw=kw: e.dma_start(o, i, **kw).then_inc(h, 16))
        return (nm, tgt)

    def dma_total(self, ch):
        nm = "D" + ch
        return (nm, self.sem[nm][1]) if nm in self.sem else None

    def final_dma_waits(self, eng):
        for nm, s in self.sem.items():
            if nm.startswith("D"):
                self.wait(eng, (nm, s[1]))

    def wait(self, eng, tok):
        if tok is None:
            return
        name, v = tok
        if self.hw[eng].get(name, 0) >= v:
            return
        self.hw[eng][name] = v
        h = self.sem[name][0]
        self.st[eng].append(lambda e, h=h, v=v: e.wait_ge(h, v))

    def barrier(self):
        toks = [(n, s[1]) for n, s in self.sem.items() if s[1] > 0]
        for e in ENGINES:
            for t in toks:
                self.wait(e, t)

    def run(self, block):
        @block.sync
        def _(e):
            for fn in self.st["sync"]:
                fn(e)

        @block.gpsimd
        def _(e):
            for fn in self.st["gpsimd"]:
                fn(e)

        @block.vector
        def _(e):
            for fn in self.st["vector"]:
                fn(e)

        @block.scalar
        def _(e):
            for fn in self.st["scalar"]:
                fn(e)

        @block.tensor
        def _(e):
            for fn in self.st["tensor"]:
                fn(e)


def build(nc, ins, out, cfg):
    c = cfg
    stack = ExitStack()
    b = B(nc, stack)

    PS = stack.enter_context(nc.psum_tensor("PS", [P, 4096], F32))

    def psf(bank, width):
        return PS[:, bank * 512:bank * 512 + width]

    def psbf(bank, col0, width=P):
        return PS[:].bitcast(BF16)[:, bank * 1024 + col0:bank * 1024 + col0 + width]

    dt = nc.dram_tensor
    wq_bf = dt("wq_bf", [c.D, c.D], BF16, kind="Internal").ap()
    wk_bf = dt("wk_bf", [c.KV, c.D], BF16, kind="Internal").ap()
    wv_bf = dt("wv_bf", [c.KV, c.D], BF16, kind="Internal").ap()
    wo_bf = dt("wo_bf", [c.D, c.D], BF16, kind="Internal").ap()
    w1T_h = dt("w1T_h", [c.HC, P, c.DC * P], BF16, kind="Internal").ap()
    w2T_h = dt("w2T_h", [c.HC, P, c.DC * P], BF16, kind="Internal").ap()
    w3T_h = dt("w3T_h", [c.DQ, c.HC, P, c.D // c.DQ], BF16, kind="Internal").ap()
    y_h = dt("y_h", [c.NB, c.S, c.D], F32, kind="Internal").ap()
    thr_h = dt("thr_h", [4, 1], F32, kind="Internal").ap()
    ln2T_h = dt("ln2T_h", [c.DC, P, c.S], BF16, kind="Internal").ap()
    ln2T_hp = ln2T_h.rearrange("dc p s -> p dc s")

    ARENA0 = 20480
    SB_LIMIT = nc.SBUF_PARTITION_SIZE_BYTES - 32
    ARENA = nc.alloc_sbuf_tensor_at("ARENA", [P, SB_LIMIT - ARENA0],
                                    mybir.dt.uint8, offset=ARENA0)
    off = [ARENA0]

    def sb(name, shape, dtype, at=None):
        o = off[0] if at is None else at
        esz = mybir.dt.size(dtype)
        n = int(np.prod(shape[1:]))
        assert (o - ARENA0) % esz == 0, (name, o)
        base = ARENA[:].bitcast(dtype)
        v = base[0:shape[0], (o - ARENA0) // esz:(o - ARENA0) // esz + n]
        if len(shape) == 3:
            v = v.rearrange("p (a bq) -> p a bq", a=shape[1])
        elif len(shape) == 4:
            v = v.rearrange("p (a bq cq) -> p a bq cq", a=shape[1], bq=shape[2])
        SBMAP.append((name, o, o + n * esz, shape, str(dtype)))
        off[0] = o + ((n * esz + 31) // 32) * 32
        return v

    ident = sb("ident", [P, P], BF16)
    cos_sb = sb("cos_sb", [P, c.S], BF16)
    ssin_sb = sb("ssin_sb", [P, c.S], BF16)
    maskv_sb = sb("maskv_sb", [P, c.QGT, c.QGW], BF16)
    ones_f = sb("ones_f", [1, P], F32)
    stA = sb("stA", [P, 16], F32)
    thr128 = sb("thr128", [P, 4], F32)
    nthr128 = sb("nthr128", [P, 4], F32)
    ngch = (2 * c.HF * c.D + c.D * c.HF) // (P * c.GCH)
    gpart = sb("gpart", [P, ngch + 8], F32)
    gsum = sb("gsum", [1, 8], F32)
    c_eps = sb("c_eps", [P, 1], F32)
    c_ncs = sb("c_ncs", [P, 1], F32)
    c_zero = sb("c_zero", [P, 1], F32)
    fdump = sb("fdump", [1, 64], F32)
    nc.const_aps.register(F32, c.EPS, c_eps)
    nc.const_aps.register(F32, -c.CSHIFT, c_ncs)
    nc.const_aps.register(F32, 0.0, c_zero)
    TA = off[0]

    wslA = sb("wslA", [P, c.DC, c.CGW], BF16, at=TA)
    WSLB_AT = off[0]
    wslB = sb("wslB", [P, c.DC, c.CGW], BF16)
    x_t0 = sb("x_t0", [P, c.D], F32)
    ln1bf = sb("ln1bf", [P, c.D], BF16)
    LN1TH_AT = off[0]
    ln1Th = sb("ln1Th", [P, c.DC, c.SH], BF16)
    RTMP_AT = off[0]
    rtmp = sb("rtmp", [P, c.SH], BF16)
    g3 = [sb(f"g3_{i}", [P, c.GCH], F32) for i in range(5)]
    qa_s = sb("qa_s", [P, c.GCH], BF16)
    qb_s = sb("qb_s", [P, c.GCH], BF16)
    wqh = sb("wqh", [P, 3, c.DC, P], BF16)
    slab = sb("slab", [P, c.DC * P], BF16)
    A_END = off[0]
    sq_scr = sb("sq_scr", [P, c.D], F32, at=TA)
    x_t0b = sb("x_t0b", [P, c.D], F32, at=WSLB_AT)
    assert c.D * 4 <= c.DC * c.CGW * 2
    assert c.D * 4 <= 2 * c.DC * c.CGW * 2

    off[0] = TA
    attnT = sb("attnT", [P, 2, c.TT, c.QGW], BF16, at=TA)
    o_nrm = sb("o_nrm", [P, 4, P], BF16)
    recip = sb("recip", [P, 2], F32)
    C_END = off[0]

    off[0] = TA
    woT = sb("woT", [P, c.DC, c.D], BF16, at=TA)
    x2_t = sb("x2_t", [P, c.D], F32)
    sq_d = sb("sq_d", [P, c.D], F32)
    ln2T_t = sb("ln2T_t", [P, c.DC, P], BF16)
    D_TA_END = off[0]

    TA_SZ = max(A_END, C_END, D_TA_END) - TA
    LB = TA + TA_SZ

    qT = sb("qT", [P, c.NQ, c.S], BF16, at=LB)
    oT = sb("oT", [P, c.NQ, c.S], BF16, at=LB)
    kT = sb("kT", [P, c.NKV, c.S], BF16)
    vaug = sb("vaug", [P, c.NKV, c.TT, c.HD + 1], BF16)
    LB_END = off[0]
    KTOFF = LB + c.NQ * c.S * 2
    off[0] = KTOFF
    atn_o = sb("atn_o", [P, c.D], F32, at=KTOFF)
    tmp_d = sb("tmp_d", [P, c.D], F32)
    y_t = sb("y_t", [P, c.D], F32)
    ln2bf = sb("ln2bf", [P, c.D], BF16)
    D_END = off[0]

    off[0] = TA
    ln2Th = sb("ln2Th", [P, c.DC, c.SH], BF16, at=TA)
    hidT = sb("hidT", [P, c.HC, c.SH], BF16)
    W1SL_AT = off[0]
    w1sl = sb("w1sl", [P, 2, c.DC, P], BF16)
    W2SL_AT = off[0]
    w2sl = sb("w2sl", [P, 2, c.DC, P], BF16)
    w3sl = sb("w3sl", [P, 2, c.D // c.DQ], BF16)
    sig = sb("sig", [P, min(c.SH, 1024)], F32)
    ffnbf = sb("ffnbf", [P, c.HTT, c.D], BF16)
    y_e = sb("y_e", [P, c.D], F32)
    E_END0 = off[0]
    sq_e = sb("sq_e", [P, c.D], F32, at=W1SL_AT)
    tmp_e = sb("tmp_e", [P, c.D], F32, at=W2SL_AT)
    off[0] = E_END0
    E_END = off[0]
    assert c.D * 4 <= 2 * c.DC * P * 2

    base_x = (max(C_END, LN1TH_AT) + 31) & ~31
    navail = (RTMP_AT - base_x) // (c.GCH * 4)
    NEXTRA = max(0, min(10, navail - ((c.DC * P * 2) + c.GCH * 4 - 1)
                        // (c.GCH * 4)))
    g3x = [sb(f"g3x{j}", [P, c.GCH], F32, at=base_x + j * c.GCH * 4)
           for j in range(NEXTRA)]
    slab2 = None
    if base_x + NEXTRA * c.GCH * 4 + c.DC * P * 2 <= RTMP_AT:
        slab2 = sb("slab2", [P, c.DC * P], BF16,
                   at=base_x + NEXTRA * c.GCH * 4)
    off[0] = E_END

    assert max(E_END, D_END, LB_END) <= SB_LIMIT, (
        "sbuf overflow", A_END, E_END, D_END, LB_END, SB_LIMIT)

    def fence(reads):
        toks = []
        for eng, ap, n in reads:
            base = 0 if eng == "scalar" else 32
            if eng == "scalar":
                toks.append(b.emit("scalar", lambda e, ap=ap, n=n, bo=base: e.copy(
                    fdump[0:1, bo:bo + n], ap)))
            else:
                toks.append(b.emit("vector",
                                   lambda e, ap=ap, n=n, bo=base: e.tensor_copy(
                                       fdump[0:1, bo:bo + n], ap)))
        return toks

    ST_SUM, ST_SQ, ST_MEAN, ST_M2, ST_VAR, ST_LNV, ST_RSTD = 0, 2, 4, 6, 8, 10, 12

    def ln_full(src_ap, dst_ap, sq_buf, width, extra_wait_v=None):
        s = b.ln_i % 2
        b.ln_i += 1
        g = b.ln_guard[s]
        b.wait("vector", g)
        b.wait("scalar", g)
        b.emit("vector", lambda e: e.reduce_sum(
            stA[:, ST_SUM + s:ST_SUM + s + 1], src_ap, axis=AX.X))
        atok = b.emit("scalar", lambda e: e.activation(
            sq_buf[:, 0:width], src_ap, AF.Square,
            accum_out=stA[:, ST_SQ + s:ST_SQ + s + 1]))
        b.wait("vector", atok)
        b.emit("vector", lambda e: e.tensor_scalar_mul(
            stA[:, ST_MEAN + s:ST_MEAN + s + 1],
            stA[:, ST_SUM + s:ST_SUM + s + 1], 1.0 / width))
        b.emit("vector", lambda e: e.tensor_mul(
            stA[:, ST_M2 + s:ST_M2 + s + 1],
            stA[:, ST_MEAN + s:ST_MEAN + s + 1],
            stA[:, ST_MEAN + s:ST_MEAN + s + 1]))
        vtok = b.emit("vector", lambda e: e.scalar_tensor_tensor(
            stA[:, ST_VAR + s:ST_VAR + s + 1], stA[:, ST_SQ + s:ST_SQ + s + 1],
            1.0 / width, stA[:, ST_M2 + s:ST_M2 + s + 1],
            ALU.mult, ALU.subtract))
        b.wait("scalar", vtok)
        b.emit("scalar", lambda e: e.activation(
            stA[:, ST_LNV + s:ST_LNV + s + 1], stA[:, ST_VAR + s:ST_VAR + s + 1],
            AF.Ln, bias=c.EPS))
        atok2 = b.emit("scalar", lambda e: e.activation(
            stA[:, ST_RSTD + s:ST_RSTD + s + 1], stA[:, ST_LNV + s:ST_LNV + s + 1],
            AF.Exp, scale=-0.5))
        b.wait("vector", atok2)
        b.wait("vector", extra_wait_v)
        tok = b.emit("vector", lambda e: e.tensor_scalar(
            dst_ap, src_ap, stA[:, ST_MEAN + s:ST_MEAN + s + 1],
            stA[:, ST_RSTD + s:ST_RSTD + s + 1], ALU.subtract, ALU.mult))
        b.ln_guard[s] = tok
        return tok

    b.dma("sync", ident[:], ins["ident_c"], ch="cst")
    b.dma("sync", maskv_sb[:], ins["maskv_c"].rearrange("a p bq -> p a bq"),
          ch="cst")
    b.dma("gpsimd", cos_sb[:], ins["cos_c"], ch="cos")
    b.dma("gpsimd", ssin_sb[:], ins["ssin_c"], ch="cos")
    b.emit("vector", lambda e: e.memset(ones_f[:], 1.0))
    b.emit("vector", lambda e: e.memset(c_eps[:], c.EPS))
    b.emit("vector", lambda e: e.memset(c_ncs[:], -c.CSHIFT))
    b.emit("vector", lambda e: e.memset(c_zero[:], 0.0))
    b.barrier()

    for ci, (w_in, w_sc, rows) in enumerate((
            (ins["wq"], wq_bf, c.D), (ins["wk"], wk_bf, c.KV),
            (ins["wv"], wv_bf, c.KV), (ins["wo"], wo_bf, c.D))):
        step = 256
        for r0 in range(0, rows, step):
            r1 = min(rows, r0 + step)
            b.dma("gpsimd", w_sc[r0:r1, :], w_in[r0:r1, :], ch=f"cast{ci}")
    qkv_cast = [b.dma_total(f"cast{i}") for i in range(4)]
    wo_cast = b.dma_total("cast3")

    QV = collections.deque()
    QVL = collections.deque()
    QT = collections.deque()
    LEAD = 4
    G3 = [(g3[i], f"g{i}") for i in range(5)]
    pst = {"rr": 0, "pend": collections.deque(), "gt": {}, "qc": 0,
           "sink_tok": [None, None], "gamma_last": None,
           "wq_rd": [None, None, None], "slab_st": [None, None]}
    pts_g = [psbf(i, 0) for i in range(2)]
    ptq_g = [psbf(2 + i, 0) for i in range(2)]
    pt_tok_g = [None, None]
    ptq_tok = [None, None]

    mats = [(ins["w1"], 0, w1T_h, False), (ins["w2"], 1, w2T_h, False),
            (ins["w3"], 2, w3T_h, True)]

    def gamma_load(w_in, r0):
        def fn():
            bid = pst["rr"] % len(G3)
            pst["rr"] += 1
            gb, ch = G3[bid]
            ie = ("sync" if bid % 2 else "gpsimd") \
                if pst.get("eng_pref") == "ALT" else "gpsimd"
            if ie == "sync":
                ch = ch + "s"
            b.wait(ie, pst["gt"].get(bid))
            b.dma(ie, gb[:],
                  w_in.rearrange("a bq -> (a bq)").rearrange(
                      "(n p x) -> n p x", p=P, x=c.GCH)[r0], ch=ch)
            pst["pend"].append((bid, ch))
        return fn

    def use_act():
        pref = pst.get("eng_pref", "A")
        if pref == "ALT":
            pst["alt"] = pst.get("alt", 0) + 1
            return pst["alt"] % 2 == 0
        return pref == "A"

    def gamma_comp(col):
        def fn():
            bid, ch = pst["pend"].popleft()
            gb, _ = G3[bid]
            if use_act():
                b.wait("scalar", b.dma_total(ch))
                b.wait("scalar", pst["sink_tok"][0])
                tok = b.emit("scalar", lambda e, gb=gb, col=col: e.activation(
                    qa_s[:], gb[:], AF.Abs, accum_out=gpart[:, col:col + 1]))
                pst["sink_tok"][0] = tok
                pst["gamma_A"] = tok
            else:
                b.wait("vector", b.dma_total(ch))
                tok = b.emit("vector", lambda e, gb=gb, col=col: e.reduce_sum(
                    gpart[:, col:col + 1], gb[:], axis=AX.X,
                    apply_absolute_value=True))
                pst["gamma_V"] = tok
            pst["gt"][bid] = tok
        return fn

    mat_cols = {}
    gcol = 0
    nchunk = (c.HF * c.D) // (P * c.GCH)
    for (w_in, m, _, is_w3) in mats:
        mat_cols[m] = (gcol, nchunk)
        for t in range(nchunk):
            QV.append((gamma_load(w_in, t), gamma_comp(gcol)))
            gcol += 1
    assert gcol <= ngch

    def thr_item():
        b.wait("vector", pst.get("gamma_A"))
        b.wait("vector", pst.get("gamma_V"))
        for (w_in, m, _, is_w3) in mats:
            c0, nt = mat_cols[m]
            nelem = float(nt * P * c.GCH)
            tok = b.emit("vector", lambda e, c0=c0, nt=nt, m=m: e.reduce_sum(
                gpart[:, ngch + m:ngch + m + 1], gpart[:, c0:c0 + nt], axis=AX.X))
            b.wait("gpsimd", tok)
            tok = b.emit("gpsimd", lambda e, m=m: e.tensor_reduce(
                gsum[:, m:m + 1], gpart[:, ngch + m:ngch + m + 1], axis=AX.C,
                op=ALU.add))
            b.wait("gpsimd", tok)
            b.emit("gpsimd", lambda e, m=m, ne=nelem: e.tensor_scalar(
                gsum[:, 4 + m:5 + m], gsum[:, m:m + 1], 0.5 / ne, 0.5 * c.QEPS,
                ALU.mult, ALU.add))
            b.emit("gpsimd", lambda e, m=m: e.tensor_copy(
                fdump[0:1, 48:49], gsum[0:1, 4 + m:5 + m]))
            b.dma("gpsimd", thr_h[m], gsum[0:1, 4 + m:5 + m], ch="thr")
            ltok = b.dma("gpsimd", thr128[:, m:m + 1],
                         thr_h[m].broadcast_to((P, 1)), ch="thr")
            b.wait("gpsimd", ltok)
            pst["thr_done"] = b.emit("gpsimd", lambda e, m=m: e.tensor_scalar_mul(
                nthr128[:, m:m + 1], thr128[:, m:m + 1], -1.0))
    QV.append((None, thr_item))

    def transp_item(m, hc, wT_out, is_w3, vtok, tpar):
        def fn():
            if pst.get("grown") and slab2 is not None:
                si = pst.get("si", 0) % 2
                pst["si"] = pst.get("si", 0) + 1
            else:
                si = 0
            sl = slab if si == 0 else slab2
            slch = "sl0" if si == 0 else "sl1"
            b.wait("tensor", vtok)
            last_t = None
            for dc in range(c.DC):
                ps = dc % 2
                b.wait("tensor", ptq_tok[ps])
                src = wqh[:, tpar, dc, :]
                ttok = b.emit("tensor", lambda e, src=src, ps=ptq_g[ps]:
                              e.transpose(ps, src, ident[:]), inc="T")
                last_t = ttok
                eng = "scalar" if dc % 2 == 0 else "vector"
                b.wait(eng, ttok)
                b.wait(eng, pst["slab_st"][si])
                if eng == "scalar":
                    fn2 = (lambda e, dc=dc, ps=ptq_g[ps], sl=sl:
                           e.copy(sl[:, dc * P:(dc + 1) * P], ps))
                else:
                    fn2 = (lambda e, dc=dc, ps=ptq_g[ps], sl=sl:
                           e.tensor_copy(sl[:, dc * P:(dc + 1) * P], ps))
                ptq_tok[ps] = b.emit(eng, fn2)
            pst["wq_rd"][tpar] = last_t
            sl_f = sl[:].rearrange("p (a j) -> p a j", j=P)
            for t_ in fence([("scalar", sl_f[0:1, 0::2, P - 1], c.DC // 2),
                             ("vector", sl_f[0:1, 1::2, P - 1], c.DC // 2)]):
                b.wait("gpsimd", t_)
            if is_w3:
                wdq = c.D // c.DQ
                for dq in range(c.DQ):
                    pst["slab_st"][si] = b.dma(
                        "gpsimd", wT_out[dq, hc], sl[:, dq * wdq:(dq + 1) * wdq],
                        ch=slch)
            else:
                pst["slab_st"][si] = b.dma("gpsimd", wT_out[hc], sl[:], ch=slch)
        return fn

    def quant_load(w_in, hc, half, is_w3):
        def fn():
            bid = pst["rr"] % len(G3)
            pst["rr"] += 1
            gb, ch = G3[bid]
            ie = ("sync" if bid % 2 else "gpsimd") \
                if pst.get("eng_pref") == "ALT" else "gpsimd"
            if ie == "sync":
                ch = ch + "s"
            b.wait(ie, pst["gt"].get(bid))
            nb = c.GCH // P
            if is_w3:
                w3v = w_in.rearrange("(dc p) (hc j) -> hc p dc j", p=P, j=P)
                srcv = w3v[hc][:, half * nb:(half + 1) * nb, :]
                b.dma(ie, gb[:].rearrange("p (a j) -> p a j", j=P),
                      srcv, ch=ch)
            else:
                b.dma(ie, gb[:],
                      w_in[hc * P:(hc + 1) * P,
                           half * c.GCH:(half + 1) * c.GCH],
                      ch=ch)
            pst["pend"].append((bid, ch))
        return fn

    def quant_comp(m, hc, half, is_w3, wT_out, last_half, tpar):
        def fn():
            bid, ch = pst["pend"].popleft()
            gb, _ = G3[bid]
            nb = c.GCH // P
            dst = wqh[:, tpar, half * nb:(half + 1) * nb, :].rearrange(
                "p a j -> p (a j)")
            if use_act():
                par = pst["qc"] % 2
                pst["qc"] += 1
                sink = qa_s if par == 0 else qb_s
                b.wait("scalar", b.dma_total(ch))
                b.wait("scalar", pst["wq_rd"][tpar])
                b.wait("scalar", pst.get("thr_done"))
                b.wait("scalar", pst["sink_tok"][par])
                b.emit("scalar", lambda e, dst=dst, gb=gb, m=m: e.activation(
                    dst, gb[:], AF.Sign, bias=nthr128[:, m:m + 1]))
                stok = b.emit("scalar", lambda e, sink=sink, gb=gb, m=m:
                              e.activation(sink[:], gb[:], AF.Sign,
                                           bias=thr128[:, m:m + 1]))
                b.wait("vector", stok)
                vtok = b.emit("vector", lambda e, dst=dst, sink=sink:
                              e.tensor_add(dst, dst, sink[:]))
                pst["sink_tok"][par] = vtok
            else:
                b.wait("vector", b.dma_total(ch))
                b.wait("vector", pst["wq_rd"][tpar])
                b.wait("vector", pst.get("thr_done"))
                b.wait("vector", pst["sink_tok"][0])
                b.wait("vector", pst["sink_tok"][1])
                b.emit("vector", lambda e, gb=gb, m=m: e.tensor_scalar(
                    qa_s[:], gb[:], thr128[:, m:m + 1], 2.0,
                    ALU.is_ge, ALU.mult))
                b.emit("vector", lambda e, gb=gb, m=m: e.tensor_scalar(
                    qb_s[:], gb[:], nthr128[:, m:m + 1], 2.0,
                    ALU.is_le, ALU.mult))
                vtok = b.emit("vector", lambda e, dst=dst:
                              e.tensor_sub(dst, qa_s[:], qb_s[:]))
                pst["sink_tok"][0] = vtok
                pst["sink_tok"][1] = vtok
            pst["gt"][bid] = vtok
            if last_half:
                QT.append(transp_item(m, hc, wT_out, is_w3, vtok, tpar))
        return fn

    ti = 0
    for (w_in, m, wT_out, is_w3) in mats:
        width = c.D if not is_w3 else c.DC * P
        nhalf = width // c.GCH
        for hc in range(c.HC):
            for half in range(nhalf):
                QV.append((quant_load(w_in, hc, half, is_w3),
                           quant_comp(m, hc, half, is_w3, wT_out,
                                      half == nhalf - 1, ti % 3)))
            ti += 1

    for it in list(QV):
        QVL.append(it[0])
    for _ in range(LEAD):
        if QVL:
            fn0 = QVL.popleft()
            if fn0 is not None:
                fn0()

    def drip(q, n, qt_ok=True):
        for _ in range(n):
            if q is QV:
                if len(QT) > 2:
                    if not qt_ok:
                        return
                    while len(QT) > 2:
                        QT.popleft()()
                if q:
                    if QVL:
                        fn0 = QVL.popleft()
                        if fn0 is not None:
                            fn0()
                    _, comp = q.popleft()
                    comp()
            elif q:
                q.popleft()()

    x_in = ins["x"]
    NH = c.NH
    for bb in range(c.NB):
        b.emit("vector", lambda e: e.memset(vaug[:, :, :, c.HD:c.HD + 1], 1.0))

        for hf in range(2):
            t0 = hf * c.HTT
            pq = [psf(4 + i, c.CGW) for i in range(3)]
            pq_tok = [None, None, None]
            xbufs = [x_t0, x_t0b]
            x_rd = [None, None]
            lnbf_guard = None
            b.wait("gpsimd", pst.get("xp0"))
            xtoks = [b.dma("gpsimd", xbufs[0][:],
                           x_in[bb, t0 * P:(t0 + 1) * P, :], ch="x00")]
            for i in range(c.HTT):
                tt = t0 + i
                xb = xbufs[i % 2]
                if i + 1 < c.HTT:
                    b.wait("gpsimd", x_rd[(i + 1) % 2])
                    b.wait("gpsimd", pst.get("xp" + str((i + 1) % 2)))
                    xtoks.append(b.dma(
                        "gpsimd", xbufs[(i + 1) % 2][:],
                        x_in[bb, (tt + 1) * P:(tt + 2) * P, :],
                        ch=f"x0{(i + 1) % 2}"))
                b.wait("vector", xtoks[i])
                b.wait("scalar", xtoks[i])
                vtok = ln_full(xb[:], ln1bf[:], sq_scr, c.D,
                               extra_wait_v=lnbf_guard)
                x_rd[i % 2] = vtok
                pst["xp" + str(i % 2)] = vtok
                b.wait("tensor", vtok)
                last_t = None
                for dc in range(c.DC):
                    ps = dc % 2
                    b.wait("tensor", pt_tok_g[ps])
                    ttok = b.emit("tensor", lambda e, dc=dc, ps=pts_g[ps]:
                                  e.transpose(ps, ln1bf[:, dc * P:(dc + 1) * P],
                                              ident[:]), inc="T")
                    last_t = ttok
                    eng = "scalar" if dc % 2 == 0 else "vector"
                    b.wait(eng, ttok)
                    if eng == "scalar":
                        fn = (lambda e, dc=dc, i=i, ps=pts_g[ps]: e.copy(
                            ln1Th[:, dc, i * P:(i + 1) * P], ps))
                    else:
                        fn = (lambda e, dc=dc, i=i, ps=pts_g[ps]: e.tensor_copy(
                            ln1Th[:, dc, i * P:(i + 1) * P], ps))
                    pt_tok_g[ps] = b.emit(eng, fn)
                lnbf_guard = last_t
                drip(QV, 1)
                drip(QT, 1)
            ln1_done = [("A", b.sem["A"][1]), ("V", b.sem["V"][1])]

            w_bufs = [wslA, wslB]
            w_rd = [None, None]
            for t_ in qkv_cast:
                b.wait("sync", t_)
            pqf = [psf(4, 1024), psf(6, 1024)]
            pqf_tok = [None] * 2
            pgi = 0
            for cg in range(c.CGN):
                wb = w_bufs[cg % 2]
                for t_ in ln1_done:
                    b.wait("sync", t_)
                b.wait("sync", w_rd[cg % 2])
                for dc in range(c.DC):
                    if cg < c.QCG:
                        srcw = wq_bf[cg * c.CGW:(cg + 1) * c.CGW,
                                     dc * P:(dc + 1) * P]
                    elif cg < c.QCG + c.KCG:
                        kk = cg - c.QCG
                        srcw = wk_bf[kk * c.CGW:(kk + 1) * c.CGW,
                                     dc * P:(dc + 1) * P]
                    else:
                        vv = cg - c.QCG - c.KCG
                        srcw = wv_bf[vv * c.CGW:(vv + 1) * c.CGW,
                                     dc * P:(dc + 1) * P]
                    b.dma("sync", wb[:, dc, :], srcw, transpose=True,
                          ch=f"ws{cg % 2}_{dc % 4}")
                ws_toks = [b.dma_total(f"ws{cg % 2}_{j}") for j in range(4)]
                if cg < c.QCG + c.KCG:
                    for blk in range(c.NH):
                        pi = pgi % 2
                        pgi += 1
                        b.wait("tensor", pqf_tok[pi])
                        for t_ in ws_toks:
                            b.wait("tensor", t_)
                        for t_ in ln1_done:
                            b.wait("tensor", t_)
                        mm_tok = None
                        for dc in range(c.DC):
                            for seg in range(c.SH // 512):
                                mm_tok = b.emit(
                                    "tensor",
                                    lambda e, dc=dc, blk=blk, seg=seg, wb=wb, pi=pi:
                                    e.matmul(
                                        pqf[pi][:, seg * 512:(seg + 1) * 512],
                                        wb[:, dc, blk * c.HD:(blk + 1) * c.HD],
                                        ln1Th[:, dc, seg * 512:(seg + 1) * 512],
                                        start=(dc == 0), stop=(dc == c.DC - 1)),
                                    inc="T" if (dc == c.DC - 1 and
                                                seg == c.SH // 512 - 1) else None)
                        w_rd[cg % 2] = mm_tok
                        if cg < c.QCG:
                            dst = qT[:, cg * c.NH + blk,
                                     t0 * P:t0 * P + c.SH]
                        else:
                            kk = cg - c.QCG
                            dst = kT[:, kk * c.NH + blk,
                                     t0 * P:t0 * P + c.SH]
                        cols = slice(t0 * P, t0 * P + c.SH)
                        h2 = c.HD // 2
                        b.wait("vector", mm_tok)
                        b.emit("vector", lambda e, pi=pi, dst=dst, cols=cols:
                               e.tensor_mul(dst, pqf[pi][:, 0:c.SH],
                                            cos_sb[:, cols]))
                        b.emit("vector", lambda e, pi=pi, cols=cols, h2=h2:
                               e.tensor_mul(rtmp[0:h2, 0:c.SH],
                                            pqf[pi][h2:c.HD, 0:c.SH],
                                            ssin_sb[0:h2, cols]))
                        b.emit("vector", lambda e, pi=pi, cols=cols, h2=h2:
                               e.tensor_mul(rtmp[h2:c.HD, 0:c.SH],
                                            pqf[pi][0:h2, 0:c.SH],
                                            ssin_sb[h2:c.HD, cols]))
                        vtok = b.emit("vector", lambda e, dst=dst: e.tensor_add(
                            dst, dst, rtmp[:, 0:c.SH]))
                        pqf_tok[pi] = vtok
                        drip(QV, 3)
                        drip(QT, 1)
                else:
                    vv = cg - c.QCG - c.KCG
                    for i in range(c.HTT):
                        tt = t0 + i
                        pi = pgi % 2
                        pgi += 1
                        b.wait("tensor", pqf_tok[pi])
                        for t_ in ws_toks:
                            b.wait("tensor", t_)
                        for t_ in ln1_done:
                            b.wait("tensor", t_)
                        mm_tok = None
                        for dc in range(c.DC):
                            mm_tok = b.emit(
                                "tensor",
                                lambda e, dc=dc, i=i, wb=wb, pi=pi: e.matmul(
                                    pqf[pi][:, 0:c.CGW],
                                    ln1Th[:, dc, i * P:(i + 1) * P],
                                    wb[:, dc, :], start=(dc == 0),
                                    stop=(dc == c.DC - 1)),
                                inc="T" if dc == c.DC - 1 else None)
                        w_rd[cg % 2] = mm_tok
                        b.wait("scalar", mm_tok)
                        pv = pqf[pi][:, 0:c.CGW].rearrange("p (g j) -> p g j",
                                                           j=c.HD)
                        atok = b.emit("scalar", lambda e, pv=pv, tt=tt, vv=vv:
                                      e.copy(vaug[:, vv * c.NH:(vv + 1) * c.NH, tt,
                                             0:c.HD], pv))
                        pqf_tok[pi] = atok
                        drip(QV, 3)
                        drip(QT, 1)
            b.barrier()

        pst["eng_pref"] = "V"
        if bb == 0 and not pst.get("grown") and NEXTRA > 0:
            pst["grown"] = True
            b.wait("gpsimd", ("T", b.sem["T"][1]))
            for j in range(NEXTRA):
                G3.append((g3x[j], f"gx{j}"))
            for _ in range(NEXTRA):
                if QVL:
                    fn0 = QVL.popleft()
                    if fn0 is not None:
                        fn0()

        pbs = [psf(0, c.QGW), psf(1, c.QGW), psf(7, c.QGW)]
        pos = [psf(3 + i, c.HD + 1) for i in range(2)]
        ptos = [psbf(5 + i, 0) for i in range(2)]
        pb_tok = [None] * 3
        po_tok = [None] * 2
        pto_tok = [None] * 2
        av_done = [None, None]
        groups = [(h, qg) for h in range(c.NQ) for qg in range(c.NG)]
        gstate = {}
        c_pend = collections.deque()

        def c_scores(gi):
            h, qg = groups[gi]
            g = h // c.GRP
            ab = gi % 2
            kts = c.QGT * qg + c.QGT
            exp_tok = None
            for kt in range(kts):
                pbi = (gstate.get("pbc", 0)) % 3
                gstate["pbc"] = gstate.get("pbc", 0) + 1
                b.wait("tensor", pb_tok[pbi])
                mtok = b.emit(
                    "tensor",
                    lambda e, g=g, kt=kt, h=h, qg=qg, pbi=pbi: e.matmul(
                        pbs[pbi], kT[:, g, kt * P:(kt + 1) * P],
                        qT[:, h, qg * c.QGW:(qg + 1) * c.QGW],
                        start=True, stop=True), inc="T")
                dep = mtok
                if kt >= c.QGT * qg:
                    i = kt - c.QGT * qg
                    b.wait("vector", mtok)
                    dep = b.emit("vector", lambda e, pbi=pbi, i=i:
                                 e.tensor_add(pbs[pbi], pbs[pbi],
                                              maskv_sb[:, i, :]))
                b.wait("scalar", dep)
                b.wait("scalar", av_done[ab])
                exp_tok = b.emit("scalar", lambda e, kt=kt, pbi=pbi, ab=ab:
                                 e.activation(attnT[:, ab, kt, :], pbs[pbi],
                                              AF.Exp, scale=c.SCALE,
                                              bias=-c.CSHIFT))
                pb_tok[pbi] = exp_tok
            gstate[gi] = exp_tok

        def c_av(gi):
            h, qg = groups[gi]
            g = h // c.GRP
            ab = gi % 2
            exp_tok = gstate.pop(gi)
            last_av = None
            for qt in range(c.QGT):
                qabs = qg * c.QGT + qt
                poi = (gstate.get("poc", 0)) % 2
                gstate["poc"] = gstate.get("poc", 0) + 1
                b.wait("tensor", po_tok[poi])
                b.wait("tensor", exp_tok)
                nkt = c.QGT * qg + qt + 1
                for kt in range(nkt):
                    last_av = b.emit(
                        "tensor",
                        lambda e, kt=kt, qt=qt, g=g, poi=poi, nkt=nkt, ab=ab:
                        e.matmul(pos[poi],
                                 attnT[:, ab, kt, qt * P:(qt + 1) * P],
                                 vaug[:, g, kt, :], start=(kt == 0),
                                 stop=(kt == nkt - 1)),
                        inc="T" if kt == nkt - 1 else None)
                b.wait("vector", last_av)
                oni = (gstate.get("onc", 0)) % 4
                gstate["onc"] = gstate.get("onc", 0) + 1
                b.wait("vector", gstate.get(("on", oni)))
                b.emit("vector", lambda e, poi=poi: e.reciprocal(
                    recip[:, poi:poi + 1], pos[poi][:, c.HD:c.HD + 1]))
                vtok = b.emit("vector", lambda e, poi=poi, oni=oni:
                              e.tensor_scalar_mul(
                                  o_nrm[:, oni, :], pos[poi][:, 0:c.HD],
                                  recip[:, poi:poi + 1]))
                po_tok[poi] = vtok

                def _otr(h=h, qabs=qabs, oni=oni, vtok=vtok):
                    pti = (gstate.get("ptc", 0)) % 2
                    gstate["ptc"] = gstate.get("ptc", 0) + 1
                    b.wait("tensor", vtok)
                    b.wait("tensor", pto_tok[pti])
                    ttok = b.emit("tensor", lambda e, oni=oni, pti=pti:
                                  e.transpose(ptos[pti], o_nrm[:, oni, :],
                                              ident[:]), inc="T")
                    gstate[("on", oni)] = ttok
                    b.wait("scalar", ttok)
                    pto_tok[pti] = b.emit(
                        "scalar", lambda e, h=h, qabs=qabs, pti=pti: e.copy(
                            oT[:, h, qabs * P:(qabs + 1) * P], ptos[pti]))
                c_pend.append(_otr)
                if len(c_pend) > 3:
                    c_pend.popleft()()
            av_done[ab] = last_av
            drip(QV, 3, qt_ok=False)

        c_scores(0)
        for gi in range(len(groups)):
            if gi + 1 < len(groups):
                c_scores(gi + 1)
            c_av(gi)
        while c_pend:
            c_pend.popleft()()
        b.barrier()
        pst["eng_pref"] = "ALT"
        while QV:
            drip(QV, 1)
        while QT:
            QT.popleft()()
        b.barrier()

        b.wait("sync", wo_cast)
        b.wait("sync", b.dma_total("sl1"))
        for dc in range(c.DC):
            b.dma("sync", woT[:, dc, :], wo_bf[0:c.D, dc * P:(dc + 1) * P],
                  transpose=True, ch=f"wo{dc % 4}")
        wo_loaded = [b.dma_total(f"wo{j}") for j in range(4)]
        pwo = psf(0, c.D)
        ptd = [psbf(4 + i, 0) for i in range(4)]
        pt_tok = [None] * 4
        pwo_tok = None
        x_rd = None
        y_st = None
        l2_st = None
        lnT_guard = None
        d_ao_rd = None
        for tt in range(c.TT):
            b.wait("gpsimd", x_rd)
            xtok = b.dma("gpsimd", x2_t[:], x_in[bb, tt * P:(tt + 1) * P, :],
                         ch="x2")
            for t_ in wo_loaded:
                b.wait("tensor", t_)
            b.wait("tensor", pwo_tok)
            mm_tok = None
            for dc in range(c.DC):
                for seg in range(c.D // 512):
                    last = (dc == c.DC - 1 and seg == c.D // 512 - 1)
                    mm_tok = b.emit(
                        "tensor", lambda e, dc=dc, seg=seg, tt=tt: e.matmul(
                            pwo[:, seg * 512:(seg + 1) * 512],
                            oT[:, dc, tt * P:(tt + 1) * P],
                            woT[:, dc, seg * 512:(seg + 1) * 512],
                            start=(dc == 0), stop=(dc == c.DC - 1)),
                        inc="T" if last else None)
            b.wait("scalar", mm_tok)
            b.wait("scalar", d_ao_rd)
            atok = b.emit("scalar", lambda e: e.copy(atn_o[:], pwo))
            pwo_tok = atok
            b.wait("vector", atok)
            b.wait("scalar", atok)
            d_ao_rd = ln_full(atn_o[:], tmp_d[:], sq_d, c.D)
            b.wait("vector", xtok)
            b.wait("vector", y_st)
            vtok = b.emit("vector", lambda e: e.tensor_add(
                y_t[:], tmp_d[:], x2_t[:]))
            x_rd = vtok
            for t_ in fence([("vector", y_t[0:1, c.D - 16:c.D], 16)]):
                b.wait("gpsimd", t_)
            y_st = b.dma("gpsimd", y_h[bb, tt * P:(tt + 1) * P, :], y_t[:],
                         ch="yst")
            b.wait("scalar", vtok)
            v2 = ln_full(y_t[:], ln2bf[:], sq_d, c.D, extra_wait_v=lnT_guard)
            b.wait("tensor", v2)
            last_t = None
            for dc in range(c.DC):
                ps = dc % 4
                b.wait("tensor", pt_tok[ps])
                b.wait("tensor", l2_st)
                ttok = b.emit("tensor", lambda e, dc=dc, ps=ptd[ps]: e.transpose(
                    ps, ln2bf[:, dc * P:(dc + 1) * P], ident[:]), inc="T")
                last_t = ttok
                eng = "scalar" if dc % 2 == 0 else "vector"
                b.wait(eng, ttok)
                if eng == "scalar":
                    fn = (lambda e, dc=dc, ps=ptd[ps]: e.copy(
                        ln2T_t[:, dc, :], ps))
                else:
                    fn = (lambda e, dc=dc, ps=ptd[ps]: e.tensor_copy(
                        ln2T_t[:, dc, :], ps))
                pt_tok[ps] = b.emit(eng, fn)
            lnT_guard = last_t
            for t_ in fence([("scalar", ln2T_t[0:1, 0::2, P - 1], c.DC // 2),
                             ("vector", ln2T_t[0:1, 1::2, P - 1], c.DC // 2)]):
                b.wait("gpsimd", t_)
            l2_st = b.dma("gpsimd", ln2T_hp[:, :, tt * P:(tt + 1) * P],
                          ln2T_t[:], ch="l2st")
        b.barrier()

        for hf in range(2):
            t0 = hf * c.HTT
            b.wait("vector", b.dma_total("sl1"))
            ltok = b.dma("sync", ln2Th[:], ln2T_hp[:, :, hf * c.SH:(hf + 1) * c.SH],
                         ch="l2ld")
            segs = c.SH // 512
            px1 = [psf(0 * segs, c.SH), psf(1 * segs, c.SH)]
            px2 = [psf(2 * segs, c.SH), psf(3 * segs, c.SH)]
            px1_tok = [None, None]
            px2_tok = [None, None]
            w_rd = [None, None]
            sig_guard = None
            for hc in range(c.HC):
                wi = hc % 2
                b.wait("sync", w_rd[wi])
                b.dma("sync", w1sl[:, wi, :, :].rearrange("p a bq -> p (a bq)"),
                      w1T_h[hc], ch=f"wab{wi}")
                l2 = b.dma("sync", w2sl[:, wi, :, :].rearrange("p a bq -> p (a bq)"),
                           w2T_h[hc], ch=f"wab{wi}")
                b.wait("tensor", l2)
                b.wait("tensor", ltok)
                b.wait("tensor", px1_tok[wi])
                b.wait("tensor", px2_tok[wi])
                mm_tok = None
                for dc in range(c.DC):
                    for seg in range(segs):
                        b.emit("tensor", lambda e, dc=dc, seg=seg, wi=wi:
                               e.matmul(px1[wi][:, seg * 512:(seg + 1) * 512],
                                        w1sl[:, wi, dc, :],
                                        ln2Th[:, dc, seg * 512:(seg + 1) * 512],
                                        start=(dc == 0), stop=(dc == c.DC - 1)))
                    for seg in range(segs):
                        last = (dc == c.DC - 1 and seg == segs - 1)
                        mm_tok = b.emit(
                            "tensor", lambda e, dc=dc, seg=seg, wi=wi: e.matmul(
                                px2[wi][:, seg * 512:(seg + 1) * 512],
                                w2sl[:, wi, dc, :],
                                ln2Th[:, dc, seg * 512:(seg + 1) * 512],
                                start=(dc == 0), stop=(dc == c.DC - 1)),
                            inc="T" if last else None)
                w_rd[wi] = mm_tok
                b.wait("scalar", mm_tok)
                b.wait("scalar", sig_guard)
                atok = b.emit("scalar", lambda e, wi=wi: e.activation(
                    sig[:, 0:c.SH], px1[wi], AF.Sigmoid, scale=0.5))
                px1_tok[wi] = atok
                b.wait("vector", atok)
                vtok = b.emit("vector", lambda e, hc=hc, wi=wi: e.tensor_mul(
                    hidT[:, hc, :], sig[:, 0:c.SH], px2[wi]))
                px2_tok[wi] = vtok
                sig_guard = vtok
            hid_done = ("V", b.sem["V"][1])
            e_guards = [w_rd[0], w_rd[1], b.dma_total("wab0"),
                        b.dma_total("wab1")]
            wdq = c.D // c.DQ
            pfs = [psf(i, wdq) for i in range(c.HTT)]
            pf_tok = [None] * c.HTT
            w3_rd = [None, None]
            for dq in range(c.DQ):
                for hc in range(c.HC):
                    wi = hc % 2
                    b.wait("sync", w3_rd[wi])
                    l3 = b.dma("sync", w3sl[:, wi, :], w3T_h[dq, hc],
                               ch=f"w3{wi}")
                    b.wait("tensor", l3)
                    b.wait("tensor", hid_done)
                    mm_tok = None
                    for i in range(c.HTT):
                        if hc == 0:
                            b.wait("tensor", pf_tok[i])
                        mm_tok = b.emit(
                            "tensor", lambda e, i=i, hc=hc, wi=wi: e.matmul(
                                pfs[i], hidT[:, hc, i * P:(i + 1) * P],
                                w3sl[:, wi, :], start=(hc == 0),
                                stop=(hc == c.HC - 1)),
                            inc="T" if i == c.HTT - 1 else None)
                    w3_rd[wi] = mm_tok
                    if hc == c.HC - 1:
                        b.wait("scalar", mm_tok)
                        for i in range(c.HTT):
                            pf_tok[i] = b.emit(
                                "scalar", lambda e, i=i, dq=dq: e.activation(
                                    ffnbf[:, i, dq * wdq:(dq + 1) * wdq],
                                    pfs[i], AF.Copy, scale=0.25))
            ffn_done = ("A", b.sem["A"][1])
            y_rd = None
            for i in range(c.HTT):
                tt = t0 + i
                b.wait("gpsimd", y_rd)
                b.wait("gpsimd", b.dma_total("ost0"))
                b.wait("gpsimd", b.dma_total("ost1"))
                ytok = b.dma("gpsimd", y_e[:], y_h[bb, tt * P:(tt + 1) * P, :],
                             ch="yld")
                for e_ in ("vector", "scalar"):
                    b.wait(e_, ffn_done)
                    b.wait(e_, ytok)
                    for t_ in e_guards:
                        b.wait(e_, t_)
                ln_full(ffnbf[:, i, :], tmp_e[:], sq_e, c.D)
                vtok = b.emit("vector", lambda e: e.tensor_add(
                    y_e[:], tmp_e[:], y_e[:]))
                y_rd = vtok
                for t_ in fence([("vector", y_e[0:1, c.D - 16:c.D], 16)]):
                    b.wait("gpsimd", t_)
                b.dma("gpsimd", out[bb, tt * P:(tt + 1) * P, :], y_e[:],
                      ch=f"ost{i % 2}")
            b.barrier()
        b.barrier()

    b.final_dma_waits("gpsimd")
    b.final_dma_waits("sync")

    with nc.Block() as block:
        b.run(block)
    stack.close()
    return nc


def build_nc(cfg):
    nc = bass.Bass("TRN2", target_bir_lowering=False, debug=False)
    c = cfg
    ins = {
        "x": nc.dram_tensor("x", [c.NB, c.S, c.D], F32, kind="ExternalInput").ap(),
        "wq": nc.dram_tensor("wq", [c.D, c.D], F32, kind="ExternalInput").ap(),
        "wk": nc.dram_tensor("wk", [c.KV, c.D], F32, kind="ExternalInput").ap(),
        "wv": nc.dram_tensor("wv", [c.KV, c.D], F32, kind="ExternalInput").ap(),
        "wo": nc.dram_tensor("wo", [c.D, c.D], F32, kind="ExternalInput").ap(),
        "w1": nc.dram_tensor("w1", [c.HF, c.D], F32, kind="ExternalInput").ap(),
        "w2": nc.dram_tensor("w2", [c.HF, c.D], F32, kind="ExternalInput").ap(),
        "w3": nc.dram_tensor("w3", [c.D, c.HF], F32, kind="ExternalInput").ap(),
        "cos_c": nc.dram_tensor("cos_c", [c.HD, c.S], F32, kind="ExternalInput").ap(),
        "ssin_c": nc.dram_tensor("ssin_c", [c.HD, c.S], F32, kind="ExternalInput").ap(),
        "ident_c": nc.dram_tensor("ident_c", [P, P], BF16, kind="ExternalInput").ap(),
        "maskv_c": nc.dram_tensor("maskv_c", [c.QGT, P, c.QGT * P], BF16,
                                  kind="ExternalInput").ap(),
    }
    out = nc.dram_tensor("out", [c.NB, c.S, c.D], F32, kind="ExternalOutput").ap()
    build(nc, ins, out, cfg)
    return nc


def kernel(L0_attn_x, L0_attn_Wq, L0_attn_Wk, L0_attn_Wv, L0_attn_Wo,
           L0_ffn_w1, L0_ffn_w2, L0_ffn_w3):
    cfg = Cfg()
    consts = host_constants(cfg)
    x = np.ascontiguousarray(np.asarray(L0_attn_x, dtype=np.float32))
    shards = np.split(x, N_CORES, axis=0)
    common = {
        "wq": np.ascontiguousarray(np.asarray(L0_attn_Wq, np.float32)),
        "wk": np.ascontiguousarray(np.asarray(L0_attn_Wk, np.float32)),
        "wv": np.ascontiguousarray(np.asarray(L0_attn_Wv, np.float32)),
        "wo": np.ascontiguousarray(np.asarray(L0_attn_Wo, np.float32)),
        "w1": np.ascontiguousarray(np.asarray(L0_ffn_w1, np.float32)),
        "w2": np.ascontiguousarray(np.asarray(L0_ffn_w2, np.float32)),
        "w3": np.ascontiguousarray(np.asarray(L0_ffn_w3, np.float32)),
        **consts,
    }
    nc = build_nc(cfg)
    in_maps = [{"x": s, **common} for s in shards]
    res = run_bass_kernel_spmd(nc, in_maps, core_ids=list(range(N_CORES)))
    return np.concatenate([np.asarray(r["out"], np.float32) for r in res.results],
                          axis=0)


# version 29
# speedup vs baseline: 1.1041x; 1.1041x over previous
import collections
import math
from contextlib import ExitStack

import numpy as np
import ml_dtypes
import concourse.bass as bass
import concourse.mybir as mybir
from concourse.bass_utils import run_bass_kernel_spmd

F32 = mybir.dt.float32
BF16 = mybir.dt.bfloat16
AF = mybir.ActivationFunctionType
ALU = mybir.AluOpType
AX = mybir.AxisListType

N_CORES = 8
P = 128
SBMAP = []


class Cfg:
    def __init__(self, S=2048, D=2048, HD=128, NQ=16, NKV=4, HF=5632, NB=2,
                 MAX_POS=4096, THETA=10000.0):
        self.S, self.D, self.HD, self.NQ, self.NKV, self.HF, self.NB = S, D, HD, NQ, NKV, HF, NB
        self.MAX_POS, self.THETA = MAX_POS, THETA
        self.KV = NKV * HD
        self.GRP = NQ // NKV
        self.TT = S // P
        self.HTT = self.TT // 2
        self.SH = S // 2
        self.DC = D // P
        self.HC = HF // P
        self.QGT = min(4, self.TT)
        self.NG = self.TT // self.QGT
        self.QGW = self.QGT * P
        self.DQ = max(1, D // 512)
        self.CGW = 256
        self.CGN = (D + 2 * self.KV) // self.CGW
        self.QCG = D // self.CGW
        self.KCG = self.KV // self.CGW
        self.NH = self.CGW // HD
        self.SCALE = 1.0 / math.sqrt(HD)
        self.CSHIFT = 30.0
        self.EPS = 1e-5
        self.QEPS = 1e-8
        self.GCH = 512
        assert self.KV % self.CGW == 0 and D % 512 == 0 and self.HTT <= 8
        assert self.SH % 512 == 0 and self.TT % self.QGT == 0
        assert self.D % self.GCH == 0
        assert (self.HF * self.D) % (128 * self.GCH) == 0


def host_constants(cfg):
    HD = cfg.HD
    k = np.arange(0, HD, 2)[:HD // 2].astype(np.float32)
    inv_freq = 1.0 / cfg.THETA ** (k / HD)
    positions = np.arange(cfg.MAX_POS, dtype=np.float32)
    angles = positions[:, None] * inv_freq[None, :]
    angles = np.concatenate([angles, angles], axis=-1)
    cos = np.cos(angles)[:cfg.S].astype(np.float32)
    sin = np.sin(angles)[:cfg.S].astype(np.float32)
    ssin = np.concatenate([-sin[:, :HD // 2], sin[:, HD // 2:]], axis=1)
    cos = np.ascontiguousarray(cos.T)
    ssin = np.ascontiguousarray(ssin.T)
    ident = np.eye(P, dtype=ml_dtypes.bfloat16)
    QGT = cfg.QGT
    maskv = np.zeros((QGT, P, QGT * P), dtype=np.float32)
    for i in range(QGT):
        for j in range(QGT):
            blk = maskv[i, :, j * P:(j + 1) * P]
            if j < i:
                blk[:] = -1e30
            elif j == i:
                pp, cc = np.meshgrid(np.arange(P), np.arange(P), indexing="ij")
                blk[pp > cc] = -1e30
    return {"cos_c": cos, "ssin_c": ssin, "ident_c": ident,
            "maskv_c": maskv.astype(ml_dtypes.bfloat16)}


ENGINES = ("sync", "gpsimd", "vector", "scalar", "tensor")


class B:

    _SELF = {"vector": "V", "scalar": "A", "gpsimd": "G"}

    def __init__(self, nc, stack):
        self.nc = nc
        self.stack = stack
        self.st = {e: [] for e in ENGINES}
        self.sem = {}
        self.hw = {e: {} for e in ENGINES}
        for name in ("T", "V", "A", "G"):
            h = stack.enter_context(nc.semaphore(name))
            self.sem[name] = [h, 0]
        self.ln_guard = [None, None]
        self.ln_i = 0

    def emit(self, eng, fn, inc=None, k=1):
        nm = self._SELF.get(eng)
        if nm is not None:
            s = self.sem[nm]
            self.wait(eng, (nm, s[1]))
            s[1] += 1
            tgt = s[1]
            self.st[eng].append(lambda e, fn=fn, h=s[0]: fn(e).then_inc(h, 1))
            return (nm, tgt)
        if inc is not None:
            s = self.sem[inc]
            s[1] += k
            tgt = s[1]
            self.st[eng].append(lambda e, fn=fn, h=s[0], k=k: fn(e).then_inc(h, k))
            return (inc, tgt)
        self.st[eng].append(fn)
        return None

    def dma(self, eng, out_ap, in_ap, ch="d0", **kw):
        nm = "D" + ch
        if nm not in self.sem:
            h = self.stack.enter_context(self.nc.semaphore(nm))
            self.sem[nm] = [h, 0]
        s = self.sem[nm]
        self.wait(eng, (nm, s[1]))
        s[1] += 16
        tgt = s[1]
        self.st[eng].append(
            lambda e, o=out_ap, i=in_ap, h=s[0], kw=kw: e.dma_start(o, i, **kw).then_inc(h, 16))
        return (nm, tgt)

    def dma_total(self, ch):
        nm = "D" + ch
        return (nm, self.sem[nm][1]) if nm in self.sem else None

    def final_dma_waits(self, eng):
        for nm, s in self.sem.items():
            if nm.startswith("D"):
                self.wait(eng, (nm, s[1]))

    def wait(self, eng, tok):
        if tok is None:
            return
        name, v = tok
        if self.hw[eng].get(name, 0) >= v:
            return
        self.hw[eng][name] = v
        h = self.sem[name][0]
        self.st[eng].append(lambda e, h=h, v=v: e.wait_ge(h, v))

    def barrier(self):
        toks = [(n, s[1]) for n, s in self.sem.items() if s[1] > 0]
        for e in ENGINES:
            for t in toks:
                self.wait(e, t)

    def run(self, block):
        @block.sync
        def _(e):
            for fn in self.st["sync"]:
                fn(e)

        @block.gpsimd
        def _(e):
            for fn in self.st["gpsimd"]:
                fn(e)

        @block.vector
        def _(e):
            for fn in self.st["vector"]:
                fn(e)

        @block.scalar
        def _(e):
            for fn in self.st["scalar"]:
                fn(e)

        @block.tensor
        def _(e):
            for fn in self.st["tensor"]:
                fn(e)


def build(nc, ins, out, cfg):
    c = cfg
    stack = ExitStack()
    b = B(nc, stack)

    PS = stack.enter_context(nc.psum_tensor("PS", [P, 4096], F32))

    def psf(bank, width):
        return PS[:, bank * 512:bank * 512 + width]

    def psbf(bank, col0, width=P):
        return PS[:].bitcast(BF16)[:, bank * 1024 + col0:bank * 1024 + col0 + width]

    dt = nc.dram_tensor
    wq_bf = dt("wq_bf", [c.D, c.D], BF16, kind="Internal").ap()
    wk_bf = dt("wk_bf", [c.KV, c.D], BF16, kind="Internal").ap()
    wv_bf = dt("wv_bf", [c.KV, c.D], BF16, kind="Internal").ap()
    wo_bf = dt("wo_bf", [c.D, c.D], BF16, kind="Internal").ap()
    w1T_h = dt("w1T_h", [c.HC, P, c.DC * P], BF16, kind="Internal").ap()
    w2T_h = dt("w2T_h", [c.HC, P, c.DC * P], BF16, kind="Internal").ap()
    w3T_h = dt("w3T_h", [c.DQ, c.HC, P, c.D // c.DQ], BF16, kind="Internal").ap()
    y_h = dt("y_h", [c.NB, c.S, c.D], F32, kind="Internal").ap()
    thr_h = dt("thr_h", [4, 1], F32, kind="Internal").ap()
    ln2T_h = dt("ln2T_h", [c.DC, P, c.S], BF16, kind="Internal").ap()
    ln2T_hp = ln2T_h.rearrange("dc p s -> p dc s")

    ARENA0 = 20480
    SB_LIMIT = nc.SBUF_PARTITION_SIZE_BYTES - 32
    ARENA = nc.alloc_sbuf_tensor_at("ARENA", [P, SB_LIMIT - ARENA0],
                                    mybir.dt.uint8, offset=ARENA0)
    off = [ARENA0]

    def sb(name, shape, dtype, at=None):
        o = off[0] if at is None else at
        esz = mybir.dt.size(dtype)
        n = int(np.prod(shape[1:]))
        assert (o - ARENA0) % esz == 0, (name, o)
        base = ARENA[:].bitcast(dtype)
        v = base[0:shape[0], (o - ARENA0) // esz:(o - ARENA0) // esz + n]
        if len(shape) == 3:
            v = v.rearrange("p (a bq) -> p a bq", a=shape[1])
        elif len(shape) == 4:
            v = v.rearrange("p (a bq cq) -> p a bq cq", a=shape[1], bq=shape[2])
        SBMAP.append((name, o, o + n * esz, shape, str(dtype)))
        off[0] = o + ((n * esz + 31) // 32) * 32
        return v

    ident = sb("ident", [P, P], BF16)
    cos_sb = sb("cos_sb", [P, c.S], BF16)
    ssin_sb = sb("ssin_sb", [P, c.S], BF16)
    maskv_sb = sb("maskv_sb", [P, c.QGT, c.QGW], BF16)
    ones_f = sb("ones_f", [1, P], F32)
    stA = sb("stA", [P, 16], F32)
    thr128 = sb("thr128", [P, 4], F32)
    nthr128 = sb("nthr128", [P, 4], F32)
    ngch = (2 * c.HF * c.D + c.D * c.HF) // (P * c.GCH)
    gpart = sb("gpart", [P, ngch + 8], F32)
    gsum = sb("gsum", [1, 8], F32)
    c_eps = sb("c_eps", [P, 1], F32)
    c_ncs = sb("c_ncs", [P, 1], F32)
    c_zero = sb("c_zero", [P, 1], F32)
    fdump = sb("fdump", [1, 64], F32)
    nc.const_aps.register(F32, c.EPS, c_eps)
    nc.const_aps.register(F32, -c.CSHIFT, c_ncs)
    nc.const_aps.register(F32, 0.0, c_zero)
    TA = off[0]

    wslA = sb("wslA", [P, c.DC, c.CGW], BF16, at=TA)
    WSLB_AT = off[0]
    wslB = sb("wslB", [P, c.DC, c.CGW], BF16)
    x_t0 = sb("x_t0", [P, c.D], F32)
    ln1bf = sb("ln1bf", [P, c.D], BF16)
    LN1TH_AT = off[0]
    ln1Th = sb("ln1Th", [P, c.DC, c.SH], BF16)
    RTMP_AT = off[0]
    rtmp = sb("rtmp", [P, c.SH], BF16)
    g3 = [sb(f"g3_{i}", [P, c.GCH], F32) for i in range(5)]
    qa_s = sb("qa_s", [P, c.GCH], BF16)
    qb_s = sb("qb_s", [P, c.GCH], BF16)
    wqh = sb("wqh", [P, 3, c.DC, P], BF16)
    slab = sb("slab", [P, c.DC * P], BF16)
    A_END = off[0]
    sq_scr = sb("sq_scr", [P, c.D], F32, at=TA)
    x_t0b = sb("x_t0b", [P, c.D], F32, at=WSLB_AT)
    assert c.D * 4 <= c.DC * c.CGW * 2
    assert c.D * 4 <= 2 * c.DC * c.CGW * 2

    off[0] = TA
    attnT = sb("attnT", [P, 2, c.TT, c.QGW], BF16, at=TA)
    o_nrm = sb("o_nrm", [P, 4, P], BF16)
    recip = sb("recip", [P, 2], F32)
    C_END = off[0]

    off[0] = TA
    woT = sb("woT", [P, c.DC, c.D], BF16, at=TA)
    x2_t = sb("x2_t", [P, c.D], F32)
    sq_d = sb("sq_d", [P, c.D], F32)
    ln2T_t = sb("ln2T_t", [P, c.DC, P], BF16)
    D_TA_END = off[0]

    TA_SZ = max(A_END, C_END, D_TA_END) - TA
    LB = TA + TA_SZ

    qT = sb("qT", [P, c.NQ, c.S], BF16, at=LB)
    oT = sb("oT", [P, c.NQ, c.S], BF16, at=LB)
    kT = sb("kT", [P, c.NKV, c.S], BF16)
    vaug = sb("vaug", [P, c.NKV, c.TT, c.HD + 1], BF16)
    LB_END = off[0]
    KTOFF = LB + c.NQ * c.S * 2
    off[0] = KTOFF
    atn_o = sb("atn_o", [P, c.D], F32, at=KTOFF)
    tmp_d = sb("tmp_d", [P, c.D], F32)
    y_t = sb("y_t", [P, c.D], F32)
    ln2bf = sb("ln2bf", [P, c.D], BF16)
    D_END = off[0]

    off[0] = TA
    ln2Th = sb("ln2Th", [P, c.DC, c.SH], BF16, at=TA)
    hidT = sb("hidT", [P, c.HC, c.SH], BF16)
    W1SL_AT = off[0]
    w1sl = sb("w1sl", [P, 2, c.DC, P], BF16)
    W2SL_AT = off[0]
    w2sl = sb("w2sl", [P, 2, c.DC, P], BF16)
    w3sl = sb("w3sl", [P, 2, c.D // c.DQ], BF16)
    sig = sb("sig", [P, min(c.SH, 1024)], F32)
    ffnbf = sb("ffnbf", [P, c.HTT, c.D], BF16)
    y_e = sb("y_e", [P, c.D], F32)
    E_END0 = off[0]
    sq_e = sb("sq_e", [P, c.D], F32, at=W1SL_AT)
    tmp_e = sb("tmp_e", [P, c.D], F32, at=W2SL_AT)
    off[0] = E_END0
    E_END = off[0]
    assert c.D * 4 <= 2 * c.DC * P * 2

    base_x = (max(C_END, LN1TH_AT) + 31) & ~31
    NEXTRA = max(0, min(12, (RTMP_AT - base_x) // (c.GCH * 4)))
    g3x = [sb(f"g3x{j}", [P, c.GCH], F32, at=base_x + j * c.GCH * 4)
           for j in range(NEXTRA)]
    off[0] = E_END

    assert max(E_END, D_END, LB_END) <= SB_LIMIT, (
        "sbuf overflow", A_END, E_END, D_END, LB_END, SB_LIMIT)

    def fence(reads):
        toks = []
        for eng, ap, n in reads:
            base = 0 if eng == "scalar" else 32
            if eng == "scalar":
                toks.append(b.emit("scalar", lambda e, ap=ap, n=n, bo=base: e.copy(
                    fdump[0:1, bo:bo + n], ap)))
            else:
                toks.append(b.emit("vector",
                                   lambda e, ap=ap, n=n, bo=base: e.tensor_copy(
                                       fdump[0:1, bo:bo + n], ap)))
        return toks

    ST_SUM, ST_SQ, ST_MEAN, ST_M2, ST_VAR, ST_LNV, ST_RSTD = 0, 2, 4, 6, 8, 10, 12

    def ln_full(src_ap, dst_ap, sq_buf, width, extra_wait_v=None):
        s = b.ln_i % 2
        b.ln_i += 1
        g = b.ln_guard[s]
        b.wait("vector", g)
        b.wait("scalar", g)
        b.emit("vector", lambda e: e.reduce_sum(
            stA[:, ST_SUM + s:ST_SUM + s + 1], src_ap, axis=AX.X))
        atok = b.emit("scalar", lambda e: e.activation(
            sq_buf[:, 0:width], src_ap, AF.Square,
            accum_out=stA[:, ST_SQ + s:ST_SQ + s + 1]))
        b.wait("vector", atok)
        b.emit("vector", lambda e: e.tensor_scalar_mul(
            stA[:, ST_MEAN + s:ST_MEAN + s + 1],
            stA[:, ST_SUM + s:ST_SUM + s + 1], 1.0 / width))
        b.emit("vector", lambda e: e.tensor_mul(
            stA[:, ST_M2 + s:ST_M2 + s + 1],
            stA[:, ST_MEAN + s:ST_MEAN + s + 1],
            stA[:, ST_MEAN + s:ST_MEAN + s + 1]))
        vtok = b.emit("vector", lambda e: e.scalar_tensor_tensor(
            stA[:, ST_VAR + s:ST_VAR + s + 1], stA[:, ST_SQ + s:ST_SQ + s + 1],
            1.0 / width, stA[:, ST_M2 + s:ST_M2 + s + 1],
            ALU.mult, ALU.subtract))
        b.wait("scalar", vtok)
        b.emit("scalar", lambda e: e.activation(
            stA[:, ST_LNV + s:ST_LNV + s + 1], stA[:, ST_VAR + s:ST_VAR + s + 1],
            AF.Ln, bias=c.EPS))
        atok2 = b.emit("scalar", lambda e: e.activation(
            stA[:, ST_RSTD + s:ST_RSTD + s + 1], stA[:, ST_LNV + s:ST_LNV + s + 1],
            AF.Exp, scale=-0.5))
        b.wait("vector", atok2)
        b.wait("vector", extra_wait_v)
        tok = b.emit("vector", lambda e: e.tensor_scalar(
            dst_ap, src_ap, stA[:, ST_MEAN + s:ST_MEAN + s + 1],
            stA[:, ST_RSTD + s:ST_RSTD + s + 1], ALU.subtract, ALU.mult))
        b.ln_guard[s] = tok
        return tok

    b.dma("sync", ident[:], ins["ident_c"], ch="cst")
    b.dma("sync", maskv_sb[:], ins["maskv_c"].rearrange("a p bq -> p a bq"),
          ch="cst")
    b.dma("gpsimd", cos_sb[:], ins["cos_c"], ch="cos")
    b.dma("gpsimd", ssin_sb[:], ins["ssin_c"], ch="cos")
    b.emit("vector", lambda e: e.memset(ones_f[:], 1.0))
    b.emit("vector", lambda e: e.memset(c_eps[:], c.EPS))
    b.emit("vector", lambda e: e.memset(c_ncs[:], -c.CSHIFT))
    b.emit("vector", lambda e: e.memset(c_zero[:], 0.0))
    b.barrier()

    for ci, (w_in, w_sc, rows) in enumerate((
            (ins["wq"], wq_bf, c.D), (ins["wk"], wk_bf, c.KV),
            (ins["wv"], wv_bf, c.KV), (ins["wo"], wo_bf, c.D))):
        step = 256
        for r0 in range(0, rows, step):
            r1 = min(rows, r0 + step)
            b.dma("gpsimd", w_sc[r0:r1, :], w_in[r0:r1, :], ch=f"cast{ci}")
    qkv_cast = [b.dma_total(f"cast{i}") for i in range(4)]
    wo_cast = b.dma_total("cast3")

    QV = collections.deque()
    QVL = collections.deque()
    QT = collections.deque()
    LEAD = 4
    G3 = [(g3[i], f"g{i}") for i in range(5)]
    pst = {"rr": 0, "pend": collections.deque(), "gt": {}, "qc": 0,
           "sink_tok": [None, None], "gamma_last": None,
           "wq_rd": [None, None, None], "slab_st": None}
    pts_g = [psbf(i, 0) for i in range(2)]
    ptq_g = [psbf(2 + i, 0) for i in range(2)]
    pt_tok_g = [None, None]
    ptq_tok = [None, None]

    mats = [(ins["w1"], 0, w1T_h, False), (ins["w2"], 1, w2T_h, False),
            (ins["w3"], 2, w3T_h, True)]

    def gamma_load(w_in, r0):
        def fn():
            bid = pst["rr"] % len(G3)
            pst["rr"] += 1
            gb, ch = G3[bid]
            b.wait("gpsimd", pst["gt"].get(bid))
            b.dma("gpsimd", gb[:],
                  w_in.rearrange("a bq -> (a bq)").rearrange(
                      "(n p x) -> n p x", p=P, x=c.GCH)[r0], ch=ch)
            pst["pend"].append(bid)
        return fn

    def use_act():
        pref = pst.get("eng_pref", "A")
        if pref == "ALT":
            pst["alt"] = pst.get("alt", 0) + 1
            return pst["alt"] % 2 == 0
        return pref == "A"

    def gamma_comp(col):
        def fn():
            bid = pst["pend"].popleft()
            gb, ch = G3[bid]
            if use_act():
                b.wait("scalar", b.dma_total(ch))
                b.wait("scalar", pst["sink_tok"][0])
                tok = b.emit("scalar", lambda e, gb=gb, col=col: e.activation(
                    qa_s[:], gb[:], AF.Abs, accum_out=gpart[:, col:col + 1]))
                pst["sink_tok"][0] = tok
                pst["gamma_A"] = tok
            else:
                b.wait("vector", b.dma_total(ch))
                tok = b.emit("vector", lambda e, gb=gb, col=col: e.reduce_sum(
                    gpart[:, col:col + 1], gb[:], axis=AX.X,
                    apply_absolute_value=True))
                pst["gamma_V"] = tok
            pst["gt"][bid] = tok
        return fn

    mat_cols = {}
    gcol = 0
    nchunk = (c.HF * c.D) // (P * c.GCH)
    for (w_in, m, _, is_w3) in mats:
        mat_cols[m] = (gcol, nchunk)
        for t in range(nchunk):
            QV.append((gamma_load(w_in, t), gamma_comp(gcol)))
            gcol += 1
    assert gcol <= ngch

    def thr_item():
        b.wait("vector", pst.get("gamma_A"))
        b.wait("vector", pst.get("gamma_V"))
        for (w_in, m, _, is_w3) in mats:
            c0, nt = mat_cols[m]
            nelem = float(nt * P * c.GCH)
            tok = b.emit("vector", lambda e, c0=c0, nt=nt, m=m: e.reduce_sum(
                gpart[:, ngch + m:ngch + m + 1], gpart[:, c0:c0 + nt], axis=AX.X))
            b.wait("gpsimd", tok)
            tok = b.emit("gpsimd", lambda e, m=m: e.tensor_reduce(
                gsum[:, m:m + 1], gpart[:, ngch + m:ngch + m + 1], axis=AX.C,
                op=ALU.add))
            b.wait("gpsimd", tok)
            b.emit("gpsimd", lambda e, m=m, ne=nelem: e.tensor_scalar(
                gsum[:, 4 + m:5 + m], gsum[:, m:m + 1], 0.5 / ne, 0.5 * c.QEPS,
                ALU.mult, ALU.add))
            b.emit("gpsimd", lambda e, m=m: e.tensor_copy(
                fdump[0:1, 48:49], gsum[0:1, 4 + m:5 + m]))
            b.dma("gpsimd", thr_h[m], gsum[0:1, 4 + m:5 + m], ch="thr")
            ltok = b.dma("gpsimd", thr128[:, m:m + 1],
                         thr_h[m].broadcast_to((P, 1)), ch="thr")
            b.wait("gpsimd", ltok)
            pst["thr_done"] = b.emit("gpsimd", lambda e, m=m: e.tensor_scalar_mul(
                nthr128[:, m:m + 1], thr128[:, m:m + 1], -1.0))
    QV.append((None, thr_item))

    def transp_item(m, hc, wT_out, is_w3, vtok, tpar):
        def fn():
            b.wait("tensor", vtok)
            last_t = None
            for dc in range(c.DC):
                ps = dc % 2
                b.wait("tensor", ptq_tok[ps])
                src = wqh[:, tpar, dc, :]
                ttok = b.emit("tensor", lambda e, src=src, ps=ptq_g[ps]:
                              e.transpose(ps, src, ident[:]), inc="T")
                last_t = ttok
                eng = "scalar" if dc % 2 == 0 else "vector"
                b.wait(eng, ttok)
                b.wait(eng, pst["slab_st"])
                if eng == "scalar":
                    fn2 = (lambda e, dc=dc, ps=ptq_g[ps]:
                           e.copy(slab[:, dc * P:(dc + 1) * P], ps))
                else:
                    fn2 = (lambda e, dc=dc, ps=ptq_g[ps]:
                           e.tensor_copy(slab[:, dc * P:(dc + 1) * P], ps))
                ptq_tok[ps] = b.emit(eng, fn2)
            pst["wq_rd"][tpar] = last_t
            sl_f = slab[:].rearrange("p (a j) -> p a j", j=P)
            for t_ in fence([("scalar", sl_f[0:1, 0::2, P - 1], c.DC // 2),
                             ("vector", sl_f[0:1, 1::2, P - 1], c.DC // 2)]):
                b.wait("gpsimd", t_)
            if is_w3:
                wdq = c.D // c.DQ
                for dq in range(c.DQ):
                    pst["slab_st"] = b.dma(
                        "gpsimd", wT_out[dq, hc], slab[:, dq * wdq:(dq + 1) * wdq],
                        ch="sl0")
            else:
                pst["slab_st"] = b.dma("gpsimd", wT_out[hc], slab[:], ch="sl0")
        return fn

    def quant_load(w_in, hc, half, is_w3):
        def fn():
            bid = pst["rr"] % len(G3)
            pst["rr"] += 1
            gb, ch = G3[bid]
            b.wait("gpsimd", pst["gt"].get(bid))
            nb = c.GCH // P
            if is_w3:
                w3v = w_in.rearrange("(dc p) (hc j) -> hc p dc j", p=P, j=P)
                srcv = w3v[hc][:, half * nb:(half + 1) * nb, :]
                b.dma("gpsimd", gb[:].rearrange("p (a j) -> p a j", j=P),
                      srcv, ch=ch)
            else:
                b.dma("gpsimd", gb[:],
                      w_in[hc * P:(hc + 1) * P,
                           half * c.GCH:(half + 1) * c.GCH],
                      ch=ch)
            pst["pend"].append(bid)
        return fn

    def quant_comp(m, hc, half, is_w3, wT_out, last_half, tpar):
        def fn():
            bid = pst["pend"].popleft()
            gb, ch = G3[bid]
            nb = c.GCH // P
            dst = wqh[:, tpar, half * nb:(half + 1) * nb, :].rearrange(
                "p a j -> p (a j)")
            if use_act():
                par = pst["qc"] % 2
                pst["qc"] += 1
                sink = qa_s if par == 0 else qb_s
                b.wait("scalar", b.dma_total(ch))
                b.wait("scalar", pst["wq_rd"][tpar])
                b.wait("scalar", pst.get("thr_done"))
                b.wait("scalar", pst["sink_tok"][par])
                b.emit("scalar", lambda e, dst=dst, gb=gb, m=m: e.activation(
                    dst, gb[:], AF.Sign, bias=nthr128[:, m:m + 1]))
                stok = b.emit("scalar", lambda e, sink=sink, gb=gb, m=m:
                              e.activation(sink[:], gb[:], AF.Sign,
                                           bias=thr128[:, m:m + 1]))
                b.wait("vector", stok)
                vtok = b.emit("vector", lambda e, dst=dst, sink=sink:
                              e.tensor_add(dst, dst, sink[:]))
                pst["sink_tok"][par] = vtok
            else:
                b.wait("vector", b.dma_total(ch))
                b.wait("vector", pst["wq_rd"][tpar])
                b.wait("vector", pst.get("thr_done"))
                b.wait("vector", pst["sink_tok"][0])
                b.wait("vector", pst["sink_tok"][1])
                b.emit("vector", lambda e, gb=gb, m=m: e.tensor_scalar(
                    qa_s[:], gb[:], thr128[:, m:m + 1], 2.0,
                    ALU.is_ge, ALU.mult))
                b.emit("vector", lambda e, gb=gb, m=m: e.tensor_scalar(
                    qb_s[:], gb[:], nthr128[:, m:m + 1], 2.0,
                    ALU.is_le, ALU.mult))
                vtok = b.emit("vector", lambda e, dst=dst:
                              e.tensor_sub(dst, qa_s[:], qb_s[:]))
                pst["sink_tok"][0] = vtok
                pst["sink_tok"][1] = vtok
            pst["gt"][bid] = vtok
            if last_half:
                QT.append(transp_item(m, hc, wT_out, is_w3, vtok, tpar))
        return fn

    ti = 0
    for (w_in, m, wT_out, is_w3) in mats:
        width = c.D if not is_w3 else c.DC * P
        nhalf = width // c.GCH
        for hc in range(c.HC):
            for half in range(nhalf):
                QV.append((quant_load(w_in, hc, half, is_w3),
                           quant_comp(m, hc, half, is_w3, wT_out,
                                      half == nhalf - 1, ti % 3)))
            ti += 1

    for it in list(QV):
        QVL.append(it[0])
    for _ in range(LEAD):
        if QVL:
            fn0 = QVL.popleft()
            if fn0 is not None:
                fn0()

    def drip(q, n, qt_ok=True):
        for _ in range(n):
            if q is QV:
                if len(QT) > 2:
                    if not qt_ok:
                        return
                    while len(QT) > 2:
                        QT.popleft()()
                if q:
                    if QVL:
                        fn0 = QVL.popleft()
                        if fn0 is not None:
                            fn0()
                    _, comp = q.popleft()
                    comp()
            elif q:
                q.popleft()()

    x_in = ins["x"]
    NH = c.NH
    for bb in range(c.NB):
        b.emit("vector", lambda e: e.memset(vaug[:, :, :, c.HD:c.HD + 1], 1.0))

        for hf in range(2):
            t0 = hf * c.HTT
            pq = [psf(4 + i, c.CGW) for i in range(3)]
            pq_tok = [None, None, None]
            xbufs = [x_t0, x_t0b]
            x_rd = [pst.get("xp0"), pst.get("xp1")]
            lnbf_guard = None
            b.wait("gpsimd", x_rd[0])
            xtoks = [b.dma("gpsimd", xbufs[0][:],
                           x_in[bb, t0 * P:(t0 + 1) * P, :], ch="x00")]
            for i in range(c.HTT):
                tt = t0 + i
                xb = xbufs[i % 2]
                if i + 1 < c.HTT:
                    b.wait("gpsimd", x_rd[(i + 1) % 2])
                    xtoks.append(b.dma(
                        "gpsimd", xbufs[(i + 1) % 2][:],
                        x_in[bb, (tt + 1) * P:(tt + 2) * P, :],
                        ch=f"x0{(i + 1) % 2}"))
                b.wait("vector", xtoks[i])
                b.wait("scalar", xtoks[i])
                vtok = ln_full(xb[:], ln1bf[:], sq_scr, c.D,
                               extra_wait_v=lnbf_guard)
                x_rd[i % 2] = vtok
                pst["xp" + str(i % 2)] = vtok
                b.wait("tensor", vtok)
                last_t = None
                for dc in range(c.DC):
                    ps = dc % 2
                    b.wait("tensor", pt_tok_g[ps])
                    ttok = b.emit("tensor", lambda e, dc=dc, ps=pts_g[ps]:
                                  e.transpose(ps, ln1bf[:, dc * P:(dc + 1) * P],
                                              ident[:]), inc="T")
                    last_t = ttok
                    eng = "scalar" if dc % 2 == 0 else "vector"
                    b.wait(eng, ttok)
                    if eng == "scalar":
                        fn = (lambda e, dc=dc, i=i, ps=pts_g[ps]: e.copy(
                            ln1Th[:, dc, i * P:(i + 1) * P], ps))
                    else:
                        fn = (lambda e, dc=dc, i=i, ps=pts_g[ps]: e.tensor_copy(
                            ln1Th[:, dc, i * P:(i + 1) * P], ps))
                    pt_tok_g[ps] = b.emit(eng, fn)
                lnbf_guard = last_t
                drip(QV, 1)
                drip(QT, 1)
            ln1_done = [("A", b.sem["A"][1]), ("V", b.sem["V"][1])]

            w_bufs = [wslA, wslB]
            w_rd = [None, None]
            for t_ in qkv_cast:
                b.wait("sync", t_)
            pqf = [psf(4, 1024), psf(6, 1024)]
            pqf_tok = [None] * 2
            pgi = 0
            for cg in range(c.CGN):
                wb = w_bufs[cg % 2]
                for t_ in ln1_done:
                    b.wait("sync", t_)
                b.wait("sync", w_rd[cg % 2])
                for dc in range(c.DC):
                    if cg < c.QCG:
                        srcw = wq_bf[cg * c.CGW:(cg + 1) * c.CGW,
                                     dc * P:(dc + 1) * P]
                    elif cg < c.QCG + c.KCG:
                        kk = cg - c.QCG
                        srcw = wk_bf[kk * c.CGW:(kk + 1) * c.CGW,
                                     dc * P:(dc + 1) * P]
                    else:
                        vv = cg - c.QCG - c.KCG
                        srcw = wv_bf[vv * c.CGW:(vv + 1) * c.CGW,
                                     dc * P:(dc + 1) * P]
                    b.dma("sync", wb[:, dc, :], srcw, transpose=True,
                          ch=f"ws{cg % 2}_{dc % 4}")
                ws_toks = [b.dma_total(f"ws{cg % 2}_{j}") for j in range(4)]
                if cg < c.QCG + c.KCG:
                    for blk in range(c.NH):
                        pi = pgi % 2
                        pgi += 1
                        b.wait("tensor", pqf_tok[pi])
                        for t_ in ws_toks:
                            b.wait("tensor", t_)
                        for t_ in ln1_done:
                            b.wait("tensor", t_)
                        mm_tok = None
                        for dc in range(c.DC):
                            for seg in range(c.SH // 512):
                                mm_tok = b.emit(
                                    "tensor",
                                    lambda e, dc=dc, blk=blk, seg=seg, wb=wb, pi=pi:
                                    e.matmul(
                                        pqf[pi][:, seg * 512:(seg + 1) * 512],
                                        wb[:, dc, blk * c.HD:(blk + 1) * c.HD],
                                        ln1Th[:, dc, seg * 512:(seg + 1) * 512],
                                        start=(dc == 0), stop=(dc == c.DC - 1)),
                                    inc="T" if (dc == c.DC - 1 and
                                                seg == c.SH // 512 - 1) else None)
                        w_rd[cg % 2] = mm_tok
                        if cg < c.QCG:
                            dst = qT[:, cg * c.NH + blk,
                                     t0 * P:t0 * P + c.SH]
                        else:
                            kk = cg - c.QCG
                            dst = kT[:, kk * c.NH + blk,
                                     t0 * P:t0 * P + c.SH]
                        cols = slice(t0 * P, t0 * P + c.SH)
                        h2 = c.HD // 2
                        b.wait("vector", mm_tok)
                        b.emit("vector", lambda e, pi=pi, dst=dst, cols=cols:
                               e.tensor_mul(dst, pqf[pi][:, 0:c.SH],
                                            cos_sb[:, cols]))
                        b.emit("vector", lambda e, pi=pi, cols=cols, h2=h2:
                               e.tensor_mul(rtmp[0:h2, 0:c.SH],
                                            pqf[pi][h2:c.HD, 0:c.SH],
                                            ssin_sb[0:h2, cols]))
                        b.emit("vector", lambda e, pi=pi, cols=cols, h2=h2:
                               e.tensor_mul(rtmp[h2:c.HD, 0:c.SH],
                                            pqf[pi][0:h2, 0:c.SH],
                                            ssin_sb[h2:c.HD, cols]))
                        vtok = b.emit("vector", lambda e, dst=dst: e.tensor_add(
                            dst, dst, rtmp[:, 0:c.SH]))
                        pqf_tok[pi] = vtok
                        drip(QV, 3)
                        drip(QT, 1)
                else:
                    vv = cg - c.QCG - c.KCG
                    for i in range(c.HTT):
                        tt = t0 + i
                        pi = pgi % 2
                        pgi += 1
                        b.wait("tensor", pqf_tok[pi])
                        for t_ in ws_toks:
                            b.wait("tensor", t_)
                        for t_ in ln1_done:
                            b.wait("tensor", t_)
                        mm_tok = None
                        for dc in range(c.DC):
                            mm_tok = b.emit(
                                "tensor",
                                lambda e, dc=dc, i=i, wb=wb, pi=pi: e.matmul(
                                    pqf[pi][:, 0:c.CGW],
                                    ln1Th[:, dc, i * P:(i + 1) * P],
                                    wb[:, dc, :], start=(dc == 0),
                                    stop=(dc == c.DC - 1)),
                                inc="T" if dc == c.DC - 1 else None)
                        w_rd[cg % 2] = mm_tok
                        b.wait("scalar", mm_tok)
                        pv = pqf[pi][:, 0:c.CGW].rearrange("p (g j) -> p g j",
                                                           j=c.HD)
                        atok = b.emit("scalar", lambda e, pv=pv, tt=tt, vv=vv:
                                      e.copy(vaug[:, vv * c.NH:(vv + 1) * c.NH, tt,
                                             0:c.HD], pv))
                        pqf_tok[pi] = atok
                        drip(QV, 3)
                        drip(QT, 1)
            b.barrier()

        pst["eng_pref"] = "V"
        if bb == 0 and not pst.get("grown") and NEXTRA > 0:
            pst["grown"] = True
            b.wait("gpsimd", ("T", b.sem["T"][1]))
            for j in range(NEXTRA):
                G3.append((g3x[j], f"gx{j}"))
            for _ in range(NEXTRA):
                if QVL:
                    fn0 = QVL.popleft()
                    if fn0 is not None:
                        fn0()

        pbs = [psf(0, c.QGW), psf(1, c.QGW), psf(7, c.QGW)]
        pos = [psf(3 + i, c.HD + 1) for i in range(2)]
        ptos = [psbf(5 + i, 0) for i in range(2)]
        pb_tok = [None] * 3
        po_tok = [None] * 2
        pto_tok = [None] * 2
        av_done = [None, None]
        groups = [(h, qg) for h in range(c.NQ) for qg in range(c.NG)]
        gstate = {}
        c_pend = collections.deque()

        def c_scores(gi):
            h, qg = groups[gi]
            g = h // c.GRP
            ab = gi % 2
            kts = c.QGT * qg + c.QGT
            exp_tok = None
            for kt in range(kts):
                pbi = (gstate.get("pbc", 0)) % 3
                gstate["pbc"] = gstate.get("pbc", 0) + 1
                b.wait("tensor", pb_tok[pbi])
                mtok = b.emit(
                    "tensor",
                    lambda e, g=g, kt=kt, h=h, qg=qg, pbi=pbi: e.matmul(
                        pbs[pbi], kT[:, g, kt * P:(kt + 1) * P],
                        qT[:, h, qg * c.QGW:(qg + 1) * c.QGW],
                        start=True, stop=True), inc="T")
                dep = mtok
                if kt >= c.QGT * qg:
                    i = kt - c.QGT * qg
                    b.wait("vector", mtok)
                    dep = b.emit("vector", lambda e, pbi=pbi, i=i:
                                 e.tensor_add(pbs[pbi], pbs[pbi],
                                              maskv_sb[:, i, :]))
                b.wait("scalar", dep)
                b.wait("scalar", av_done[ab])
                exp_tok = b.emit("scalar", lambda e, kt=kt, pbi=pbi, ab=ab:
                                 e.activation(attnT[:, ab, kt, :], pbs[pbi],
                                              AF.Exp, scale=c.SCALE,
                                              bias=-c.CSHIFT))
                pb_tok[pbi] = exp_tok
            gstate[gi] = exp_tok

        def c_av(gi):
            h, qg = groups[gi]
            g = h // c.GRP
            ab = gi % 2
            exp_tok = gstate.pop(gi)
            last_av = None
            for qt in range(c.QGT):
                qabs = qg * c.QGT + qt
                poi = (gstate.get("poc", 0)) % 2
                gstate["poc"] = gstate.get("poc", 0) + 1
                b.wait("tensor", po_tok[poi])
                b.wait("tensor", exp_tok)
                nkt = c.QGT * qg + qt + 1
                for kt in range(nkt):
                    last_av = b.emit(
                        "tensor",
                        lambda e, kt=kt, qt=qt, g=g, poi=poi, nkt=nkt, ab=ab:
                        e.matmul(pos[poi],
                                 attnT[:, ab, kt, qt * P:(qt + 1) * P],
                                 vaug[:, g, kt, :], start=(kt == 0),
                                 stop=(kt == nkt - 1)),
                        inc="T" if kt == nkt - 1 else None)
                b.wait("vector", last_av)
                oni = (gstate.get("onc", 0)) % 4
                gstate["onc"] = gstate.get("onc", 0) + 1
                b.wait("vector", gstate.get(("on", oni)))
                b.emit("vector", lambda e, poi=poi: e.reciprocal(
                    recip[:, poi:poi + 1], pos[poi][:, c.HD:c.HD + 1]))
                vtok = b.emit("vector", lambda e, poi=poi, oni=oni:
                              e.tensor_scalar_mul(
                                  o_nrm[:, oni, :], pos[poi][:, 0:c.HD],
                                  recip[:, poi:poi + 1]))
                po_tok[poi] = vtok

                def _otr(h=h, qabs=qabs, oni=oni, vtok=vtok):
                    pti = (gstate.get("ptc", 0)) % 2
                    gstate["ptc"] = gstate.get("ptc", 0) + 1
                    b.wait("tensor", vtok)
                    b.wait("tensor", pto_tok[pti])
                    ttok = b.emit("tensor", lambda e, oni=oni, pti=pti:
                                  e.transpose(ptos[pti], o_nrm[:, oni, :],
                                              ident[:]), inc="T")
                    gstate[("on", oni)] = ttok
                    b.wait("scalar", ttok)
                    pto_tok[pti] = b.emit(
                        "scalar", lambda e, h=h, qabs=qabs, pti=pti: e.copy(
                            oT[:, h, qabs * P:(qabs + 1) * P], ptos[pti]))
                c_pend.append(_otr)
                if len(c_pend) > 3:
                    c_pend.popleft()()
            av_done[ab] = last_av
            drip(QV, 3, qt_ok=False)

        c_scores(0)
        for gi in range(len(groups)):
            if gi + 1 < len(groups):
                c_scores(gi + 1)
            c_av(gi)
        while c_pend:
            c_pend.popleft()()
        b.barrier()
        pst["eng_pref"] = "ALT"
        while QV:
            drip(QV, 1)
        while QT:
            QT.popleft()()
        b.barrier()

        b.wait("sync", wo_cast)
        for dc in range(c.DC):
            b.dma("sync", woT[:, dc, :], wo_bf[0:c.D, dc * P:(dc + 1) * P],
                  transpose=True, ch=f"wo{dc % 4}")
        wo_loaded = [b.dma_total(f"wo{j}") for j in range(4)]
        pwo = psf(0, c.D)
        ptd = [psbf(4 + i, 0) for i in range(4)]
        pt_tok = [None] * 4
        pwo_tok = None
        x_rd = None
        y_st = None
        l2_st = None
        lnT_guard = None
        d_ao_rd = None
        for tt in range(c.TT):
            b.wait("gpsimd", x_rd)
            xtok = b.dma("gpsimd", x2_t[:], x_in[bb, tt * P:(tt + 1) * P, :],
                         ch="x2")
            for t_ in wo_loaded:
                b.wait("tensor", t_)
            b.wait("tensor", pwo_tok)
            mm_tok = None
            for dc in range(c.DC):
                for seg in range(c.D // 512):
                    last = (dc == c.DC - 1 and seg == c.D // 512 - 1)
                    mm_tok = b.emit(
                        "tensor", lambda e, dc=dc, seg=seg, tt=tt: e.matmul(
                            pwo[:, seg * 512:(seg + 1) * 512],
                            oT[:, dc, tt * P:(tt + 1) * P],
                            woT[:, dc, seg * 512:(seg + 1) * 512],
                            start=(dc == 0), stop=(dc == c.DC - 1)),
                        inc="T" if last else None)
            b.wait("scalar", mm_tok)
            b.wait("scalar", d_ao_rd)
            atok = b.emit("scalar", lambda e: e.copy(atn_o[:], pwo))
            pwo_tok = atok
            b.wait("vector", atok)
            b.wait("scalar", atok)
            d_ao_rd = ln_full(atn_o[:], tmp_d[:], sq_d, c.D)
            b.wait("vector", xtok)
            b.wait("vector", y_st)
            vtok = b.emit("vector", lambda e: e.tensor_add(
                y_t[:], tmp_d[:], x2_t[:]))
            x_rd = vtok
            for t_ in fence([("vector", y_t[0:1, c.D - 16:c.D], 16)]):
                b.wait("gpsimd", t_)
            y_st = b.dma("gpsimd", y_h[bb, tt * P:(tt + 1) * P, :], y_t[:],
                         ch="yst")
            b.wait("scalar", vtok)
            v2 = ln_full(y_t[:], ln2bf[:], sq_d, c.D, extra_wait_v=lnT_guard)
            b.wait("tensor", v2)
            last_t = None
            for dc in range(c.DC):
                ps = dc % 4
                b.wait("tensor", pt_tok[ps])
                b.wait("tensor", l2_st)
                ttok = b.emit("tensor", lambda e, dc=dc, ps=ptd[ps]: e.transpose(
                    ps, ln2bf[:, dc * P:(dc + 1) * P], ident[:]), inc="T")
                last_t = ttok
                eng = "scalar" if dc % 2 == 0 else "vector"
                b.wait(eng, ttok)
                if eng == "scalar":
                    fn = (lambda e, dc=dc, ps=ptd[ps]: e.copy(
                        ln2T_t[:, dc, :], ps))
                else:
                    fn = (lambda e, dc=dc, ps=ptd[ps]: e.tensor_copy(
                        ln2T_t[:, dc, :], ps))
                pt_tok[ps] = b.emit(eng, fn)
            lnT_guard = last_t
            for t_ in fence([("scalar", ln2T_t[0:1, 0::2, P - 1], c.DC // 2),
                             ("vector", ln2T_t[0:1, 1::2, P - 1], c.DC // 2)]):
                b.wait("gpsimd", t_)
            l2_st = b.dma("gpsimd", ln2T_hp[:, :, tt * P:(tt + 1) * P],
                          ln2T_t[:], ch="l2st")
        b.barrier()

        for hf in range(2):
            t0 = hf * c.HTT
            ltok = b.dma("sync", ln2Th[:], ln2T_hp[:, :, hf * c.SH:(hf + 1) * c.SH],
                         ch="l2ld")
            segs = c.SH // 512
            px1 = [psf(0 * segs, c.SH), psf(1 * segs, c.SH)]
            px2 = [psf(2 * segs, c.SH), psf(3 * segs, c.SH)]
            px1_tok = [None, None]
            px2_tok = [None, None]
            w_rd = [None, None]
            sig_guard = None
            for hc in range(c.HC):
                wi = hc % 2
                b.wait("sync", w_rd[wi])
                b.dma("sync", w1sl[:, wi, :, :].rearrange("p a bq -> p (a bq)"),
                      w1T_h[hc], ch=f"wab{wi}")
                l2 = b.dma("sync", w2sl[:, wi, :, :].rearrange("p a bq -> p (a bq)"),
                           w2T_h[hc], ch=f"wab{wi}")
                b.wait("tensor", l2)
                b.wait("tensor", ltok)
                b.wait("tensor", px1_tok[wi])
                b.wait("tensor", px2_tok[wi])
                mm_tok = None
                for dc in range(c.DC):
                    for seg in range(segs):
                        b.emit("tensor", lambda e, dc=dc, seg=seg, wi=wi:
                               e.matmul(px1[wi][:, seg * 512:(seg + 1) * 512],
                                        w1sl[:, wi, dc, :],
                                        ln2Th[:, dc, seg * 512:(seg + 1) * 512],
                                        start=(dc == 0), stop=(dc == c.DC - 1)))
                    for seg in range(segs):
                        last = (dc == c.DC - 1 and seg == segs - 1)
                        mm_tok = b.emit(
                            "tensor", lambda e, dc=dc, seg=seg, wi=wi: e.matmul(
                                px2[wi][:, seg * 512:(seg + 1) * 512],
                                w2sl[:, wi, dc, :],
                                ln2Th[:, dc, seg * 512:(seg + 1) * 512],
                                start=(dc == 0), stop=(dc == c.DC - 1)),
                            inc="T" if last else None)
                w_rd[wi] = mm_tok
                b.wait("scalar", mm_tok)
                b.wait("scalar", sig_guard)
                atok = b.emit("scalar", lambda e, wi=wi: e.activation(
                    sig[:, 0:c.SH], px1[wi], AF.Sigmoid, scale=0.5))
                px1_tok[wi] = atok
                b.wait("vector", atok)
                vtok = b.emit("vector", lambda e, hc=hc, wi=wi: e.tensor_mul(
                    hidT[:, hc, :], sig[:, 0:c.SH], px2[wi]))
                px2_tok[wi] = vtok
                sig_guard = vtok
            hid_done = ("V", b.sem["V"][1])
            e_guards = [w_rd[0], w_rd[1], b.dma_total("wab0"),
                        b.dma_total("wab1")]
            wdq = c.D // c.DQ
            pfs = [psf(i, wdq) for i in range(c.HTT)]
            pf_tok = [None] * c.HTT
            w3_rd = [None, None]
            for dq in range(c.DQ):
                for hc in range(c.HC):
                    wi = hc % 2
                    b.wait("sync", w3_rd[wi])
                    l3 = b.dma("sync", w3sl[:, wi, :], w3T_h[dq, hc],
                               ch=f"w3{wi}")
                    b.wait("tensor", l3)
                    b.wait("tensor", hid_done)
                    mm_tok = None
                    for i in range(c.HTT):
                        if hc == 0:
                            b.wait("tensor", pf_tok[i])
                        mm_tok = b.emit(
                            "tensor", lambda e, i=i, hc=hc, wi=wi: e.matmul(
                                pfs[i], hidT[:, hc, i * P:(i + 1) * P],
                                w3sl[:, wi, :], start=(hc == 0),
                                stop=(hc == c.HC - 1)),
                            inc="T" if i == c.HTT - 1 else None)
                    w3_rd[wi] = mm_tok
                    if hc == c.HC - 1:
                        b.wait("scalar", mm_tok)
                        for i in range(c.HTT):
                            pf_tok[i] = b.emit(
                                "scalar", lambda e, i=i, dq=dq: e.activation(
                                    ffnbf[:, i, dq * wdq:(dq + 1) * wdq],
                                    pfs[i], AF.Copy, scale=0.25))
            ffn_done = ("A", b.sem["A"][1])
            y_rd = None
            for i in range(c.HTT):
                tt = t0 + i
                b.wait("gpsimd", y_rd)
                b.wait("gpsimd", b.dma_total("ost0"))
                b.wait("gpsimd", b.dma_total("ost1"))
                ytok = b.dma("gpsimd", y_e[:], y_h[bb, tt * P:(tt + 1) * P, :],
                             ch="yld")
                for e_ in ("vector", "scalar"):
                    b.wait(e_, ffn_done)
                    b.wait(e_, ytok)
                    for t_ in e_guards:
                        b.wait(e_, t_)
                ln_full(ffnbf[:, i, :], tmp_e[:], sq_e, c.D)
                vtok = b.emit("vector", lambda e: e.tensor_add(
                    y_e[:], tmp_e[:], y_e[:]))
                y_rd = vtok
                for t_ in fence([("vector", y_e[0:1, c.D - 16:c.D], 16)]):
                    b.wait("gpsimd", t_)
                b.dma("gpsimd", out[bb, tt * P:(tt + 1) * P, :], y_e[:],
                      ch=f"ost{i % 2}")
            b.barrier()
        b.barrier()

    b.final_dma_waits("gpsimd")
    b.final_dma_waits("sync")

    with nc.Block() as block:
        b.run(block)
    stack.close()
    return nc


def build_nc(cfg):
    nc = bass.Bass("TRN2", target_bir_lowering=False, debug=False)
    c = cfg
    ins = {
        "x": nc.dram_tensor("x", [c.NB, c.S, c.D], F32, kind="ExternalInput").ap(),
        "wq": nc.dram_tensor("wq", [c.D, c.D], F32, kind="ExternalInput").ap(),
        "wk": nc.dram_tensor("wk", [c.KV, c.D], F32, kind="ExternalInput").ap(),
        "wv": nc.dram_tensor("wv", [c.KV, c.D], F32, kind="ExternalInput").ap(),
        "wo": nc.dram_tensor("wo", [c.D, c.D], F32, kind="ExternalInput").ap(),
        "w1": nc.dram_tensor("w1", [c.HF, c.D], F32, kind="ExternalInput").ap(),
        "w2": nc.dram_tensor("w2", [c.HF, c.D], F32, kind="ExternalInput").ap(),
        "w3": nc.dram_tensor("w3", [c.D, c.HF], F32, kind="ExternalInput").ap(),
        "cos_c": nc.dram_tensor("cos_c", [c.HD, c.S], F32, kind="ExternalInput").ap(),
        "ssin_c": nc.dram_tensor("ssin_c", [c.HD, c.S], F32, kind="ExternalInput").ap(),
        "ident_c": nc.dram_tensor("ident_c", [P, P], BF16, kind="ExternalInput").ap(),
        "maskv_c": nc.dram_tensor("maskv_c", [c.QGT, P, c.QGT * P], BF16,
                                  kind="ExternalInput").ap(),
    }
    out = nc.dram_tensor("out", [c.NB, c.S, c.D], F32, kind="ExternalOutput").ap()
    build(nc, ins, out, cfg)
    return nc


def kernel(L0_attn_x, L0_attn_Wq, L0_attn_Wk, L0_attn_Wv, L0_attn_Wo,
           L0_ffn_w1, L0_ffn_w2, L0_ffn_w3):
    cfg = Cfg()
    consts = host_constants(cfg)
    x = np.ascontiguousarray(np.asarray(L0_attn_x, dtype=np.float32))
    shards = np.split(x, N_CORES, axis=0)
    common = {
        "wq": np.ascontiguousarray(np.asarray(L0_attn_Wq, np.float32)),
        "wk": np.ascontiguousarray(np.asarray(L0_attn_Wk, np.float32)),
        "wv": np.ascontiguousarray(np.asarray(L0_attn_Wv, np.float32)),
        "wo": np.ascontiguousarray(np.asarray(L0_attn_Wo, np.float32)),
        "w1": np.ascontiguousarray(np.asarray(L0_ffn_w1, np.float32)),
        "w2": np.ascontiguousarray(np.asarray(L0_ffn_w2, np.float32)),
        "w3": np.ascontiguousarray(np.asarray(L0_ffn_w3, np.float32)),
        **consts,
    }
    nc = build_nc(cfg)
    in_maps = [{"x": s, **common} for s in shards]
    res = run_bass_kernel_spmd(nc, in_maps, core_ids=list(range(N_CORES)))
    return np.concatenate([np.asarray(r["out"], np.float32) for r in res.results],
                          axis=0)


# version 30
# speedup vs baseline: 1.1259x; 1.0198x over previous
import collections
import math
from contextlib import ExitStack

import numpy as np
import ml_dtypes
import concourse.bass as bass
import concourse.mybir as mybir
from concourse.bass_utils import run_bass_kernel_spmd

F32 = mybir.dt.float32
BF16 = mybir.dt.bfloat16
AF = mybir.ActivationFunctionType
ALU = mybir.AluOpType
AX = mybir.AxisListType

N_CORES = 8
P = 128
SBMAP = []


class Cfg:
    def __init__(self, S=2048, D=2048, HD=128, NQ=16, NKV=4, HF=5632, NB=2,
                 MAX_POS=4096, THETA=10000.0):
        self.S, self.D, self.HD, self.NQ, self.NKV, self.HF, self.NB = S, D, HD, NQ, NKV, HF, NB
        self.MAX_POS, self.THETA = MAX_POS, THETA
        self.KV = NKV * HD
        self.GRP = NQ // NKV
        self.TT = S // P
        self.HTT = self.TT // 2
        self.SH = S // 2
        self.DC = D // P
        self.HC = HF // P
        self.QGT = min(4, self.TT)
        self.NG = self.TT // self.QGT
        self.QGW = self.QGT * P
        self.DQ = max(1, D // 512)
        self.CGW = 256
        self.CGN = (D + 2 * self.KV) // self.CGW
        self.QCG = D // self.CGW
        self.KCG = self.KV // self.CGW
        self.NH = self.CGW // HD
        self.SCALE = 1.0 / math.sqrt(HD)
        self.CSHIFT = 30.0
        self.EPS = 1e-5
        self.QEPS = 1e-8
        self.GCH = 512
        assert self.KV % self.CGW == 0 and D % 512 == 0 and self.HTT <= 8
        assert self.SH % 512 == 0 and self.TT % self.QGT == 0
        assert self.D % self.GCH == 0
        assert (self.HF * self.D) % (128 * self.GCH) == 0


def host_constants(cfg):
    HD = cfg.HD
    k = np.arange(0, HD, 2)[:HD // 2].astype(np.float32)
    inv_freq = 1.0 / cfg.THETA ** (k / HD)
    positions = np.arange(cfg.MAX_POS, dtype=np.float32)
    angles = positions[:, None] * inv_freq[None, :]
    angles = np.concatenate([angles, angles], axis=-1)
    cos = np.cos(angles)[:cfg.S].astype(np.float32)
    sin = np.sin(angles)[:cfg.S].astype(np.float32)
    ssin = np.concatenate([-sin[:, :HD // 2], sin[:, HD // 2:]], axis=1)
    cos = np.ascontiguousarray(cos.T)
    ssin = np.ascontiguousarray(ssin.T)
    ident = np.eye(P, dtype=ml_dtypes.bfloat16)
    QGT = cfg.QGT
    maskv = np.zeros((QGT, P, QGT * P), dtype=np.float32)
    for i in range(QGT):
        for j in range(QGT):
            blk = maskv[i, :, j * P:(j + 1) * P]
            if j < i:
                blk[:] = -1e30
            elif j == i:
                pp, cc = np.meshgrid(np.arange(P), np.arange(P), indexing="ij")
                blk[pp > cc] = -1e30
    return {"cos_c": cos, "ssin_c": ssin, "ident_c": ident,
            "maskv_c": maskv.astype(ml_dtypes.bfloat16)}


ENGINES = ("sync", "gpsimd", "vector", "scalar", "tensor")


class B:

    _SELF = {"vector": "V", "scalar": "A", "gpsimd": "G"}

    def __init__(self, nc, stack):
        self.nc = nc
        self.stack = stack
        self.st = {e: [] for e in ENGINES}
        self.sem = {}
        self.hw = {e: {} for e in ENGINES}
        for name in ("T", "V", "A", "G"):
            h = stack.enter_context(nc.semaphore(name))
            self.sem[name] = [h, 0]
        self.ln_guard = [None, None]
        self.ln_i = 0

    def emit(self, eng, fn, inc=None, k=1):
        nm = self._SELF.get(eng)
        if nm is not None:
            s = self.sem[nm]
            self.wait(eng, (nm, s[1]))
            s[1] += 1
            tgt = s[1]
            self.st[eng].append(lambda e, fn=fn, h=s[0]: fn(e).then_inc(h, 1))
            return (nm, tgt)
        if inc is not None:
            s = self.sem[inc]
            s[1] += k
            tgt = s[1]
            self.st[eng].append(lambda e, fn=fn, h=s[0], k=k: fn(e).then_inc(h, k))
            return (inc, tgt)
        self.st[eng].append(fn)
        return None

    def dma(self, eng, out_ap, in_ap, ch="d0", **kw):
        nm = "D" + ch
        if nm not in self.sem:
            h = self.stack.enter_context(self.nc.semaphore(nm))
            self.sem[nm] = [h, 0]
        s = self.sem[nm]
        self.wait(eng, (nm, s[1]))
        s[1] += 16
        tgt = s[1]
        self.st[eng].append(
            lambda e, o=out_ap, i=in_ap, h=s[0], kw=kw: e.dma_start(o, i, **kw).then_inc(h, 16))
        return (nm, tgt)

    def dma_total(self, ch):
        nm = "D" + ch
        return (nm, self.sem[nm][1]) if nm in self.sem else None

    def final_dma_waits(self, eng):
        for nm, s in self.sem.items():
            if nm.startswith("D"):
                self.wait(eng, (nm, s[1]))

    def wait(self, eng, tok):
        if tok is None:
            return
        name, v = tok
        if self.hw[eng].get(name, 0) >= v:
            return
        self.hw[eng][name] = v
        h = self.sem[name][0]
        self.st[eng].append(lambda e, h=h, v=v: e.wait_ge(h, v))

    def barrier(self):
        toks = [(n, s[1]) for n, s in self.sem.items() if s[1] > 0]
        for e in ENGINES:
            for t in toks:
                self.wait(e, t)

    def run(self, block):
        @block.sync
        def _(e):
            for fn in self.st["sync"]:
                fn(e)

        @block.gpsimd
        def _(e):
            for fn in self.st["gpsimd"]:
                fn(e)

        @block.vector
        def _(e):
            for fn in self.st["vector"]:
                fn(e)

        @block.scalar
        def _(e):
            for fn in self.st["scalar"]:
                fn(e)

        @block.tensor
        def _(e):
            for fn in self.st["tensor"]:
                fn(e)


def build(nc, ins, out, cfg):
    c = cfg
    stack = ExitStack()
    b = B(nc, stack)

    PS = stack.enter_context(nc.psum_tensor("PS", [P, 4096], F32))

    def psf(bank, width):
        return PS[:, bank * 512:bank * 512 + width]

    def psbf(bank, col0, width=P):
        return PS[:].bitcast(BF16)[:, bank * 1024 + col0:bank * 1024 + col0 + width]

    dt = nc.dram_tensor
    wq_bf = dt("wq_bf", [c.D, c.D], BF16, kind="Internal").ap()
    wk_bf = dt("wk_bf", [c.KV, c.D], BF16, kind="Internal").ap()
    wv_bf = dt("wv_bf", [c.KV, c.D], BF16, kind="Internal").ap()
    wo_bf = dt("wo_bf", [c.D, c.D], BF16, kind="Internal").ap()
    w1T_h = dt("w1T_h", [c.HC, P, c.DC * P], BF16, kind="Internal").ap()
    w2T_h = dt("w2T_h", [c.HC, P, c.DC * P], BF16, kind="Internal").ap()
    w3T_h = dt("w3T_h", [c.DQ, c.HC, P, c.D // c.DQ], BF16, kind="Internal").ap()
    y_h = dt("y_h", [c.NB, c.S, c.D], F32, kind="Internal").ap()
    thr_h = dt("thr_h", [4, 1], F32, kind="Internal").ap()
    ln2T_h = dt("ln2T_h", [c.DC, P, c.S], BF16, kind="Internal").ap()
    ln2T_hp = ln2T_h.rearrange("dc p s -> p dc s")

    ARENA0 = 20480
    SB_LIMIT = nc.SBUF_PARTITION_SIZE_BYTES - 32
    ARENA = nc.alloc_sbuf_tensor_at("ARENA", [P, SB_LIMIT - ARENA0],
                                    mybir.dt.uint8, offset=ARENA0)
    off = [ARENA0]

    def sb(name, shape, dtype, at=None):
        o = off[0] if at is None else at
        esz = mybir.dt.size(dtype)
        n = int(np.prod(shape[1:]))
        assert (o - ARENA0) % esz == 0, (name, o)
        base = ARENA[:].bitcast(dtype)
        v = base[0:shape[0], (o - ARENA0) // esz:(o - ARENA0) // esz + n]
        if len(shape) == 3:
            v = v.rearrange("p (a bq) -> p a bq", a=shape[1])
        elif len(shape) == 4:
            v = v.rearrange("p (a bq cq) -> p a bq cq", a=shape[1], bq=shape[2])
        SBMAP.append((name, o, o + n * esz, shape, str(dtype)))
        off[0] = o + ((n * esz + 31) // 32) * 32
        return v

    ident = sb("ident", [P, P], BF16)
    cos_sb = sb("cos_sb", [P, c.S], BF16)
    ssin_sb = sb("ssin_sb", [P, c.S], BF16)
    maskv_sb = sb("maskv_sb", [P, c.QGT, c.QGW], BF16)
    ones_f = sb("ones_f", [1, P], F32)
    stA = sb("stA", [P, 16], F32)
    thr128 = sb("thr128", [P, 4], F32)
    nthr128 = sb("nthr128", [P, 4], F32)
    ngch = (2 * c.HF * c.D + c.D * c.HF) // (P * c.GCH)
    gpart = sb("gpart", [P, ngch + 8], F32)
    gsum = sb("gsum", [1, 8], F32)
    c_eps = sb("c_eps", [P, 1], F32)
    c_ncs = sb("c_ncs", [P, 1], F32)
    c_zero = sb("c_zero", [P, 1], F32)
    fdump = sb("fdump", [1, 64], F32)
    nc.const_aps.register(F32, c.EPS, c_eps)
    nc.const_aps.register(F32, -c.CSHIFT, c_ncs)
    nc.const_aps.register(F32, 0.0, c_zero)
    TA = off[0]

    wslA = sb("wslA", [P, c.DC, c.CGW], BF16, at=TA)
    wslB = sb("wslB", [P, c.DC, c.CGW], BF16)
    x_t0 = sb("x_t0", [P, c.D], F32)
    ln1bf = sb("ln1bf", [P, c.D], BF16)
    LN1TH_AT = off[0]
    ln1Th = sb("ln1Th", [P, c.DC, c.SH], BF16)
    RTMP_AT = off[0]
    rtmp = sb("rtmp", [P, c.SH], BF16)
    g3 = [sb(f"g3_{i}", [P, c.GCH], F32) for i in range(5)]
    qa_s = sb("qa_s", [P, c.GCH], BF16)
    qb_s = sb("qb_s", [P, c.GCH], BF16)
    wqh = sb("wqh", [P, 3, c.DC, P], BF16)
    slab = sb("slab", [P, c.DC * P], BF16)
    A_END = off[0]
    sq_scr = sb("sq_scr", [P, c.D], F32, at=TA)
    assert c.D * 4 <= 2 * c.DC * c.CGW * 2

    off[0] = TA
    attnT = sb("attnT", [P, 2, c.TT, c.QGW], BF16, at=TA)
    o_nrm = sb("o_nrm", [P, 4, P], BF16)
    recip = sb("recip", [P, 2], F32)
    C_END = off[0]

    off[0] = TA
    woT = sb("woT", [P, c.DC, c.D], BF16, at=TA)
    x2_t = sb("x2_t", [P, c.D], F32)
    sq_d = sb("sq_d", [P, c.D], F32)
    ln2T_t = sb("ln2T_t", [P, c.DC, P], BF16)
    D_TA_END = off[0]

    TA_SZ = max(A_END, C_END, D_TA_END) - TA
    LB = TA + TA_SZ

    qT = sb("qT", [P, c.NQ, c.S], BF16, at=LB)
    oT = sb("oT", [P, c.NQ, c.S], BF16, at=LB)
    kT = sb("kT", [P, c.NKV, c.S], BF16)
    vaug = sb("vaug", [P, c.NKV, c.TT, c.HD + 1], BF16)
    LB_END = off[0]
    KTOFF = LB + c.NQ * c.S * 2
    off[0] = KTOFF
    atn_o = sb("atn_o", [P, c.D], F32, at=KTOFF)
    tmp_d = sb("tmp_d", [P, c.D], F32)
    y_t = sb("y_t", [P, c.D], F32)
    ln2bf = sb("ln2bf", [P, c.D], BF16)
    D_END = off[0]

    off[0] = TA
    ln2Th = sb("ln2Th", [P, c.DC, c.SH], BF16, at=TA)
    hidT = sb("hidT", [P, c.HC, c.SH], BF16)
    W1SL_AT = off[0]
    w1sl = sb("w1sl", [P, 2, c.DC, P], BF16)
    W2SL_AT = off[0]
    w2sl = sb("w2sl", [P, 2, c.DC, P], BF16)
    w3sl = sb("w3sl", [P, 2, c.D // c.DQ], BF16)
    sig = sb("sig", [P, min(c.SH, 1024)], F32)
    ffnbf = sb("ffnbf", [P, c.HTT, c.D], BF16)
    y_e = sb("y_e", [P, c.D], F32)
    E_END0 = off[0]
    sq_e = sb("sq_e", [P, c.D], F32, at=W1SL_AT)
    tmp_e = sb("tmp_e", [P, c.D], F32, at=W2SL_AT)
    off[0] = E_END0
    E_END = off[0]
    assert c.D * 4 <= 2 * c.DC * P * 2

    base_x = (max(C_END, LN1TH_AT) + 31) & ~31
    NEXTRA = max(0, min(12, (RTMP_AT - base_x) // (c.GCH * 4)))
    g3x = [sb(f"g3x{j}", [P, c.GCH], F32, at=base_x + j * c.GCH * 4)
           for j in range(NEXTRA)]
    off[0] = E_END

    assert max(E_END, D_END, LB_END) <= SB_LIMIT, (
        "sbuf overflow", A_END, E_END, D_END, LB_END, SB_LIMIT)

    def fence(reads):
        toks = []
        for eng, ap, n in reads:
            base = 0 if eng == "scalar" else 32
            if eng == "scalar":
                toks.append(b.emit("scalar", lambda e, ap=ap, n=n, bo=base: e.copy(
                    fdump[0:1, bo:bo + n], ap)))
            else:
                toks.append(b.emit("vector",
                                   lambda e, ap=ap, n=n, bo=base: e.tensor_copy(
                                       fdump[0:1, bo:bo + n], ap)))
        return toks

    ST_SUM, ST_SQ, ST_MEAN, ST_M2, ST_VAR, ST_LNV, ST_RSTD = 0, 2, 4, 6, 8, 10, 12

    def ln_full(src_ap, dst_ap, sq_buf, width, extra_wait_v=None):
        s = b.ln_i % 2
        b.ln_i += 1
        g = b.ln_guard[s]
        b.wait("vector", g)
        b.wait("scalar", g)
        b.emit("vector", lambda e: e.reduce_sum(
            stA[:, ST_SUM + s:ST_SUM + s + 1], src_ap, axis=AX.X))
        atok = b.emit("scalar", lambda e: e.activation(
            sq_buf[:, 0:width], src_ap, AF.Square,
            accum_out=stA[:, ST_SQ + s:ST_SQ + s + 1]))
        b.wait("vector", atok)
        b.emit("vector", lambda e: e.tensor_scalar_mul(
            stA[:, ST_MEAN + s:ST_MEAN + s + 1],
            stA[:, ST_SUM + s:ST_SUM + s + 1], 1.0 / width))
        b.emit("vector", lambda e: e.tensor_mul(
            stA[:, ST_M2 + s:ST_M2 + s + 1],
            stA[:, ST_MEAN + s:ST_MEAN + s + 1],
            stA[:, ST_MEAN + s:ST_MEAN + s + 1]))
        vtok = b.emit("vector", lambda e: e.scalar_tensor_tensor(
            stA[:, ST_VAR + s:ST_VAR + s + 1], stA[:, ST_SQ + s:ST_SQ + s + 1],
            1.0 / width, stA[:, ST_M2 + s:ST_M2 + s + 1],
            ALU.mult, ALU.subtract))
        b.wait("scalar", vtok)
        b.emit("scalar", lambda e: e.activation(
            stA[:, ST_LNV + s:ST_LNV + s + 1], stA[:, ST_VAR + s:ST_VAR + s + 1],
            AF.Ln, bias=c.EPS))
        atok2 = b.emit("scalar", lambda e: e.activation(
            stA[:, ST_RSTD + s:ST_RSTD + s + 1], stA[:, ST_LNV + s:ST_LNV + s + 1],
            AF.Exp, scale=-0.5))
        b.wait("vector", atok2)
        b.wait("vector", extra_wait_v)
        tok = b.emit("vector", lambda e: e.tensor_scalar(
            dst_ap, src_ap, stA[:, ST_MEAN + s:ST_MEAN + s + 1],
            stA[:, ST_RSTD + s:ST_RSTD + s + 1], ALU.subtract, ALU.mult))
        b.ln_guard[s] = tok
        return tok

    b.dma("sync", ident[:], ins["ident_c"], ch="cst")
    b.dma("sync", maskv_sb[:], ins["maskv_c"].rearrange("a p bq -> p a bq"),
          ch="cst")
    b.dma("gpsimd", cos_sb[:], ins["cos_c"], ch="cos")
    b.dma("gpsimd", ssin_sb[:], ins["ssin_c"], ch="cos")
    b.emit("vector", lambda e: e.memset(ones_f[:], 1.0))
    b.emit("vector", lambda e: e.memset(c_eps[:], c.EPS))
    b.emit("vector", lambda e: e.memset(c_ncs[:], -c.CSHIFT))
    b.emit("vector", lambda e: e.memset(c_zero[:], 0.0))
    b.barrier()

    for ci, (w_in, w_sc, rows) in enumerate((
            (ins["wq"], wq_bf, c.D), (ins["wk"], wk_bf, c.KV),
            (ins["wv"], wv_bf, c.KV), (ins["wo"], wo_bf, c.D))):
        step = 256
        for r0 in range(0, rows, step):
            r1 = min(rows, r0 + step)
            b.dma("gpsimd", w_sc[r0:r1, :], w_in[r0:r1, :], ch=f"cast{ci}")
    qkv_cast = [b.dma_total(f"cast{i}") for i in range(4)]
    wo_cast = b.dma_total("cast3")

    QV = collections.deque()
    QVL = collections.deque()
    QT = collections.deque()
    LEAD = 4
    G3 = [(g3[i], f"g{i}") for i in range(5)]
    pst = {"rr": 0, "pend": collections.deque(), "gt": {}, "qc": 0,
           "sink_tok": [None, None], "gamma_last": None,
           "wq_rd": [None, None, None], "slab_st": None}
    pts_g = [psbf(i, 0) for i in range(2)]
    ptq_g = [psbf(2 + i, 0) for i in range(2)]
    pt_tok_g = [None, None]
    ptq_tok = [None, None]

    mats = [(ins["w1"], 0, w1T_h, False), (ins["w2"], 1, w2T_h, False),
            (ins["w3"], 2, w3T_h, True)]

    def gamma_load(w_in, r0):
        def fn():
            bid = pst["rr"] % len(G3)
            pst["rr"] += 1
            gb, ch = G3[bid]
            b.wait("gpsimd", pst["gt"].get(bid))
            b.dma("gpsimd", gb[:],
                  w_in.rearrange("a bq -> (a bq)").rearrange(
                      "(n p x) -> n p x", p=P, x=c.GCH)[r0], ch=ch)
            pst["pend"].append(bid)
        return fn

    def use_act():
        pref = pst.get("eng_pref", "A")
        if pref == "ALT":
            pst["alt"] = pst.get("alt", 0) + 1
            return pst["alt"] % 2 == 0
        return pref == "A"

    def gamma_comp(col):
        def fn():
            bid = pst["pend"].popleft()
            gb, ch = G3[bid]
            if use_act():
                b.wait("scalar", b.dma_total(ch))
                b.wait("scalar", pst["sink_tok"][0])
                tok = b.emit("scalar", lambda e, gb=gb, col=col: e.activation(
                    qa_s[:], gb[:], AF.Abs, accum_out=gpart[:, col:col + 1]))
                pst["sink_tok"][0] = tok
                pst["gamma_A"] = tok
            else:
                b.wait("vector", b.dma_total(ch))
                tok = b.emit("vector", lambda e, gb=gb, col=col: e.reduce_sum(
                    gpart[:, col:col + 1], gb[:], axis=AX.X,
                    apply_absolute_value=True))
                pst["gamma_V"] = tok
            pst["gt"][bid] = tok
        return fn

    mat_cols = {}
    gcol = 0
    nchunk = (c.HF * c.D) // (P * c.GCH)
    for (w_in, m, _, is_w3) in mats:
        mat_cols[m] = (gcol, nchunk)
        for t in range(nchunk):
            QV.append((gamma_load(w_in, t), gamma_comp(gcol)))
            gcol += 1
    assert gcol <= ngch

    def thr_item():
        b.wait("vector", pst.get("gamma_A"))
        b.wait("vector", pst.get("gamma_V"))
        for (w_in, m, _, is_w3) in mats:
            c0, nt = mat_cols[m]
            nelem = float(nt * P * c.GCH)
            tok = b.emit("vector", lambda e, c0=c0, nt=nt, m=m: e.reduce_sum(
                gpart[:, ngch + m:ngch + m + 1], gpart[:, c0:c0 + nt], axis=AX.X))
            b.wait("gpsimd", tok)
            tok = b.emit("gpsimd", lambda e, m=m: e.tensor_reduce(
                gsum[:, m:m + 1], gpart[:, ngch + m:ngch + m + 1], axis=AX.C,
                op=ALU.add))
            b.wait("gpsimd", tok)
            b.emit("gpsimd", lambda e, m=m, ne=nelem: e.tensor_scalar(
                gsum[:, 4 + m:5 + m], gsum[:, m:m + 1], 0.5 / ne, 0.5 * c.QEPS,
                ALU.mult, ALU.add))
            b.emit("gpsimd", lambda e, m=m: e.tensor_copy(
                fdump[0:1, 48:49], gsum[0:1, 4 + m:5 + m]))
            b.dma("gpsimd", thr_h[m], gsum[0:1, 4 + m:5 + m], ch="thr")
            ltok = b.dma("gpsimd", thr128[:, m:m + 1],
                         thr_h[m].broadcast_to((P, 1)), ch="thr")
            b.wait("gpsimd", ltok)
            pst["thr_done"] = b.emit("gpsimd", lambda e, m=m: e.tensor_scalar_mul(
                nthr128[:, m:m + 1], thr128[:, m:m + 1], -1.0))
    QV.append((None, thr_item))

    def transp_item(m, hc, wT_out, is_w3, vtok, tpar):
        def fn():
            b.wait("tensor", vtok)
            last_t = None
            for dc in range(c.DC):
                ps = dc % 2
                b.wait("tensor", ptq_tok[ps])
                src = wqh[:, tpar, dc, :]
                ttok = b.emit("tensor", lambda e, src=src, ps=ptq_g[ps]:
                              e.transpose(ps, src, ident[:]), inc="T")
                last_t = ttok
                eng = "scalar" if dc % 2 == 0 else "vector"
                b.wait(eng, ttok)
                b.wait(eng, pst["slab_st"])
                if eng == "scalar":
                    fn2 = (lambda e, dc=dc, ps=ptq_g[ps]:
                           e.copy(slab[:, dc * P:(dc + 1) * P], ps))
                else:
                    fn2 = (lambda e, dc=dc, ps=ptq_g[ps]:
                           e.tensor_copy(slab[:, dc * P:(dc + 1) * P], ps))
                ptq_tok[ps] = b.emit(eng, fn2)
            pst["wq_rd"][tpar] = last_t
            sl_f = slab[:].rearrange("p (a j) -> p a j", j=P)
            for t_ in fence([("scalar", sl_f[0:1, 0::2, P - 1], c.DC // 2),
                             ("vector", sl_f[0:1, 1::2, P - 1], c.DC // 2)]):
                b.wait("gpsimd", t_)
            if is_w3:
                wdq = c.D // c.DQ
                for dq in range(c.DQ):
                    pst["slab_st"] = b.dma(
                        "gpsimd", wT_out[dq, hc], slab[:, dq * wdq:(dq + 1) * wdq],
                        ch="sl0")
            else:
                pst["slab_st"] = b.dma("gpsimd", wT_out[hc], slab[:], ch="sl0")
        return fn

    def quant_load(w_in, hc, half, is_w3):
        def fn():
            bid = pst["rr"] % len(G3)
            pst["rr"] += 1
            gb, ch = G3[bid]
            b.wait("gpsimd", pst["gt"].get(bid))
            nb = c.GCH // P
            if is_w3:
                w3v = w_in.rearrange("(dc p) (hc j) -> hc p dc j", p=P, j=P)
                srcv = w3v[hc][:, half * nb:(half + 1) * nb, :]
                b.dma("gpsimd", gb[:].rearrange("p (a j) -> p a j", j=P),
                      srcv, ch=ch)
            else:
                b.dma("gpsimd", gb[:],
                      w_in[hc * P:(hc + 1) * P,
                           half * c.GCH:(half + 1) * c.GCH],
                      ch=ch)
            pst["pend"].append(bid)
        return fn

    def quant_comp(m, hc, half, is_w3, wT_out, last_half, tpar):
        def fn():
            bid = pst["pend"].popleft()
            gb, ch = G3[bid]
            nb = c.GCH // P
            dst = wqh[:, tpar, half * nb:(half + 1) * nb, :].rearrange(
                "p a j -> p (a j)")
            if use_act():
                par = pst["qc"] % 2
                pst["qc"] += 1
                sink = qa_s if par == 0 else qb_s
                b.wait("scalar", b.dma_total(ch))
                b.wait("scalar", pst["wq_rd"][tpar])
                b.wait("scalar", pst.get("thr_done"))
                b.wait("scalar", pst["sink_tok"][par])
                b.emit("scalar", lambda e, dst=dst, gb=gb, m=m: e.activation(
                    dst, gb[:], AF.Sign, bias=nthr128[:, m:m + 1]))
                stok = b.emit("scalar", lambda e, sink=sink, gb=gb, m=m:
                              e.activation(sink[:], gb[:], AF.Sign,
                                           bias=thr128[:, m:m + 1]))
                b.wait("vector", stok)
                vtok = b.emit("vector", lambda e, dst=dst, sink=sink:
                              e.tensor_add(dst, dst, sink[:]))
                pst["sink_tok"][par] = vtok
            else:
                b.wait("vector", b.dma_total(ch))
                b.wait("vector", pst["wq_rd"][tpar])
                b.wait("vector", pst.get("thr_done"))
                b.wait("vector", pst["sink_tok"][0])
                b.wait("vector", pst["sink_tok"][1])
                b.emit("vector", lambda e, gb=gb, m=m: e.tensor_scalar(
                    qa_s[:], gb[:], thr128[:, m:m + 1], 2.0,
                    ALU.is_ge, ALU.mult))
                b.emit("vector", lambda e, gb=gb, m=m: e.tensor_scalar(
                    qb_s[:], gb[:], nthr128[:, m:m + 1], 2.0,
                    ALU.is_le, ALU.mult))
                vtok = b.emit("vector", lambda e, dst=dst:
                              e.tensor_sub(dst, qa_s[:], qb_s[:]))
                pst["sink_tok"][0] = vtok
                pst["sink_tok"][1] = vtok
            pst["gt"][bid] = vtok
            if last_half:
                QT.append(transp_item(m, hc, wT_out, is_w3, vtok, tpar))
        return fn

    ti = 0
    for (w_in, m, wT_out, is_w3) in mats:
        width = c.D if not is_w3 else c.DC * P
        nhalf = width // c.GCH
        for hc in range(c.HC):
            for half in range(nhalf):
                QV.append((quant_load(w_in, hc, half, is_w3),
                           quant_comp(m, hc, half, is_w3, wT_out,
                                      half == nhalf - 1, ti % 3)))
            ti += 1

    for it in list(QV):
        QVL.append(it[0])
    for _ in range(LEAD):
        if QVL:
            fn0 = QVL.popleft()
            if fn0 is not None:
                fn0()

    def drip(q, n, qt_ok=True):
        for _ in range(n):
            if q is QV:
                if len(QT) > 2:
                    if not qt_ok:
                        return
                    while len(QT) > 2:
                        QT.popleft()()
                if q:
                    if QVL:
                        fn0 = QVL.popleft()
                        if fn0 is not None:
                            fn0()
                    _, comp = q.popleft()
                    comp()
            elif q:
                q.popleft()()

    x_in = ins["x"]
    NH = c.NH
    for bb in range(c.NB):
        b.emit("vector", lambda e: e.memset(vaug[:, :, :, c.HD:c.HD + 1], 1.0))

        for hf in range(2):
            t0 = hf * c.HTT
            pq = [psf(4 + i, c.CGW) for i in range(3)]
            pq_tok = [None, None, None]
            x_rd = [None]
            lnbf_guard = None
            for i in range(c.HTT):
                tt = t0 + i
                xb = x_t0
                b.wait("gpsimd", x_rd[0])
                xtok = b.dma("gpsimd", xb[:], x_in[bb, tt * P:(tt + 1) * P, :],
                             ch="x0")
                b.wait("vector", xtok)
                b.wait("scalar", xtok)
                vtok = ln_full(xb[:], ln1bf[:], sq_scr, c.D,
                               extra_wait_v=lnbf_guard)
                x_rd[0] = vtok
                b.wait("tensor", vtok)
                last_t = None
                for dc in range(c.DC):
                    ps = dc % 2
                    b.wait("tensor", pt_tok_g[ps])
                    ttok = b.emit("tensor", lambda e, dc=dc, ps=pts_g[ps]:
                                  e.transpose(ps, ln1bf[:, dc * P:(dc + 1) * P],
                                              ident[:]), inc="T")
                    last_t = ttok
                    eng = "scalar" if dc % 2 == 0 else "vector"
                    b.wait(eng, ttok)
                    if eng == "scalar":
                        fn = (lambda e, dc=dc, i=i, ps=pts_g[ps]: e.copy(
                            ln1Th[:, dc, i * P:(i + 1) * P], ps))
                    else:
                        fn = (lambda e, dc=dc, i=i, ps=pts_g[ps]: e.tensor_copy(
                            ln1Th[:, dc, i * P:(i + 1) * P], ps))
                    pt_tok_g[ps] = b.emit(eng, fn)
                lnbf_guard = last_t
                drip(QV, 1)
                drip(QT, 1)
            ln1_done = [("A", b.sem["A"][1]), ("V", b.sem["V"][1])]

            w_bufs = [wslA, wslB]
            w_rd = [None, None]
            for t_ in qkv_cast:
                b.wait("sync", t_)
            pqf = [psf(4, 1024), psf(6, 1024)]
            pqf_tok = [None] * 2
            pgi = 0
            for cg in range(c.CGN):
                wb = w_bufs[cg % 2]
                for t_ in ln1_done:
                    b.wait("sync", t_)
                b.wait("sync", w_rd[cg % 2])
                for dc in range(c.DC):
                    if cg < c.QCG:
                        srcw = wq_bf[cg * c.CGW:(cg + 1) * c.CGW,
                                     dc * P:(dc + 1) * P]
                    elif cg < c.QCG + c.KCG:
                        kk = cg - c.QCG
                        srcw = wk_bf[kk * c.CGW:(kk + 1) * c.CGW,
                                     dc * P:(dc + 1) * P]
                    else:
                        vv = cg - c.QCG - c.KCG
                        srcw = wv_bf[vv * c.CGW:(vv + 1) * c.CGW,
                                     dc * P:(dc + 1) * P]
                    b.dma("sync", wb[:, dc, :], srcw, transpose=True,
                          ch=f"ws{cg % 2}_{dc % 8}")
                ws_toks = [b.dma_total(f"ws{cg % 2}_{j}") for j in range(8)]
                if cg < c.QCG + c.KCG:
                    for blk in range(c.NH):
                        pi = pgi % 2
                        pgi += 1
                        b.wait("tensor", pqf_tok[pi])
                        for t_ in ws_toks:
                            b.wait("tensor", t_)
                        for t_ in ln1_done:
                            b.wait("tensor", t_)
                        mm_tok = None
                        for dc in range(c.DC):
                            for seg in range(c.SH // 512):
                                mm_tok = b.emit(
                                    "tensor",
                                    lambda e, dc=dc, blk=blk, seg=seg, wb=wb, pi=pi:
                                    e.matmul(
                                        pqf[pi][:, seg * 512:(seg + 1) * 512],
                                        wb[:, dc, blk * c.HD:(blk + 1) * c.HD],
                                        ln1Th[:, dc, seg * 512:(seg + 1) * 512],
                                        start=(dc == 0), stop=(dc == c.DC - 1)),
                                    inc="T" if (dc == c.DC - 1 and
                                                seg == c.SH // 512 - 1) else None)
                        w_rd[cg % 2] = mm_tok
                        if cg < c.QCG:
                            dst = qT[:, cg * c.NH + blk,
                                     t0 * P:t0 * P + c.SH]
                        else:
                            kk = cg - c.QCG
                            dst = kT[:, kk * c.NH + blk,
                                     t0 * P:t0 * P + c.SH]
                        cols = slice(t0 * P, t0 * P + c.SH)
                        h2 = c.HD // 2
                        b.wait("vector", mm_tok)
                        b.emit("vector", lambda e, pi=pi, dst=dst, cols=cols:
                               e.tensor_mul(dst, pqf[pi][:, 0:c.SH],
                                            cos_sb[:, cols]))
                        b.emit("vector", lambda e, pi=pi, cols=cols, h2=h2:
                               e.tensor_mul(rtmp[0:h2, 0:c.SH],
                                            pqf[pi][h2:c.HD, 0:c.SH],
                                            ssin_sb[0:h2, cols]))
                        b.emit("vector", lambda e, pi=pi, cols=cols, h2=h2:
                               e.tensor_mul(rtmp[h2:c.HD, 0:c.SH],
                                            pqf[pi][0:h2, 0:c.SH],
                                            ssin_sb[h2:c.HD, cols]))
                        vtok = b.emit("vector", lambda e, dst=dst: e.tensor_add(
                            dst, dst, rtmp[:, 0:c.SH]))
                        pqf_tok[pi] = vtok
                        drip(QV, 3)
                        drip(QT, 1)
                else:
                    vv = cg - c.QCG - c.KCG
                    for i in range(c.HTT):
                        tt = t0 + i
                        pi = pgi % 2
                        pgi += 1
                        b.wait("tensor", pqf_tok[pi])
                        for t_ in ws_toks:
                            b.wait("tensor", t_)
                        for t_ in ln1_done:
                            b.wait("tensor", t_)
                        mm_tok = None
                        for dc in range(c.DC):
                            mm_tok = b.emit(
                                "tensor",
                                lambda e, dc=dc, i=i, wb=wb, pi=pi: e.matmul(
                                    pqf[pi][:, 0:c.CGW],
                                    ln1Th[:, dc, i * P:(i + 1) * P],
                                    wb[:, dc, :], start=(dc == 0),
                                    stop=(dc == c.DC - 1)),
                                inc="T" if dc == c.DC - 1 else None)
                        w_rd[cg % 2] = mm_tok
                        b.wait("scalar", mm_tok)
                        pv = pqf[pi][:, 0:c.CGW].rearrange("p (g j) -> p g j",
                                                           j=c.HD)
                        atok = b.emit("scalar", lambda e, pv=pv, tt=tt, vv=vv:
                                      e.copy(vaug[:, vv * c.NH:(vv + 1) * c.NH, tt,
                                             0:c.HD], pv))
                        pqf_tok[pi] = atok
                        drip(QV, 3)
                        drip(QT, 1)
            b.barrier()

        pst["eng_pref"] = "V"
        if bb == 0 and not pst.get("grown") and NEXTRA > 0:
            pst["grown"] = True
            b.wait("gpsimd", ("T", b.sem["T"][1]))
            for j in range(NEXTRA):
                G3.append((g3x[j], f"gx{j}"))
            for _ in range(NEXTRA):
                if QVL:
                    fn0 = QVL.popleft()
                    if fn0 is not None:
                        fn0()

        pbs = [psf(0, c.QGW), psf(1, c.QGW), psf(7, c.QGW)]
        pos = [psf(3 + i, c.HD + 1) for i in range(2)]
        ptos = [psbf(5 + i, 0) for i in range(2)]
        pb_tok = [None] * 3
        po_tok = [None] * 2
        pto_tok = [None] * 2
        av_done = [None, None]
        groups = [(h, qg) for h in range(c.NQ) for qg in range(c.NG)]
        gstate = {}
        c_pend = collections.deque()

        def c_scores(gi):
            h, qg = groups[gi]
            g = h // c.GRP
            ab = gi % 2
            kts = c.QGT * qg + c.QGT
            exp_tok = None
            for kt in range(kts):
                pbi = (gstate.get("pbc", 0)) % 3
                gstate["pbc"] = gstate.get("pbc", 0) + 1
                b.wait("tensor", pb_tok[pbi])
                mtok = b.emit(
                    "tensor",
                    lambda e, g=g, kt=kt, h=h, qg=qg, pbi=pbi: e.matmul(
                        pbs[pbi], kT[:, g, kt * P:(kt + 1) * P],
                        qT[:, h, qg * c.QGW:(qg + 1) * c.QGW],
                        start=True, stop=True), inc="T")
                dep = mtok
                if kt >= c.QGT * qg:
                    i = kt - c.QGT * qg
                    b.wait("vector", mtok)
                    dep = b.emit("vector", lambda e, pbi=pbi, i=i:
                                 e.tensor_add(pbs[pbi], pbs[pbi],
                                              maskv_sb[:, i, :]))
                b.wait("scalar", dep)
                b.wait("scalar", av_done[ab])
                exp_tok = b.emit("scalar", lambda e, kt=kt, pbi=pbi, ab=ab:
                                 e.activation(attnT[:, ab, kt, :], pbs[pbi],
                                              AF.Exp, scale=c.SCALE,
                                              bias=-c.CSHIFT))
                pb_tok[pbi] = exp_tok
            gstate[gi] = exp_tok

        def c_av(gi):
            h, qg = groups[gi]
            g = h // c.GRP
            ab = gi % 2
            exp_tok = gstate.pop(gi)
            last_av = None
            for qt in range(c.QGT):
                qabs = qg * c.QGT + qt
                poi = (gstate.get("poc", 0)) % 2
                gstate["poc"] = gstate.get("poc", 0) + 1
                b.wait("tensor", po_tok[poi])
                b.wait("tensor", exp_tok)
                nkt = c.QGT * qg + qt + 1
                for kt in range(nkt):
                    last_av = b.emit(
                        "tensor",
                        lambda e, kt=kt, qt=qt, g=g, poi=poi, nkt=nkt, ab=ab:
                        e.matmul(pos[poi],
                                 attnT[:, ab, kt, qt * P:(qt + 1) * P],
                                 vaug[:, g, kt, :], start=(kt == 0),
                                 stop=(kt == nkt - 1)),
                        inc="T" if kt == nkt - 1 else None)
                b.wait("vector", last_av)
                oni = (gstate.get("onc", 0)) % 4
                gstate["onc"] = gstate.get("onc", 0) + 1
                b.wait("vector", gstate.get(("on", oni)))
                b.emit("vector", lambda e, poi=poi: e.reciprocal(
                    recip[:, poi:poi + 1], pos[poi][:, c.HD:c.HD + 1]))
                vtok = b.emit("vector", lambda e, poi=poi, oni=oni:
                              e.tensor_scalar_mul(
                                  o_nrm[:, oni, :], pos[poi][:, 0:c.HD],
                                  recip[:, poi:poi + 1]))
                po_tok[poi] = vtok

                def _otr(h=h, qabs=qabs, oni=oni, vtok=vtok):
                    pti = (gstate.get("ptc", 0)) % 2
                    gstate["ptc"] = gstate.get("ptc", 0) + 1
                    b.wait("tensor", vtok)
                    b.wait("tensor", pto_tok[pti])
                    ttok = b.emit("tensor", lambda e, oni=oni, pti=pti:
                                  e.transpose(ptos[pti], o_nrm[:, oni, :],
                                              ident[:]), inc="T")
                    gstate[("on", oni)] = ttok
                    b.wait("scalar", ttok)
                    pto_tok[pti] = b.emit(
                        "scalar", lambda e, h=h, qabs=qabs, pti=pti: e.copy(
                            oT[:, h, qabs * P:(qabs + 1) * P], ptos[pti]))
                c_pend.append(_otr)
                if len(c_pend) > 3:
                    c_pend.popleft()()
            av_done[ab] = last_av
            drip(QV, 3, qt_ok=False)

        c_scores(0)
        for gi in range(len(groups)):
            if gi + 1 < len(groups):
                c_scores(gi + 1)
            c_av(gi)
        while c_pend:
            c_pend.popleft()()
        b.barrier()
        pst["eng_pref"] = "ALT"
        while QV:
            drip(QV, 1)
        while QT:
            QT.popleft()()
        b.barrier()

        b.wait("sync", wo_cast)
        for dc in range(c.DC):
            b.dma("sync", woT[:, dc, :], wo_bf[0:c.D, dc * P:(dc + 1) * P],
                  transpose=True, ch=f"wo{dc % 8}")
        wo_loaded = [b.dma_total(f"wo{j}") for j in range(8)]
        pwo = psf(0, c.D)
        ptd = [psbf(4 + i, 0) for i in range(4)]
        pt_tok = [None] * 4
        pwo_tok = None
        x_rd = None
        y_st = None
        l2_st = None
        lnT_guard = None
        d_ao_rd = None
        for tt in range(c.TT):
            b.wait("gpsimd", x_rd)
            xtok = b.dma("gpsimd", x2_t[:], x_in[bb, tt * P:(tt + 1) * P, :],
                         ch="x2")
            for t_ in wo_loaded:
                b.wait("tensor", t_)
            b.wait("tensor", pwo_tok)
            mm_tok = None
            for dc in range(c.DC):
                for seg in range(c.D // 512):
                    last = (dc == c.DC - 1 and seg == c.D // 512 - 1)
                    mm_tok = b.emit(
                        "tensor", lambda e, dc=dc, seg=seg, tt=tt: e.matmul(
                            pwo[:, seg * 512:(seg + 1) * 512],
                            oT[:, dc, tt * P:(tt + 1) * P],
                            woT[:, dc, seg * 512:(seg + 1) * 512],
                            start=(dc == 0), stop=(dc == c.DC - 1)),
                        inc="T" if last else None)
            b.wait("scalar", mm_tok)
            b.wait("scalar", d_ao_rd)
            atok = b.emit("scalar", lambda e: e.copy(atn_o[:], pwo))
            pwo_tok = atok
            b.wait("vector", atok)
            b.wait("scalar", atok)
            d_ao_rd = ln_full(atn_o[:], tmp_d[:], sq_d, c.D)
            b.wait("vector", xtok)
            b.wait("vector", y_st)
            vtok = b.emit("vector", lambda e: e.tensor_add(
                y_t[:], tmp_d[:], x2_t[:]))
            x_rd = vtok
            for t_ in fence([("vector", y_t[0:1, c.D - 16:c.D], 16)]):
                b.wait("gpsimd", t_)
            y_st = b.dma("gpsimd", y_h[bb, tt * P:(tt + 1) * P, :], y_t[:],
                         ch="yst")
            b.wait("scalar", vtok)
            v2 = ln_full(y_t[:], ln2bf[:], sq_d, c.D, extra_wait_v=lnT_guard)
            b.wait("tensor", v2)
            last_t = None
            for dc in range(c.DC):
                ps = dc % 4
                b.wait("tensor", pt_tok[ps])
                b.wait("tensor", l2_st)
                ttok = b.emit("tensor", lambda e, dc=dc, ps=ptd[ps]: e.transpose(
                    ps, ln2bf[:, dc * P:(dc + 1) * P], ident[:]), inc="T")
                last_t = ttok
                eng = "scalar" if dc % 2 == 0 else "vector"
                b.wait(eng, ttok)
                if eng == "scalar":
                    fn = (lambda e, dc=dc, ps=ptd[ps]: e.copy(
                        ln2T_t[:, dc, :], ps))
                else:
                    fn = (lambda e, dc=dc, ps=ptd[ps]: e.tensor_copy(
                        ln2T_t[:, dc, :], ps))
                pt_tok[ps] = b.emit(eng, fn)
            lnT_guard = last_t
            for t_ in fence([("scalar", ln2T_t[0:1, 0::2, P - 1], c.DC // 2),
                             ("vector", ln2T_t[0:1, 1::2, P - 1], c.DC // 2)]):
                b.wait("gpsimd", t_)
            l2_st = b.dma("gpsimd", ln2T_hp[:, :, tt * P:(tt + 1) * P],
                          ln2T_t[:], ch="l2st")
        b.barrier()

        for hf in range(2):
            t0 = hf * c.HTT
            ltok = b.dma("sync", ln2Th[:], ln2T_hp[:, :, hf * c.SH:(hf + 1) * c.SH],
                         ch="l2ld")
            segs = c.SH // 512
            px1 = [psf(0 * segs, c.SH), psf(1 * segs, c.SH)]
            px2 = [psf(2 * segs, c.SH), psf(3 * segs, c.SH)]
            px1_tok = [None, None]
            px2_tok = [None, None]
            w_rd = [None, None]
            sig_guard = None
            for hc in range(c.HC):
                wi = hc % 2
                b.wait("sync", w_rd[wi])
                b.dma("sync", w1sl[:, wi, :, :].rearrange("p a bq -> p (a bq)"),
                      w1T_h[hc], ch=f"wab{wi}")
                l2 = b.dma("sync", w2sl[:, wi, :, :].rearrange("p a bq -> p (a bq)"),
                           w2T_h[hc], ch=f"wab{wi}")
                b.wait("tensor", l2)
                b.wait("tensor", ltok)
                b.wait("tensor", px1_tok[wi])
                b.wait("tensor", px2_tok[wi])
                mm_tok = None
                for dc in range(c.DC):
                    for seg in range(segs):
                        b.emit("tensor", lambda e, dc=dc, seg=seg, wi=wi:
                               e.matmul(px1[wi][:, seg * 512:(seg + 1) * 512],
                                        w1sl[:, wi, dc, :],
                                        ln2Th[:, dc, seg * 512:(seg + 1) * 512],
                                        start=(dc == 0), stop=(dc == c.DC - 1)))
                    for seg in range(segs):
                        last = (dc == c.DC - 1 and seg == segs - 1)
                        mm_tok = b.emit(
                            "tensor", lambda e, dc=dc, seg=seg, wi=wi: e.matmul(
                                px2[wi][:, seg * 512:(seg + 1) * 512],
                                w2sl[:, wi, dc, :],
                                ln2Th[:, dc, seg * 512:(seg + 1) * 512],
                                start=(dc == 0), stop=(dc == c.DC - 1)),
                            inc="T" if last else None)
                w_rd[wi] = mm_tok
                b.wait("scalar", mm_tok)
                b.wait("scalar", sig_guard)
                atok = b.emit("scalar", lambda e, wi=wi: e.activation(
                    sig[:, 0:c.SH], px1[wi], AF.Sigmoid, scale=0.5))
                px1_tok[wi] = atok
                b.wait("vector", atok)
                vtok = b.emit("vector", lambda e, hc=hc, wi=wi: e.tensor_mul(
                    hidT[:, hc, :], sig[:, 0:c.SH], px2[wi]))
                px2_tok[wi] = vtok
                sig_guard = vtok
            hid_done = ("V", b.sem["V"][1])
            e_guards = [w_rd[0], w_rd[1], b.dma_total("wab0"),
                        b.dma_total("wab1")]
            wdq = c.D // c.DQ
            pfs = [psf(i, wdq) for i in range(c.HTT)]
            pf_tok = [None] * c.HTT
            w3_rd = [None, None]
            for dq in range(c.DQ):
                for hc in range(c.HC):
                    wi = hc % 2
                    b.wait("sync", w3_rd[wi])
                    l3 = b.dma("sync", w3sl[:, wi, :], w3T_h[dq, hc],
                               ch=f"w3{wi}")
                    b.wait("tensor", l3)
                    b.wait("tensor", hid_done)
                    mm_tok = None
                    for i in range(c.HTT):
                        if hc == 0:
                            b.wait("tensor", pf_tok[i])
                        mm_tok = b.emit(
                            "tensor", lambda e, i=i, hc=hc, wi=wi: e.matmul(
                                pfs[i], hidT[:, hc, i * P:(i + 1) * P],
                                w3sl[:, wi, :], start=(hc == 0),
                                stop=(hc == c.HC - 1)),
                            inc="T" if i == c.HTT - 1 else None)
                    w3_rd[wi] = mm_tok
                    if hc == c.HC - 1:
                        b.wait("scalar", mm_tok)
                        for i in range(c.HTT):
                            pf_tok[i] = b.emit(
                                "scalar", lambda e, i=i, dq=dq: e.activation(
                                    ffnbf[:, i, dq * wdq:(dq + 1) * wdq],
                                    pfs[i], AF.Copy, scale=0.25))
            ffn_done = ("A", b.sem["A"][1])
            y_rd = None
            for i in range(c.HTT):
                tt = t0 + i
                b.wait("gpsimd", y_rd)
                b.wait("gpsimd", b.dma_total("ost0"))
                b.wait("gpsimd", b.dma_total("ost1"))
                ytok = b.dma("gpsimd", y_e[:], y_h[bb, tt * P:(tt + 1) * P, :],
                             ch="yld")
                for e_ in ("vector", "scalar"):
                    b.wait(e_, ffn_done)
                    b.wait(e_, ytok)
                    for t_ in e_guards:
                        b.wait(e_, t_)
                ln_full(ffnbf[:, i, :], tmp_e[:], sq_e, c.D)
                vtok = b.emit("vector", lambda e: e.tensor_add(
                    y_e[:], tmp_e[:], y_e[:]))
                y_rd = vtok
                for t_ in fence([("vector", y_e[0:1, c.D - 16:c.D], 16)]):
                    b.wait("gpsimd", t_)
                b.dma("gpsimd", out[bb, tt * P:(tt + 1) * P, :], y_e[:],
                      ch=f"ost{i % 2}")
            b.barrier()
        b.barrier()

    b.final_dma_waits("gpsimd")
    b.final_dma_waits("sync")

    with nc.Block() as block:
        b.run(block)
    stack.close()
    return nc


def build_nc(cfg):
    nc = bass.Bass("TRN2", target_bir_lowering=False, debug=False)
    c = cfg
    ins = {
        "x": nc.dram_tensor("x", [c.NB, c.S, c.D], F32, kind="ExternalInput").ap(),
        "wq": nc.dram_tensor("wq", [c.D, c.D], F32, kind="ExternalInput").ap(),
        "wk": nc.dram_tensor("wk", [c.KV, c.D], F32, kind="ExternalInput").ap(),
        "wv": nc.dram_tensor("wv", [c.KV, c.D], F32, kind="ExternalInput").ap(),
        "wo": nc.dram_tensor("wo", [c.D, c.D], F32, kind="ExternalInput").ap(),
        "w1": nc.dram_tensor("w1", [c.HF, c.D], F32, kind="ExternalInput").ap(),
        "w2": nc.dram_tensor("w2", [c.HF, c.D], F32, kind="ExternalInput").ap(),
        "w3": nc.dram_tensor("w3", [c.D, c.HF], F32, kind="ExternalInput").ap(),
        "cos_c": nc.dram_tensor("cos_c", [c.HD, c.S], F32, kind="ExternalInput").ap(),
        "ssin_c": nc.dram_tensor("ssin_c", [c.HD, c.S], F32, kind="ExternalInput").ap(),
        "ident_c": nc.dram_tensor("ident_c", [P, P], BF16, kind="ExternalInput").ap(),
        "maskv_c": nc.dram_tensor("maskv_c", [c.QGT, P, c.QGT * P], BF16,
                                  kind="ExternalInput").ap(),
    }
    out = nc.dram_tensor("out", [c.NB, c.S, c.D], F32, kind="ExternalOutput").ap()
    build(nc, ins, out, cfg)
    return nc


def kernel(L0_attn_x, L0_attn_Wq, L0_attn_Wk, L0_attn_Wv, L0_attn_Wo,
           L0_ffn_w1, L0_ffn_w2, L0_ffn_w3):
    cfg = Cfg()
    consts = host_constants(cfg)
    x = np.ascontiguousarray(np.asarray(L0_attn_x, dtype=np.float32))
    shards = np.split(x, N_CORES, axis=0)
    common = {
        "wq": np.ascontiguousarray(np.asarray(L0_attn_Wq, np.float32)),
        "wk": np.ascontiguousarray(np.asarray(L0_attn_Wk, np.float32)),
        "wv": np.ascontiguousarray(np.asarray(L0_attn_Wv, np.float32)),
        "wo": np.ascontiguousarray(np.asarray(L0_attn_Wo, np.float32)),
        "w1": np.ascontiguousarray(np.asarray(L0_ffn_w1, np.float32)),
        "w2": np.ascontiguousarray(np.asarray(L0_ffn_w2, np.float32)),
        "w3": np.ascontiguousarray(np.asarray(L0_ffn_w3, np.float32)),
        **consts,
    }
    nc = build_nc(cfg)
    in_maps = [{"x": s, **common} for s in shards]
    res = run_bass_kernel_spmd(nc, in_maps, core_ids=list(range(N_CORES)))
    return np.concatenate([np.asarray(r["out"], np.float32) for r in res.results],
                          axis=0)


# version 31
# speedup vs baseline: 1.1732x; 1.0420x over previous
import collections
import math
from contextlib import ExitStack

import numpy as np
import ml_dtypes
import concourse.bass as bass
import concourse.mybir as mybir
from concourse.bass_utils import run_bass_kernel_spmd

F32 = mybir.dt.float32
BF16 = mybir.dt.bfloat16
AF = mybir.ActivationFunctionType
ALU = mybir.AluOpType
AX = mybir.AxisListType

N_CORES = 8
P = 128
SBMAP = []


class Cfg:
    def __init__(self, S=2048, D=2048, HD=128, NQ=16, NKV=4, HF=5632, NB=2,
                 MAX_POS=4096, THETA=10000.0):
        self.S, self.D, self.HD, self.NQ, self.NKV, self.HF, self.NB = S, D, HD, NQ, NKV, HF, NB
        self.MAX_POS, self.THETA = MAX_POS, THETA
        self.KV = NKV * HD
        self.GRP = NQ // NKV
        self.TT = S // P
        self.HTT = self.TT // 2
        self.SH = S // 2
        self.DC = D // P
        self.HC = HF // P
        self.QGT = min(4, self.TT)
        self.NG = self.TT // self.QGT
        self.QGW = self.QGT * P
        self.DQ = max(1, D // 512)
        self.CGW = 256
        self.CGN = (D + 2 * self.KV) // self.CGW
        self.QCG = D // self.CGW
        self.KCG = self.KV // self.CGW
        self.NH = self.CGW // HD
        self.SCALE = 1.0 / math.sqrt(HD)
        self.CSHIFT = 30.0
        self.EPS = 1e-5
        self.QEPS = 1e-8
        self.GCH = 512
        assert self.KV % self.CGW == 0 and D % 512 == 0 and self.HTT <= 8
        assert self.SH % 512 == 0 and self.TT % self.QGT == 0
        assert self.D % self.GCH == 0
        assert (self.HF * self.D) % (128 * self.GCH) == 0


def host_constants(cfg):
    HD = cfg.HD
    k = np.arange(0, HD, 2)[:HD // 2].astype(np.float32)
    inv_freq = 1.0 / cfg.THETA ** (k / HD)
    positions = np.arange(cfg.MAX_POS, dtype=np.float32)
    angles = positions[:, None] * inv_freq[None, :]
    angles = np.concatenate([angles, angles], axis=-1)
    cos = np.cos(angles)[:cfg.S].astype(np.float32)
    sin = np.sin(angles)[:cfg.S].astype(np.float32)
    ssin = np.concatenate([-sin[:, :HD // 2], sin[:, HD // 2:]], axis=1)
    cos = np.ascontiguousarray(cos.T)
    ssin = np.ascontiguousarray(ssin.T)
    ident = np.eye(P, dtype=ml_dtypes.bfloat16)
    QGT = cfg.QGT
    maskv = np.zeros((QGT, P, QGT * P), dtype=np.float32)
    for i in range(QGT):
        for j in range(QGT):
            blk = maskv[i, :, j * P:(j + 1) * P]
            if j < i:
                blk[:] = -1e30
            elif j == i:
                pp, cc = np.meshgrid(np.arange(P), np.arange(P), indexing="ij")
                blk[pp > cc] = -1e30
    return {"cos_c": cos, "ssin_c": ssin, "ident_c": ident,
            "maskv_c": maskv.astype(ml_dtypes.bfloat16)}


ENGINES = ("sync", "gpsimd", "vector", "scalar", "tensor")


class B:

    _SELF = {"vector": "V", "scalar": "A", "gpsimd": "G"}

    def __init__(self, nc, stack):
        self.nc = nc
        self.stack = stack
        self.st = {e: [] for e in ENGINES}
        self.sem = {}
        self.hw = {e: {} for e in ENGINES}
        for name in ("T", "V", "A", "G"):
            h = stack.enter_context(nc.semaphore(name))
            self.sem[name] = [h, 0]
        self.ln_guard = [None, None]
        self.ln_i = 0

    def emit(self, eng, fn, inc=None, k=1):
        nm = self._SELF.get(eng)
        if nm is not None:
            s = self.sem[nm]
            self.wait(eng, (nm, s[1]))
            s[1] += 1
            tgt = s[1]
            self.st[eng].append(lambda e, fn=fn, h=s[0]: fn(e).then_inc(h, 1))
            return (nm, tgt)
        if inc is not None:
            s = self.sem[inc]
            s[1] += k
            tgt = s[1]
            self.st[eng].append(lambda e, fn=fn, h=s[0], k=k: fn(e).then_inc(h, k))
            return (inc, tgt)
        self.st[eng].append(fn)
        return None

    def dma(self, eng, out_ap, in_ap, ch="d0", **kw):
        nm = "D" + ch
        if nm not in self.sem:
            h = self.stack.enter_context(self.nc.semaphore(nm))
            self.sem[nm] = [h, 0]
        s = self.sem[nm]
        self.wait(eng, (nm, s[1]))
        s[1] += 16
        tgt = s[1]
        self.st[eng].append(
            lambda e, o=out_ap, i=in_ap, h=s[0], kw=kw: e.dma_start(o, i, **kw).then_inc(h, 16))
        return (nm, tgt)

    def dma_total(self, ch):
        nm = "D" + ch
        return (nm, self.sem[nm][1]) if nm in self.sem else None

    def final_dma_waits(self, eng):
        for nm, s in self.sem.items():
            if nm.startswith("D"):
                self.wait(eng, (nm, s[1]))

    def wait(self, eng, tok):
        if tok is None:
            return
        name, v = tok
        if self.hw[eng].get(name, 0) >= v:
            return
        self.hw[eng][name] = v
        h = self.sem[name][0]
        self.st[eng].append(lambda e, h=h, v=v: e.wait_ge(h, v))

    def barrier(self):
        toks = [(n, s[1]) for n, s in self.sem.items() if s[1] > 0]
        for e in ENGINES:
            for t in toks:
                self.wait(e, t)

    def run(self, block):
        @block.sync
        def _(e):
            for fn in self.st["sync"]:
                fn(e)

        @block.gpsimd
        def _(e):
            for fn in self.st["gpsimd"]:
                fn(e)

        @block.vector
        def _(e):
            for fn in self.st["vector"]:
                fn(e)

        @block.scalar
        def _(e):
            for fn in self.st["scalar"]:
                fn(e)

        @block.tensor
        def _(e):
            for fn in self.st["tensor"]:
                fn(e)


def build(nc, ins, out, cfg):
    c = cfg
    stack = ExitStack()
    b = B(nc, stack)

    PS = stack.enter_context(nc.psum_tensor("PS", [P, 4096], F32))

    def psf(bank, width):
        return PS[:, bank * 512:bank * 512 + width]

    def psbf(bank, col0, width=P):
        return PS[:].bitcast(BF16)[:, bank * 1024 + col0:bank * 1024 + col0 + width]

    dt = nc.dram_tensor
    wq_bf = dt("wq_bf", [c.D, c.D], BF16, kind="Internal").ap()
    wk_bf = dt("wk_bf", [c.KV, c.D], BF16, kind="Internal").ap()
    wv_bf = dt("wv_bf", [c.KV, c.D], BF16, kind="Internal").ap()
    wo_bf = dt("wo_bf", [c.D, c.D], BF16, kind="Internal").ap()
    w1T_h = dt("w1T_h", [c.HC, P, c.DC * P], BF16, kind="Internal").ap()
    w2T_h = dt("w2T_h", [c.HC, P, c.DC * P], BF16, kind="Internal").ap()
    w3T_h = dt("w3T_h", [c.DQ, c.HC, P, c.D // c.DQ], BF16, kind="Internal").ap()
    y_h = dt("y_h", [c.NB, c.S, c.D], F32, kind="Internal").ap()
    thr_h = dt("thr_h", [4, 1], F32, kind="Internal").ap()
    ln2T_h = dt("ln2T_h", [c.DC, P, c.S], BF16, kind="Internal").ap()
    ln2T_hp = ln2T_h.rearrange("dc p s -> p dc s")

    ARENA0 = 20480
    SB_LIMIT = nc.SBUF_PARTITION_SIZE_BYTES - 32
    ARENA = nc.alloc_sbuf_tensor_at("ARENA", [P, SB_LIMIT - ARENA0],
                                    mybir.dt.uint8, offset=ARENA0)
    off = [ARENA0]

    def sb(name, shape, dtype, at=None):
        o = off[0] if at is None else at
        esz = mybir.dt.size(dtype)
        n = int(np.prod(shape[1:]))
        assert (o - ARENA0) % esz == 0, (name, o)
        base = ARENA[:].bitcast(dtype)
        v = base[0:shape[0], (o - ARENA0) // esz:(o - ARENA0) // esz + n]
        if len(shape) == 3:
            v = v.rearrange("p (a bq) -> p a bq", a=shape[1])
        elif len(shape) == 4:
            v = v.rearrange("p (a bq cq) -> p a bq cq", a=shape[1], bq=shape[2])
        SBMAP.append((name, o, o + n * esz, shape, str(dtype)))
        off[0] = o + ((n * esz + 31) // 32) * 32
        return v

    ident = sb("ident", [P, P], BF16)
    cos_sb = sb("cos_sb", [P, c.S], BF16)
    ssin_sb = sb("ssin_sb", [P, c.S], BF16)
    maskv_sb = sb("maskv_sb", [P, c.QGT, c.QGW], BF16)
    ones_f = sb("ones_f", [1, P], F32)
    stA = sb("stA", [P, 16], F32)
    thr128 = sb("thr128", [P, 4], F32)
    nthr128 = sb("nthr128", [P, 4], F32)
    ngch = (2 * c.HF * c.D + c.D * c.HF) // (P * c.GCH)
    gpart = sb("gpart", [P, ngch + 8], F32)
    gsum = sb("gsum", [1, 8], F32)
    c_eps = sb("c_eps", [P, 1], F32)
    c_ncs = sb("c_ncs", [P, 1], F32)
    c_zero = sb("c_zero", [P, 1], F32)
    fdump = sb("fdump", [1, 64], F32)
    nc.const_aps.register(F32, c.EPS, c_eps)
    nc.const_aps.register(F32, -c.CSHIFT, c_ncs)
    nc.const_aps.register(F32, 0.0, c_zero)
    TA = off[0]

    wslA = sb("wslA", [P, c.DC, c.CGW], BF16, at=TA)
    wslB = sb("wslB", [P, c.DC, c.CGW], BF16)
    x_t0 = sb("x_t0", [P, c.D], F32)
    ln1bf = sb("ln1bf", [P, c.D], BF16)
    LN1TH_AT = off[0]
    ln1Th = sb("ln1Th", [P, c.DC, c.SH], BF16)
    RTMP_AT = off[0]
    rtmp = sb("rtmp", [P, c.SH], BF16)
    g3 = [sb(f"g3_{i}", [P, c.GCH], F32) for i in range(5)]
    qa_s = sb("qa_s", [P, c.GCH], BF16)
    qb_s = sb("qb_s", [P, c.GCH], BF16)
    wqh = sb("wqh", [P, 3, c.DC, P], BF16)
    slab = sb("slab", [P, c.DC * P], BF16)
    A_END = off[0]
    sq_scr = sb("sq_scr", [P, c.D], F32, at=TA)
    assert c.D * 4 <= 2 * c.DC * c.CGW * 2

    off[0] = TA
    attnT = sb("attnT", [P, 2, c.TT, c.QGW], BF16, at=TA)
    o_nrm = sb("o_nrm", [P, 4, P], BF16)
    recip = sb("recip", [P, 2], F32)
    C_END = off[0]

    off[0] = TA
    woT = sb("woT", [P, c.DC, c.D], BF16, at=TA)
    x2_t = sb("x2_t", [P, c.D], F32)
    sq_d = sb("sq_d", [P, c.D], F32)
    ln2T_t = sb("ln2T_t", [P, c.DC, P], BF16)
    D_TA_END = off[0]

    TA_SZ = max(A_END, C_END, D_TA_END) - TA
    LB = TA + TA_SZ

    qT = sb("qT", [P, c.NQ, c.S], BF16, at=LB)
    oT = sb("oT", [P, c.NQ, c.S], BF16, at=LB)
    kT = sb("kT", [P, c.NKV, c.S], BF16)
    vaug = sb("vaug", [P, c.NKV, c.TT, c.HD + 1], BF16)
    LB_END = off[0]
    KTOFF = LB + c.NQ * c.S * 2
    off[0] = KTOFF
    atn_o = sb("atn_o", [P, c.D], F32, at=KTOFF)
    tmp_d = sb("tmp_d", [P, c.D], F32)
    y_t = sb("y_t", [P, c.D], F32)
    ln2bf = sb("ln2bf", [P, c.D], BF16)
    D_END = off[0]

    off[0] = TA
    ln2Th = sb("ln2Th", [P, c.DC, c.SH], BF16, at=TA)
    hidT = sb("hidT", [P, c.HC, c.SH], BF16)
    W1SL_AT = off[0]
    w1sl = sb("w1sl", [P, 2, c.DC, P], BF16)
    W2SL_AT = off[0]
    w2sl = sb("w2sl", [P, 2, c.DC, P], BF16)
    w3sl = sb("w3sl", [P, 2, c.D // c.DQ], BF16)
    sig = sb("sig", [P, min(c.SH, 1024)], F32)
    ffnbf = sb("ffnbf", [P, c.HTT, c.D], BF16)
    y_e = sb("y_e", [P, c.D], F32)
    E_END0 = off[0]
    sq_e = sb("sq_e", [P, c.D], F32, at=W1SL_AT)
    tmp_e = sb("tmp_e", [P, c.D], F32, at=W2SL_AT)
    off[0] = E_END0
    E_END = off[0]
    assert c.D * 4 <= 2 * c.DC * P * 2

    base_x = (max(C_END, LN1TH_AT) + 31) & ~31
    NEXTRA = max(0, min(12, (RTMP_AT - base_x) // (c.GCH * 4)))
    g3x = [sb(f"g3x{j}", [P, c.GCH], F32, at=base_x + j * c.GCH * 4)
           for j in range(NEXTRA)]
    off[0] = E_END

    assert max(E_END, D_END, LB_END) <= SB_LIMIT, (
        "sbuf overflow", A_END, E_END, D_END, LB_END, SB_LIMIT)

    def fence(reads):
        toks = []
        for eng, ap, n in reads:
            base = 0 if eng == "scalar" else 32
            if eng == "scalar":
                toks.append(b.emit("scalar", lambda e, ap=ap, n=n, bo=base: e.copy(
                    fdump[0:1, bo:bo + n], ap)))
            else:
                toks.append(b.emit("vector",
                                   lambda e, ap=ap, n=n, bo=base: e.tensor_copy(
                                       fdump[0:1, bo:bo + n], ap)))
        return toks

    ST_SUM, ST_SQ, ST_MEAN, ST_M2, ST_VAR, ST_LNV, ST_RSTD = 0, 2, 4, 6, 8, 10, 12

    def ln_full(src_ap, dst_ap, sq_buf, width, extra_wait_v=None):
        s = b.ln_i % 2
        b.ln_i += 1
        g = b.ln_guard[s]
        b.wait("vector", g)
        b.wait("scalar", g)
        b.emit("vector", lambda e: e.reduce_sum(
            stA[:, ST_SUM + s:ST_SUM + s + 1], src_ap, axis=AX.X))
        atok = b.emit("scalar", lambda e: e.activation(
            sq_buf[:, 0:width], src_ap, AF.Square,
            accum_out=stA[:, ST_SQ + s:ST_SQ + s + 1]))
        b.wait("vector", atok)
        b.emit("vector", lambda e: e.tensor_scalar_mul(
            stA[:, ST_MEAN + s:ST_MEAN + s + 1],
            stA[:, ST_SUM + s:ST_SUM + s + 1], 1.0 / width))
        b.emit("vector", lambda e: e.tensor_mul(
            stA[:, ST_M2 + s:ST_M2 + s + 1],
            stA[:, ST_MEAN + s:ST_MEAN + s + 1],
            stA[:, ST_MEAN + s:ST_MEAN + s + 1]))
        vtok = b.emit("vector", lambda e: e.scalar_tensor_tensor(
            stA[:, ST_VAR + s:ST_VAR + s + 1], stA[:, ST_SQ + s:ST_SQ + s + 1],
            1.0 / width, stA[:, ST_M2 + s:ST_M2 + s + 1],
            ALU.mult, ALU.subtract))
        b.wait("scalar", vtok)
        b.emit("scalar", lambda e: e.activation(
            stA[:, ST_LNV + s:ST_LNV + s + 1], stA[:, ST_VAR + s:ST_VAR + s + 1],
            AF.Ln, bias=c.EPS))
        atok2 = b.emit("scalar", lambda e: e.activation(
            stA[:, ST_RSTD + s:ST_RSTD + s + 1], stA[:, ST_LNV + s:ST_LNV + s + 1],
            AF.Exp, scale=-0.5))
        b.wait("vector", atok2)
        b.wait("vector", extra_wait_v)
        tok = b.emit("vector", lambda e: e.tensor_scalar(
            dst_ap, src_ap, stA[:, ST_MEAN + s:ST_MEAN + s + 1],
            stA[:, ST_RSTD + s:ST_RSTD + s + 1], ALU.subtract, ALU.mult))
        b.ln_guard[s] = tok
        return tok

    b.dma("sync", ident[:], ins["ident_c"], ch="cst")
    b.dma("sync", maskv_sb[:], ins["maskv_c"].rearrange("a p bq -> p a bq"),
          ch="cst")
    b.dma("gpsimd", cos_sb[:], ins["cos_c"], ch="cos")
    b.dma("gpsimd", ssin_sb[:], ins["ssin_c"], ch="cos")
    b.emit("vector", lambda e: e.memset(ones_f[:], 1.0))
    b.emit("vector", lambda e: e.memset(c_eps[:], c.EPS))
    b.emit("vector", lambda e: e.memset(c_ncs[:], -c.CSHIFT))
    b.emit("vector", lambda e: e.memset(c_zero[:], 0.0))
    b.barrier()

    for ci, (w_in, w_sc, rows) in enumerate((
            (ins["wq"], wq_bf, c.D), (ins["wk"], wk_bf, c.KV),
            (ins["wv"], wv_bf, c.KV), (ins["wo"], wo_bf, c.D))):
        step = 256
        for r0 in range(0, rows, step):
            r1 = min(rows, r0 + step)
            b.dma("gpsimd", w_sc[r0:r1, :], w_in[r0:r1, :], ch=f"cast{ci}")
    qkv_cast = [b.dma_total(f"cast{i}") for i in range(4)]
    wo_cast = b.dma_total("cast3")

    QV = collections.deque()
    QVL = collections.deque()
    QT = collections.deque()
    LEAD = 4
    G3 = [(g3[i], f"g{i}") for i in range(5)]
    pst = {"rr": 0, "pend": collections.deque(), "gt": {}, "qc": 0,
           "sink_tok": [None, None], "gamma_last": None,
           "wq_rd": [None, None, None], "slab_st": None}
    pts_g = [psbf(i, 0) for i in range(2)]
    ptq_g = [psbf(2 + i, 0) for i in range(2)]
    ptq6 = [psbf(2 + i, 0) for i in range(6)]
    pt_tok_g = [None, None]
    ptq_tok = [None, None, None, None, None, None]

    mats = [(ins["w1"], 0, w1T_h, False), (ins["w2"], 1, w2T_h, False),
            (ins["w3"], 2, w3T_h, True)]

    def gamma_load(w_in, r0):
        def fn():
            bid = pst["rr"] % len(G3)
            pst["rr"] += 1
            gb, ch = G3[bid]
            b.wait("gpsimd", pst["gt"].get(bid))
            b.dma("gpsimd", gb[:],
                  w_in.rearrange("a bq -> (a bq)").rearrange(
                      "(n p x) -> n p x", p=P, x=c.GCH)[r0], ch=ch)
            pst["pend"].append(bid)
        return fn

    def use_act():
        pref = pst.get("eng_pref", "A")
        if pref == "ALT":
            pst["alt"] = pst.get("alt", 0) + 1
            return pst["alt"] % 2 == 0
        return pref == "A"

    def gamma_comp(col):
        def fn():
            bid = pst["pend"].popleft()
            gb, ch = G3[bid]
            if use_act():
                b.wait("scalar", b.dma_total(ch))
                b.wait("scalar", pst["sink_tok"][0])
                tok = b.emit("scalar", lambda e, gb=gb, col=col: e.activation(
                    qa_s[:], gb[:], AF.Abs, accum_out=gpart[:, col:col + 1]))
                pst["sink_tok"][0] = tok
                pst["gamma_A"] = tok
            else:
                b.wait("vector", b.dma_total(ch))
                tok = b.emit("vector", lambda e, gb=gb, col=col: e.reduce_sum(
                    gpart[:, col:col + 1], gb[:], axis=AX.X,
                    apply_absolute_value=True))
                pst["gamma_V"] = tok
            pst["gt"][bid] = tok
        return fn

    mat_cols = {}
    gcol = 0
    nchunk = (c.HF * c.D) // (P * c.GCH)
    for (w_in, m, _, is_w3) in mats:
        mat_cols[m] = (gcol, nchunk)
        for t in range(nchunk):
            QV.append((gamma_load(w_in, t), gamma_comp(gcol)))
            gcol += 1
    assert gcol <= ngch

    def thr_item():
        b.wait("vector", pst.get("gamma_A"))
        b.wait("vector", pst.get("gamma_V"))
        for (w_in, m, _, is_w3) in mats:
            c0, nt = mat_cols[m]
            nelem = float(nt * P * c.GCH)
            tok = b.emit("vector", lambda e, c0=c0, nt=nt, m=m: e.reduce_sum(
                gpart[:, ngch + m:ngch + m + 1], gpart[:, c0:c0 + nt], axis=AX.X))
            b.wait("gpsimd", tok)
            tok = b.emit("gpsimd", lambda e, m=m: e.tensor_reduce(
                gsum[:, m:m + 1], gpart[:, ngch + m:ngch + m + 1], axis=AX.C,
                op=ALU.add))
            b.wait("gpsimd", tok)
            b.emit("gpsimd", lambda e, m=m, ne=nelem: e.tensor_scalar(
                gsum[:, 4 + m:5 + m], gsum[:, m:m + 1], 0.5 / ne, 0.5 * c.QEPS,
                ALU.mult, ALU.add))
            b.emit("gpsimd", lambda e, m=m: e.tensor_copy(
                fdump[0:1, 48:49], gsum[0:1, 4 + m:5 + m]))
            b.dma("gpsimd", thr_h[m], gsum[0:1, 4 + m:5 + m], ch="thr")
            ltok = b.dma("gpsimd", thr128[:, m:m + 1],
                         thr_h[m].broadcast_to((P, 1)), ch="thr")
            b.wait("gpsimd", ltok)
            pst["thr_done"] = b.emit("gpsimd", lambda e, m=m: e.tensor_scalar_mul(
                nthr128[:, m:m + 1], thr128[:, m:m + 1], -1.0))
    QV.append((None, thr_item))

    def transp_item(m, hc, wT_out, is_w3, vtok, tpar):
        def fn():
            nrot = 6 if pst.get("drain") else 2
            rot = ptq6 if nrot == 6 else ptq_g
            b.wait("tensor", vtok)
            last_t = None
            for dc in range(c.DC):
                ps = dc % nrot
                b.wait("tensor", ptq_tok[ps])
                src = wqh[:, tpar, dc, :]
                ttok = b.emit("tensor", lambda e, src=src, ps=rot[ps]:
                              e.transpose(ps, src, ident[:]), inc="T")
                last_t = ttok
                eng = "scalar" if dc % 2 == 0 else "vector"
                b.wait(eng, ttok)
                b.wait(eng, pst["slab_st"])
                if eng == "scalar":
                    fn2 = (lambda e, dc=dc, ps=rot[ps]:
                           e.copy(slab[:, dc * P:(dc + 1) * P], ps))
                else:
                    fn2 = (lambda e, dc=dc, ps=rot[ps]:
                           e.tensor_copy(slab[:, dc * P:(dc + 1) * P], ps))
                ptq_tok[ps] = b.emit(eng, fn2)
            pst["wq_rd"][tpar] = last_t
            sl_f = slab[:].rearrange("p (a j) -> p a j", j=P)
            for t_ in fence([("scalar", sl_f[0:1, 0::2, P - 1], c.DC // 2),
                             ("vector", sl_f[0:1, 1::2, P - 1], c.DC // 2)]):
                b.wait("gpsimd", t_)
            if is_w3:
                wdq = c.D // c.DQ
                for dq in range(c.DQ):
                    pst["slab_st"] = b.dma(
                        "gpsimd", wT_out[dq, hc], slab[:, dq * wdq:(dq + 1) * wdq],
                        ch="sl0")
            else:
                pst["slab_st"] = b.dma("gpsimd", wT_out[hc], slab[:], ch="sl0")
        return fn

    def quant_load(w_in, hc, half, is_w3):
        def fn():
            bid = pst["rr"] % len(G3)
            pst["rr"] += 1
            gb, ch = G3[bid]
            b.wait("gpsimd", pst["gt"].get(bid))
            nb = c.GCH // P
            if is_w3:
                w3v = w_in.rearrange("(dc p) (hc j) -> hc p dc j", p=P, j=P)
                srcv = w3v[hc][:, half * nb:(half + 1) * nb, :]
                b.dma("gpsimd", gb[:].rearrange("p (a j) -> p a j", j=P),
                      srcv, ch=ch)
            else:
                b.dma("gpsimd", gb[:],
                      w_in[hc * P:(hc + 1) * P,
                           half * c.GCH:(half + 1) * c.GCH],
                      ch=ch)
            pst["pend"].append(bid)
        return fn

    def quant_comp(m, hc, half, is_w3, wT_out, last_half, tpar):
        def fn():
            bid = pst["pend"].popleft()
            gb, ch = G3[bid]
            nb = c.GCH // P
            dst = wqh[:, tpar, half * nb:(half + 1) * nb, :].rearrange(
                "p a j -> p (a j)")
            if use_act():
                par = pst["qc"] % 2
                pst["qc"] += 1
                sink = qa_s if par == 0 else qb_s
                b.wait("scalar", b.dma_total(ch))
                b.wait("scalar", pst["wq_rd"][tpar])
                b.wait("scalar", pst.get("thr_done"))
                b.wait("scalar", pst["sink_tok"][par])
                b.emit("scalar", lambda e, dst=dst, gb=gb, m=m: e.activation(
                    dst, gb[:], AF.Sign, bias=nthr128[:, m:m + 1]))
                stok = b.emit("scalar", lambda e, sink=sink, gb=gb, m=m:
                              e.activation(sink[:], gb[:], AF.Sign,
                                           bias=thr128[:, m:m + 1]))
                b.wait("vector", stok)
                vtok = b.emit("vector", lambda e, dst=dst, sink=sink:
                              e.tensor_add(dst, dst, sink[:]))
                pst["sink_tok"][par] = vtok
            else:
                b.wait("vector", b.dma_total(ch))
                b.wait("vector", pst["wq_rd"][tpar])
                b.wait("vector", pst.get("thr_done"))
                b.wait("vector", pst["sink_tok"][0])
                b.wait("vector", pst["sink_tok"][1])
                b.emit("vector", lambda e, gb=gb, m=m: e.tensor_scalar(
                    qa_s[:], gb[:], thr128[:, m:m + 1], 2.0,
                    ALU.is_ge, ALU.mult))
                b.emit("vector", lambda e, gb=gb, m=m: e.tensor_scalar(
                    qb_s[:], gb[:], nthr128[:, m:m + 1], 2.0,
                    ALU.is_le, ALU.mult))
                vtok = b.emit("vector", lambda e, dst=dst:
                              e.tensor_sub(dst, qa_s[:], qb_s[:]))
                pst["sink_tok"][0] = vtok
                pst["sink_tok"][1] = vtok
            pst["gt"][bid] = vtok
            if last_half:
                QT.append(transp_item(m, hc, wT_out, is_w3, vtok, tpar))
        return fn

    ti = 0
    for (w_in, m, wT_out, is_w3) in mats:
        width = c.D if not is_w3 else c.DC * P
        nhalf = width // c.GCH
        for hc in range(c.HC):
            for half in range(nhalf):
                QV.append((quant_load(w_in, hc, half, is_w3),
                           quant_comp(m, hc, half, is_w3, wT_out,
                                      half == nhalf - 1, ti % 3)))
            ti += 1

    for it in list(QV):
        QVL.append(it[0])
    for _ in range(LEAD):
        if QVL:
            fn0 = QVL.popleft()
            if fn0 is not None:
                fn0()

    def drip(q, n, qt_ok=True):
        for _ in range(n):
            if q is QV:
                if len(QT) > 2:
                    if not qt_ok:
                        return
                    while len(QT) > 2:
                        QT.popleft()()
                if q:
                    if QVL:
                        fn0 = QVL.popleft()
                        if fn0 is not None:
                            fn0()
                    _, comp = q.popleft()
                    comp()
            elif q:
                q.popleft()()

    x_in = ins["x"]
    NH = c.NH
    for bb in range(c.NB):
        b.emit("vector", lambda e: e.memset(vaug[:, :, :, c.HD:c.HD + 1], 1.0))

        for hf in range(2):
            t0 = hf * c.HTT
            pq = [psf(4 + i, c.CGW) for i in range(3)]
            pq_tok = [None, None, None]
            x_rd = [None]
            lnbf_guard = None
            for i in range(c.HTT):
                tt = t0 + i
                xb = x_t0
                b.wait("gpsimd", x_rd[0])
                xtok = b.dma("gpsimd", xb[:], x_in[bb, tt * P:(tt + 1) * P, :],
                             ch="x0")
                b.wait("vector", xtok)
                b.wait("scalar", xtok)
                vtok = ln_full(xb[:], ln1bf[:], sq_scr, c.D,
                               extra_wait_v=lnbf_guard)
                x_rd[0] = vtok
                b.wait("tensor", vtok)
                last_t = None
                for dc in range(c.DC):
                    ps = dc % 2
                    b.wait("tensor", pt_tok_g[ps])
                    ttok = b.emit("tensor", lambda e, dc=dc, ps=pts_g[ps]:
                                  e.transpose(ps, ln1bf[:, dc * P:(dc + 1) * P],
                                              ident[:]), inc="T")
                    last_t = ttok
                    eng = "scalar" if dc % 2 == 0 else "vector"
                    b.wait(eng, ttok)
                    if eng == "scalar":
                        fn = (lambda e, dc=dc, i=i, ps=pts_g[ps]: e.copy(
                            ln1Th[:, dc, i * P:(i + 1) * P], ps))
                    else:
                        fn = (lambda e, dc=dc, i=i, ps=pts_g[ps]: e.tensor_copy(
                            ln1Th[:, dc, i * P:(i + 1) * P], ps))
                    pt_tok_g[ps] = b.emit(eng, fn)
                lnbf_guard = last_t
                drip(QV, 1)
                drip(QT, 1)
            ln1_done = [("A", b.sem["A"][1]), ("V", b.sem["V"][1])]

            w_bufs = [wslA, wslB]
            w_rd = [None, None]
            for t_ in qkv_cast:
                b.wait("sync", t_)
            pqf = [psf(4, 1024), psf(6, 1024)]
            pqf_tok = [None] * 2
            pgi = 0
            for cg in range(c.CGN):
                wb = w_bufs[cg % 2]
                for t_ in ln1_done:
                    b.wait("sync", t_)
                b.wait("sync", w_rd[cg % 2])
                for dc in range(c.DC):
                    if cg < c.QCG:
                        srcw = wq_bf[cg * c.CGW:(cg + 1) * c.CGW,
                                     dc * P:(dc + 1) * P]
                    elif cg < c.QCG + c.KCG:
                        kk = cg - c.QCG
                        srcw = wk_bf[kk * c.CGW:(kk + 1) * c.CGW,
                                     dc * P:(dc + 1) * P]
                    else:
                        vv = cg - c.QCG - c.KCG
                        srcw = wv_bf[vv * c.CGW:(vv + 1) * c.CGW,
                                     dc * P:(dc + 1) * P]
                    b.dma("sync", wb[:, dc, :], srcw, transpose=True,
                          ch=f"ws{cg % 2}_{dc % 4}")
                ws_toks = [b.dma_total(f"ws{cg % 2}_{j}") for j in range(4)]
                if cg < c.QCG + c.KCG:
                    for blk in range(c.NH):
                        pi = pgi % 2
                        pgi += 1
                        b.wait("tensor", pqf_tok[pi])
                        for t_ in ws_toks:
                            b.wait("tensor", t_)
                        for t_ in ln1_done:
                            b.wait("tensor", t_)
                        mm_tok = None
                        for dc in range(c.DC):
                            for seg in range(c.SH // 512):
                                mm_tok = b.emit(
                                    "tensor",
                                    lambda e, dc=dc, blk=blk, seg=seg, wb=wb, pi=pi:
                                    e.matmul(
                                        pqf[pi][:, seg * 512:(seg + 1) * 512],
                                        wb[:, dc, blk * c.HD:(blk + 1) * c.HD],
                                        ln1Th[:, dc, seg * 512:(seg + 1) * 512],
                                        start=(dc == 0), stop=(dc == c.DC - 1)),
                                    inc="T" if (dc == c.DC - 1 and
                                                seg == c.SH // 512 - 1) else None)
                        w_rd[cg % 2] = mm_tok
                        if cg < c.QCG:
                            dst = qT[:, cg * c.NH + blk,
                                     t0 * P:t0 * P + c.SH]
                        else:
                            kk = cg - c.QCG
                            dst = kT[:, kk * c.NH + blk,
                                     t0 * P:t0 * P + c.SH]
                        cols = slice(t0 * P, t0 * P + c.SH)
                        h2 = c.HD // 2
                        b.wait("vector", mm_tok)
                        b.emit("vector", lambda e, pi=pi, dst=dst, cols=cols:
                               e.tensor_mul(dst, pqf[pi][:, 0:c.SH],
                                            cos_sb[:, cols]))
                        b.emit("vector", lambda e, pi=pi, cols=cols, h2=h2:
                               e.tensor_mul(rtmp[0:h2, 0:c.SH],
                                            pqf[pi][h2:c.HD, 0:c.SH],
                                            ssin_sb[0:h2, cols]))
                        b.emit("vector", lambda e, pi=pi, cols=cols, h2=h2:
                               e.tensor_mul(rtmp[h2:c.HD, 0:c.SH],
                                            pqf[pi][0:h2, 0:c.SH],
                                            ssin_sb[h2:c.HD, cols]))
                        vtok = b.emit("vector", lambda e, dst=dst: e.tensor_add(
                            dst, dst, rtmp[:, 0:c.SH]))
                        pqf_tok[pi] = vtok
                        drip(QV, 3)
                        drip(QT, 1)
                else:
                    vv = cg - c.QCG - c.KCG
                    for i in range(c.HTT):
                        tt = t0 + i
                        pi = pgi % 2
                        pgi += 1
                        b.wait("tensor", pqf_tok[pi])
                        for t_ in ws_toks:
                            b.wait("tensor", t_)
                        for t_ in ln1_done:
                            b.wait("tensor", t_)
                        mm_tok = None
                        for dc in range(c.DC):
                            mm_tok = b.emit(
                                "tensor",
                                lambda e, dc=dc, i=i, wb=wb, pi=pi: e.matmul(
                                    pqf[pi][:, 0:c.CGW],
                                    ln1Th[:, dc, i * P:(i + 1) * P],
                                    wb[:, dc, :], start=(dc == 0),
                                    stop=(dc == c.DC - 1)),
                                inc="T" if dc == c.DC - 1 else None)
                        w_rd[cg % 2] = mm_tok
                        b.wait("scalar", mm_tok)
                        pv = pqf[pi][:, 0:c.CGW].rearrange("p (g j) -> p g j",
                                                           j=c.HD)
                        atok = b.emit("scalar", lambda e, pv=pv, tt=tt, vv=vv:
                                      e.copy(vaug[:, vv * c.NH:(vv + 1) * c.NH, tt,
                                             0:c.HD], pv))
                        pqf_tok[pi] = atok
                        drip(QV, 3)
                        drip(QT, 1)
            b.barrier()

        pst["eng_pref"] = "V"
        if bb == 0 and not pst.get("grown") and NEXTRA > 0:
            pst["grown"] = True
            b.wait("gpsimd", ("T", b.sem["T"][1]))
            for j in range(NEXTRA):
                G3.append((g3x[j], f"gx{j}"))
            for _ in range(NEXTRA):
                if QVL:
                    fn0 = QVL.popleft()
                    if fn0 is not None:
                        fn0()

        pbs = [psf(0, c.QGW), psf(1, c.QGW), psf(7, c.QGW)]
        pos = [psf(3 + i, c.HD + 1) for i in range(2)]
        ptos = [psbf(5 + i, 0) for i in range(2)]
        pb_tok = [None] * 3
        po_tok = [None] * 2
        pto_tok = [None] * 2
        av_done = [None, None]
        groups = [(h, qg) for h in range(c.NQ) for qg in range(c.NG)]
        gstate = {}
        c_pend = collections.deque()

        def c_scores(gi):
            h, qg = groups[gi]
            g = h // c.GRP
            ab = gi % 2
            kts = c.QGT * qg + c.QGT
            exp_tok = None
            for kt in range(kts):
                pbi = (gstate.get("pbc", 0)) % 3
                gstate["pbc"] = gstate.get("pbc", 0) + 1
                b.wait("tensor", pb_tok[pbi])
                mtok = b.emit(
                    "tensor",
                    lambda e, g=g, kt=kt, h=h, qg=qg, pbi=pbi: e.matmul(
                        pbs[pbi], kT[:, g, kt * P:(kt + 1) * P],
                        qT[:, h, qg * c.QGW:(qg + 1) * c.QGW],
                        start=True, stop=True), inc="T")
                dep = mtok
                if kt >= c.QGT * qg:
                    i = kt - c.QGT * qg
                    b.wait("vector", mtok)
                    dep = b.emit("vector", lambda e, pbi=pbi, i=i:
                                 e.tensor_add(pbs[pbi], pbs[pbi],
                                              maskv_sb[:, i, :]))
                b.wait("scalar", dep)
                b.wait("scalar", av_done[ab])
                exp_tok = b.emit("scalar", lambda e, kt=kt, pbi=pbi, ab=ab:
                                 e.activation(attnT[:, ab, kt, :], pbs[pbi],
                                              AF.Exp, scale=c.SCALE,
                                              bias=-c.CSHIFT))
                pb_tok[pbi] = exp_tok
            gstate[gi] = exp_tok

        def c_av(gi):
            h, qg = groups[gi]
            g = h // c.GRP
            ab = gi % 2
            exp_tok = gstate.pop(gi)
            last_av = None
            for qt in range(c.QGT):
                qabs = qg * c.QGT + qt
                poi = (gstate.get("poc", 0)) % 2
                gstate["poc"] = gstate.get("poc", 0) + 1
                b.wait("tensor", po_tok[poi])
                b.wait("tensor", exp_tok)
                nkt = c.QGT * qg + qt + 1
                for kt in range(nkt):
                    last_av = b.emit(
                        "tensor",
                        lambda e, kt=kt, qt=qt, g=g, poi=poi, nkt=nkt, ab=ab:
                        e.matmul(pos[poi],
                                 attnT[:, ab, kt, qt * P:(qt + 1) * P],
                                 vaug[:, g, kt, :], start=(kt == 0),
                                 stop=(kt == nkt - 1)),
                        inc="T" if kt == nkt - 1 else None)
                b.wait("vector", last_av)
                oni = (gstate.get("onc", 0)) % 4
                gstate["onc"] = gstate.get("onc", 0) + 1
                b.wait("vector", gstate.get(("on", oni)))
                b.emit("vector", lambda e, poi=poi: e.reciprocal(
                    recip[:, poi:poi + 1], pos[poi][:, c.HD:c.HD + 1]))
                vtok = b.emit("vector", lambda e, poi=poi, oni=oni:
                              e.tensor_scalar_mul(
                                  o_nrm[:, oni, :], pos[poi][:, 0:c.HD],
                                  recip[:, poi:poi + 1]))
                po_tok[poi] = vtok

                def _otr(h=h, qabs=qabs, oni=oni, vtok=vtok):
                    pti = (gstate.get("ptc", 0)) % 2
                    gstate["ptc"] = gstate.get("ptc", 0) + 1
                    b.wait("tensor", vtok)
                    b.wait("tensor", pto_tok[pti])
                    ttok = b.emit("tensor", lambda e, oni=oni, pti=pti:
                                  e.transpose(ptos[pti], o_nrm[:, oni, :],
                                              ident[:]), inc="T")
                    gstate[("on", oni)] = ttok
                    b.wait("scalar", ttok)
                    pto_tok[pti] = b.emit(
                        "scalar", lambda e, h=h, qabs=qabs, pti=pti: e.copy(
                            oT[:, h, qabs * P:(qabs + 1) * P], ptos[pti]))
                c_pend.append(_otr)
                if len(c_pend) > 3:
                    c_pend.popleft()()
            av_done[ab] = last_av
            drip(QV, 3, qt_ok=False)

        c_scores(0)
        for gi in range(len(groups)):
            if gi + 1 < len(groups):
                c_scores(gi + 1)
            c_av(gi)
        while c_pend:
            c_pend.popleft()()
        b.barrier()
        pst["eng_pref"] = "ALT"
        pst["drain"] = True
        while QV:
            drip(QV, 1)
        while QT:
            QT.popleft()()
        pst["drain"] = False
        b.barrier()

        b.wait("sync", wo_cast)
        for dc in range(c.DC):
            b.dma("sync", woT[:, dc, :], wo_bf[0:c.D, dc * P:(dc + 1) * P],
                  transpose=True, ch=f"wo{dc % 4}")
        wo_loaded = [b.dma_total(f"wo{j}") for j in range(4)]
        pwo = psf(0, c.D)
        ptd = [psbf(4 + i, 0) for i in range(4)]
        pt_tok = [None] * 4
        pwo_tok = None
        x_rd = None
        y_st = None
        l2_st = None
        lnT_guard = None
        d_ao_rd = None
        for tt in range(c.TT):
            b.wait("gpsimd", x_rd)
            xtok = b.dma("gpsimd", x2_t[:], x_in[bb, tt * P:(tt + 1) * P, :],
                         ch="x2")
            for t_ in wo_loaded:
                b.wait("tensor", t_)
            b.wait("tensor", pwo_tok)
            mm_tok = None
            for dc in range(c.DC):
                for seg in range(c.D // 512):
                    last = (dc == c.DC - 1 and seg == c.D // 512 - 1)
                    mm_tok = b.emit(
                        "tensor", lambda e, dc=dc, seg=seg, tt=tt: e.matmul(
                            pwo[:, seg * 512:(seg + 1) * 512],
                            oT[:, dc, tt * P:(tt + 1) * P],
                            woT[:, dc, seg * 512:(seg + 1) * 512],
                            start=(dc == 0), stop=(dc == c.DC - 1)),
                        inc="T" if last else None)
            b.wait("scalar", mm_tok)
            b.wait("scalar", d_ao_rd)
            atok = b.emit("scalar", lambda e: e.copy(atn_o[:], pwo))
            pwo_tok = atok
            b.wait("vector", atok)
            b.wait("scalar", atok)
            d_ao_rd = ln_full(atn_o[:], tmp_d[:], sq_d, c.D)
            b.wait("vector", xtok)
            b.wait("vector", y_st)
            vtok = b.emit("vector", lambda e: e.tensor_add(
                y_t[:], tmp_d[:], x2_t[:]))
            x_rd = vtok
            for t_ in fence([("vector", y_t[0:1, c.D - 16:c.D], 16)]):
                b.wait("gpsimd", t_)
            y_st = b.dma("gpsimd", y_h[bb, tt * P:(tt + 1) * P, :], y_t[:],
                         ch="yst")
            b.wait("scalar", vtok)
            v2 = ln_full(y_t[:], ln2bf[:], sq_d, c.D, extra_wait_v=lnT_guard)
            b.wait("tensor", v2)
            last_t = None
            for dc in range(c.DC):
                ps = dc % 4
                b.wait("tensor", pt_tok[ps])
                b.wait("tensor", l2_st)
                ttok = b.emit("tensor", lambda e, dc=dc, ps=ptd[ps]: e.transpose(
                    ps, ln2bf[:, dc * P:(dc + 1) * P], ident[:]), inc="T")
                last_t = ttok
                eng = "scalar" if dc % 2 == 0 else "vector"
                b.wait(eng, ttok)
                if eng == "scalar":
                    fn = (lambda e, dc=dc, ps=ptd[ps]: e.copy(
                        ln2T_t[:, dc, :], ps))
                else:
                    fn = (lambda e, dc=dc, ps=ptd[ps]: e.tensor_copy(
                        ln2T_t[:, dc, :], ps))
                pt_tok[ps] = b.emit(eng, fn)
            lnT_guard = last_t
            for t_ in fence([("scalar", ln2T_t[0:1, 0::2, P - 1], c.DC // 2),
                             ("vector", ln2T_t[0:1, 1::2, P - 1], c.DC // 2)]):
                b.wait("gpsimd", t_)
            l2_st = b.dma("gpsimd", ln2T_hp[:, :, tt * P:(tt + 1) * P],
                          ln2T_t[:], ch="l2st")
        b.barrier()

        for hf in range(2):
            t0 = hf * c.HTT
            ltok = b.dma("sync", ln2Th[:], ln2T_hp[:, :, hf * c.SH:(hf + 1) * c.SH],
                         ch="l2ld")
            segs = c.SH // 512
            px1 = [psf(0 * segs, c.SH), psf(1 * segs, c.SH)]
            px2 = [psf(2 * segs, c.SH), psf(3 * segs, c.SH)]
            px1_tok = [None, None]
            px2_tok = [None, None]
            w_rd = [None, None]
            sig_guard = None
            for hc in range(c.HC):
                wi = hc % 2
                b.wait("sync", w_rd[wi])
                b.dma("sync", w1sl[:, wi, :, :].rearrange("p a bq -> p (a bq)"),
                      w1T_h[hc], ch=f"wab{wi}")
                l2 = b.dma("sync", w2sl[:, wi, :, :].rearrange("p a bq -> p (a bq)"),
                           w2T_h[hc], ch=f"wab{wi}")
                b.wait("tensor", l2)
                b.wait("tensor", ltok)
                b.wait("tensor", px1_tok[wi])
                b.wait("tensor", px2_tok[wi])
                mm_tok = None
                for dc in range(c.DC):
                    for seg in range(segs):
                        b.emit("tensor", lambda e, dc=dc, seg=seg, wi=wi:
                               e.matmul(px1[wi][:, seg * 512:(seg + 1) * 512],
                                        w1sl[:, wi, dc, :],
                                        ln2Th[:, dc, seg * 512:(seg + 1) * 512],
                                        start=(dc == 0), stop=(dc == c.DC - 1)))
                    for seg in range(segs):
                        last = (dc == c.DC - 1 and seg == segs - 1)
                        mm_tok = b.emit(
                            "tensor", lambda e, dc=dc, seg=seg, wi=wi: e.matmul(
                                px2[wi][:, seg * 512:(seg + 1) * 512],
                                w2sl[:, wi, dc, :],
                                ln2Th[:, dc, seg * 512:(seg + 1) * 512],
                                start=(dc == 0), stop=(dc == c.DC - 1)),
                            inc="T" if last else None)
                w_rd[wi] = mm_tok
                b.wait("scalar", mm_tok)
                b.wait("scalar", sig_guard)
                atok = b.emit("scalar", lambda e, wi=wi: e.activation(
                    sig[:, 0:c.SH], px1[wi], AF.Sigmoid, scale=0.5))
                px1_tok[wi] = atok
                b.wait("vector", atok)
                vtok = b.emit("vector", lambda e, hc=hc, wi=wi: e.tensor_mul(
                    hidT[:, hc, :], sig[:, 0:c.SH], px2[wi]))
                px2_tok[wi] = vtok
                sig_guard = vtok
            hid_done = ("V", b.sem["V"][1])
            e_guards = [w_rd[0], w_rd[1], b.dma_total("wab0"),
                        b.dma_total("wab1")]
            wdq = c.D // c.DQ
            pfs = [psf(i, wdq) for i in range(c.HTT)]
            pf_tok = [None] * c.HTT
            w3_rd = [None, None]
            for dq in range(c.DQ):
                for hc in range(c.HC):
                    wi = hc % 2
                    b.wait("sync", w3_rd[wi])
                    l3 = b.dma("sync", w3sl[:, wi, :], w3T_h[dq, hc],
                               ch=f"w3{wi}")
                    b.wait("tensor", l3)
                    b.wait("tensor", hid_done)
                    mm_tok = None
                    for i in range(c.HTT):
                        if hc == 0:
                            b.wait("tensor", pf_tok[i])
                        mm_tok = b.emit(
                            "tensor", lambda e, i=i, hc=hc, wi=wi: e.matmul(
                                pfs[i], hidT[:, hc, i * P:(i + 1) * P],
                                w3sl[:, wi, :], start=(hc == 0),
                                stop=(hc == c.HC - 1)),
                            inc="T" if i == c.HTT - 1 else None)
                    w3_rd[wi] = mm_tok
                    if hc == c.HC - 1:
                        b.wait("scalar", mm_tok)
                        for i in range(c.HTT):
                            pf_tok[i] = b.emit(
                                "scalar", lambda e, i=i, dq=dq: e.activation(
                                    ffnbf[:, i, dq * wdq:(dq + 1) * wdq],
                                    pfs[i], AF.Copy, scale=0.25))
            ffn_done = ("A", b.sem["A"][1])
            y_rd = None
            for i in range(c.HTT):
                tt = t0 + i
                b.wait("gpsimd", y_rd)
                b.wait("gpsimd", b.dma_total("ost0"))
                b.wait("gpsimd", b.dma_total("ost1"))
                ytok = b.dma("gpsimd", y_e[:], y_h[bb, tt * P:(tt + 1) * P, :],
                             ch="yld")
                for e_ in ("vector", "scalar"):
                    b.wait(e_, ffn_done)
                    b.wait(e_, ytok)
                    for t_ in e_guards:
                        b.wait(e_, t_)
                ln_full(ffnbf[:, i, :], tmp_e[:], sq_e, c.D)
                vtok = b.emit("vector", lambda e: e.tensor_add(
                    y_e[:], tmp_e[:], y_e[:]))
                y_rd = vtok
                for t_ in fence([("vector", y_e[0:1, c.D - 16:c.D], 16)]):
                    b.wait("gpsimd", t_)
                b.dma("gpsimd", out[bb, tt * P:(tt + 1) * P, :], y_e[:],
                      ch=f"ost{i % 2}")
            b.barrier()
        b.barrier()

    b.final_dma_waits("gpsimd")
    b.final_dma_waits("sync")

    with nc.Block() as block:
        b.run(block)
    stack.close()
    return nc


def build_nc(cfg):
    nc = bass.Bass("TRN2", target_bir_lowering=False, debug=False)
    c = cfg
    ins = {
        "x": nc.dram_tensor("x", [c.NB, c.S, c.D], F32, kind="ExternalInput").ap(),
        "wq": nc.dram_tensor("wq", [c.D, c.D], F32, kind="ExternalInput").ap(),
        "wk": nc.dram_tensor("wk", [c.KV, c.D], F32, kind="ExternalInput").ap(),
        "wv": nc.dram_tensor("wv", [c.KV, c.D], F32, kind="ExternalInput").ap(),
        "wo": nc.dram_tensor("wo", [c.D, c.D], F32, kind="ExternalInput").ap(),
        "w1": nc.dram_tensor("w1", [c.HF, c.D], F32, kind="ExternalInput").ap(),
        "w2": nc.dram_tensor("w2", [c.HF, c.D], F32, kind="ExternalInput").ap(),
        "w3": nc.dram_tensor("w3", [c.D, c.HF], F32, kind="ExternalInput").ap(),
        "cos_c": nc.dram_tensor("cos_c", [c.HD, c.S], F32, kind="ExternalInput").ap(),
        "ssin_c": nc.dram_tensor("ssin_c", [c.HD, c.S], F32, kind="ExternalInput").ap(),
        "ident_c": nc.dram_tensor("ident_c", [P, P], BF16, kind="ExternalInput").ap(),
        "maskv_c": nc.dram_tensor("maskv_c", [c.QGT, P, c.QGT * P], BF16,
                                  kind="ExternalInput").ap(),
    }
    out = nc.dram_tensor("out", [c.NB, c.S, c.D], F32, kind="ExternalOutput").ap()
    build(nc, ins, out, cfg)
    return nc


def kernel(L0_attn_x, L0_attn_Wq, L0_attn_Wk, L0_attn_Wv, L0_attn_Wo,
           L0_ffn_w1, L0_ffn_w2, L0_ffn_w3):
    cfg = Cfg()
    consts = host_constants(cfg)
    x = np.ascontiguousarray(np.asarray(L0_attn_x, dtype=np.float32))
    shards = np.split(x, N_CORES, axis=0)
    common = {
        "wq": np.ascontiguousarray(np.asarray(L0_attn_Wq, np.float32)),
        "wk": np.ascontiguousarray(np.asarray(L0_attn_Wk, np.float32)),
        "wv": np.ascontiguousarray(np.asarray(L0_attn_Wv, np.float32)),
        "wo": np.ascontiguousarray(np.asarray(L0_attn_Wo, np.float32)),
        "w1": np.ascontiguousarray(np.asarray(L0_ffn_w1, np.float32)),
        "w2": np.ascontiguousarray(np.asarray(L0_ffn_w2, np.float32)),
        "w3": np.ascontiguousarray(np.asarray(L0_ffn_w3, np.float32)),
        **consts,
    }
    nc = build_nc(cfg)
    in_maps = [{"x": s, **common} for s in shards]
    res = run_bass_kernel_spmd(nc, in_maps, core_ids=list(range(N_CORES)))
    return np.concatenate([np.asarray(r["out"], np.float32) for r in res.results],
                          axis=0)


# version 32
# speedup vs baseline: 1.2289x; 1.0475x over previous
import collections
import math
from contextlib import ExitStack

import numpy as np
import ml_dtypes
import concourse.bass as bass
import concourse.mybir as mybir
from concourse.bass_utils import run_bass_kernel_spmd

F32 = mybir.dt.float32
BF16 = mybir.dt.bfloat16
AF = mybir.ActivationFunctionType
ALU = mybir.AluOpType
AX = mybir.AxisListType

N_CORES = 8
P = 128
SBMAP = []


class Cfg:
    def __init__(self, S=2048, D=2048, HD=128, NQ=16, NKV=4, HF=5632, NB=2,
                 MAX_POS=4096, THETA=10000.0):
        self.S, self.D, self.HD, self.NQ, self.NKV, self.HF, self.NB = S, D, HD, NQ, NKV, HF, NB
        self.MAX_POS, self.THETA = MAX_POS, THETA
        self.KV = NKV * HD
        self.GRP = NQ // NKV
        self.TT = S // P
        self.HTT = self.TT // 2
        self.SH = S // 2
        self.DC = D // P
        self.HC = HF // P
        self.QGT = min(4, self.TT)
        self.NG = self.TT // self.QGT
        self.QGW = self.QGT * P
        self.DQ = max(1, D // 512)
        self.CGW = 256
        self.CGN = (D + 2 * self.KV) // self.CGW
        self.QCG = D // self.CGW
        self.KCG = self.KV // self.CGW
        self.NH = self.CGW // HD
        self.SCALE = 1.0 / math.sqrt(HD)
        self.CSHIFT = 30.0
        self.EPS = 1e-5
        self.QEPS = 1e-8
        self.GCH = 512
        assert self.KV % self.CGW == 0 and D % 512 == 0 and self.HTT <= 8
        assert self.SH % 512 == 0 and self.TT % self.QGT == 0
        assert self.D % self.GCH == 0
        assert (self.HF * self.D) % (128 * self.GCH) == 0


def host_constants(cfg):
    HD = cfg.HD
    k = np.arange(0, HD, 2)[:HD // 2].astype(np.float32)
    inv_freq = 1.0 / cfg.THETA ** (k / HD)
    positions = np.arange(cfg.MAX_POS, dtype=np.float32)
    angles = positions[:, None] * inv_freq[None, :]
    angles = np.concatenate([angles, angles], axis=-1)
    cos = np.cos(angles)[:cfg.S].astype(np.float32)
    sin = np.sin(angles)[:cfg.S].astype(np.float32)
    ssin = np.concatenate([-sin[:, :HD // 2], sin[:, HD // 2:]], axis=1)
    cos = np.ascontiguousarray(cos.T)
    ssin = np.ascontiguousarray(ssin.T)
    ident = np.eye(P, dtype=ml_dtypes.bfloat16)
    QGT = cfg.QGT
    maskv = np.zeros((QGT, P, QGT * P), dtype=np.float32)
    for i in range(QGT):
        for j in range(QGT):
            blk = maskv[i, :, j * P:(j + 1) * P]
            if j < i:
                blk[:] = -1e30
            elif j == i:
                pp, cc = np.meshgrid(np.arange(P), np.arange(P), indexing="ij")
                blk[pp > cc] = -1e30
    return {"cos_c": cos, "ssin_c": ssin, "ident_c": ident,
            "maskv_c": maskv.astype(ml_dtypes.bfloat16)}


ENGINES = ("sync", "gpsimd", "vector", "scalar", "tensor")


class B:

    _SELF = {"vector": "V", "scalar": "A", "gpsimd": "G"}

    def __init__(self, nc, stack):
        self.nc = nc
        self.stack = stack
        self.st = {e: [] for e in ENGINES}
        self.sem = {}
        self.hw = {e: {} for e in ENGINES}
        for name in ("T", "V", "A", "G"):
            h = stack.enter_context(nc.semaphore(name))
            self.sem[name] = [h, 0]
        self.ln_guard = [None, None]
        self.ln_i = 0

    def emit(self, eng, fn, inc=None, k=1):
        nm = self._SELF.get(eng)
        if nm is not None:
            s = self.sem[nm]
            self.wait(eng, (nm, s[1]))
            s[1] += 1
            tgt = s[1]
            self.st[eng].append(lambda e, fn=fn, h=s[0]: fn(e).then_inc(h, 1))
            return (nm, tgt)
        if inc is not None:
            s = self.sem[inc]
            s[1] += k
            tgt = s[1]
            self.st[eng].append(lambda e, fn=fn, h=s[0], k=k: fn(e).then_inc(h, k))
            return (inc, tgt)
        self.st[eng].append(fn)
        return None

    def dma(self, eng, out_ap, in_ap, ch="d0", **kw):
        nm = "D" + ch
        if nm not in self.sem:
            h = self.stack.enter_context(self.nc.semaphore(nm))
            self.sem[nm] = [h, 0]
        s = self.sem[nm]
        self.wait(eng, (nm, s[1]))
        s[1] += 16
        tgt = s[1]
        self.st[eng].append(
            lambda e, o=out_ap, i=in_ap, h=s[0], kw=kw: e.dma_start(o, i, **kw).then_inc(h, 16))
        return (nm, tgt)

    def dma_total(self, ch):
        nm = "D" + ch
        return (nm, self.sem[nm][1]) if nm in self.sem else None

    def final_dma_waits(self, eng):
        for nm, s in self.sem.items():
            if nm.startswith("D"):
                self.wait(eng, (nm, s[1]))

    def wait(self, eng, tok):
        if tok is None:
            return
        name, v = tok
        if self.hw[eng].get(name, 0) >= v:
            return
        self.hw[eng][name] = v
        h = self.sem[name][0]
        self.st[eng].append(lambda e, h=h, v=v: e.wait_ge(h, v))

    def barrier(self):
        toks = [(n, s[1]) for n, s in self.sem.items() if s[1] > 0]
        for e in ENGINES:
            for t in toks:
                self.wait(e, t)

    def run(self, block):
        @block.sync
        def _(e):
            for fn in self.st["sync"]:
                fn(e)

        @block.gpsimd
        def _(e):
            for fn in self.st["gpsimd"]:
                fn(e)

        @block.vector
        def _(e):
            for fn in self.st["vector"]:
                fn(e)

        @block.scalar
        def _(e):
            for fn in self.st["scalar"]:
                fn(e)

        @block.tensor
        def _(e):
            for fn in self.st["tensor"]:
                fn(e)


def build(nc, ins, out, cfg):
    c = cfg
    stack = ExitStack()
    b = B(nc, stack)

    PS = stack.enter_context(nc.psum_tensor("PS", [P, 4096], F32))

    def psf(bank, width):
        return PS[:, bank * 512:bank * 512 + width]

    def psbf(bank, col0, width=P):
        return PS[:].bitcast(BF16)[:, bank * 1024 + col0:bank * 1024 + col0 + width]

    dt = nc.dram_tensor
    wq_bf = dt("wq_bf", [c.D, c.D], BF16, kind="Internal").ap()
    wk_bf = dt("wk_bf", [c.KV, c.D], BF16, kind="Internal").ap()
    wv_bf = dt("wv_bf", [c.KV, c.D], BF16, kind="Internal").ap()
    wo_bf = dt("wo_bf", [c.D, c.D], BF16, kind="Internal").ap()
    w1T_h = dt("w1T_h", [c.HC, P, c.DC * P], BF16, kind="Internal").ap()
    w2T_h = dt("w2T_h", [c.HC, P, c.DC * P], BF16, kind="Internal").ap()
    w3T_h = dt("w3T_h", [c.DQ, c.HC, P, c.D // c.DQ], BF16, kind="Internal").ap()
    y_h = dt("y_h", [c.NB, c.S, c.D], F32, kind="Internal").ap()
    thr_h = dt("thr_h", [4, 1], F32, kind="Internal").ap()
    ln2T_h = dt("ln2T_h", [c.DC, P, c.S], BF16, kind="Internal").ap()
    ln2T_hp = ln2T_h.rearrange("dc p s -> p dc s")

    ARENA0 = 20480
    SB_LIMIT = nc.SBUF_PARTITION_SIZE_BYTES - 32
    ARENA = nc.alloc_sbuf_tensor_at("ARENA", [P, SB_LIMIT - ARENA0],
                                    mybir.dt.uint8, offset=ARENA0)
    off = [ARENA0]

    def sb(name, shape, dtype, at=None):
        o = off[0] if at is None else at
        esz = mybir.dt.size(dtype)
        n = int(np.prod(shape[1:]))
        assert (o - ARENA0) % esz == 0, (name, o)
        base = ARENA[:].bitcast(dtype)
        v = base[0:shape[0], (o - ARENA0) // esz:(o - ARENA0) // esz + n]
        if len(shape) == 3:
            v = v.rearrange("p (a bq) -> p a bq", a=shape[1])
        elif len(shape) == 4:
            v = v.rearrange("p (a bq cq) -> p a bq cq", a=shape[1], bq=shape[2])
        SBMAP.append((name, o, o + n * esz, shape, str(dtype)))
        off[0] = o + ((n * esz + 31) // 32) * 32
        return v

    ident = sb("ident", [P, P], BF16)
    cos_sb = sb("cos_sb", [P, c.S], BF16)
    ssin_sb = sb("ssin_sb", [P, c.S], BF16)
    maskv_sb = sb("maskv_sb", [P, c.QGT, c.QGW], BF16)
    ones_f = sb("ones_f", [1, P], F32)
    stA = sb("stA", [P, 16], F32)
    thr128 = sb("thr128", [P, 4], F32)
    nthr128 = sb("nthr128", [P, 4], F32)
    ngch = (2 * c.HF * c.D + c.D * c.HF) // (P * c.GCH)
    gpart = sb("gpart", [P, ngch + 8], F32)
    gsum = sb("gsum", [1, 8], F32)
    c_eps = sb("c_eps", [P, 1], F32)
    c_ncs = sb("c_ncs", [P, 1], F32)
    c_zero = sb("c_zero", [P, 1], F32)
    fdump = sb("fdump", [1, 64], F32)
    nc.const_aps.register(F32, c.EPS, c_eps)
    nc.const_aps.register(F32, -c.CSHIFT, c_ncs)
    nc.const_aps.register(F32, 0.0, c_zero)
    TA = off[0]

    wslA = sb("wslA", [P, c.DC, c.CGW], BF16, at=TA)
    wslB = sb("wslB", [P, c.DC, c.CGW], BF16)
    x_t0 = sb("x_t0", [P, c.D], F32)
    ln1bf = sb("ln1bf", [P, c.D], BF16)
    LN1TH_AT = off[0]
    ln1Th = sb("ln1Th", [P, c.DC, c.SH], BF16)
    RTMP_AT = off[0]
    rtmp = sb("rtmp", [P, c.SH], BF16)
    g3 = [sb(f"g3_{i}", [P, c.GCH], F32) for i in range(5)]
    qa_s = sb("qa_s", [P, c.GCH], BF16)
    qb_s = sb("qb_s", [P, c.GCH], BF16)
    wqh = sb("wqh", [P, 3, c.DC, P], BF16)
    slab = sb("slab", [P, c.DC * P], BF16)
    A_END = off[0]
    sq_scr = sb("sq_scr", [P, c.D], F32, at=TA)
    assert c.D * 4 <= 2 * c.DC * c.CGW * 2

    off[0] = TA
    attnT = sb("attnT", [P, 2, c.TT, c.QGW], BF16, at=TA)
    o_nrm = sb("o_nrm", [P, 4, P], BF16)
    recip = sb("recip", [P, 2], F32)
    C_END = off[0]

    off[0] = TA
    woT = sb("woT", [P, c.DC, c.D], BF16, at=TA)
    x2_t = sb("x2_t", [P, c.D], F32)
    sq_d = sb("sq_d", [P, c.D], F32)
    ln2T_t = sb("ln2T_t", [P, c.DC, P], BF16)
    D_TA_END = off[0]

    TA_SZ = max(A_END, C_END, D_TA_END) - TA
    LB = TA + TA_SZ

    qT = sb("qT", [P, c.NQ, c.S], BF16, at=LB)
    oT = sb("oT", [P, c.NQ, c.S], BF16, at=LB)
    kT = sb("kT", [P, c.NKV, c.S], BF16)
    vaug = sb("vaug", [P, c.NKV, c.TT, c.HD + 1], BF16)
    LB_END = off[0]
    KTOFF = LB + c.NQ * c.S * 2
    off[0] = KTOFF
    atn_o = sb("atn_o", [P, c.D], F32, at=KTOFF)
    tmp_d = sb("tmp_d", [P, c.D], F32)
    y_t = sb("y_t", [P, c.D], F32)
    ln2bf = sb("ln2bf", [P, c.D], BF16)
    D_END = off[0]

    off[0] = TA
    ln2Th = sb("ln2Th", [P, c.DC, c.SH], BF16, at=TA)
    hidT = sb("hidT", [P, c.HC, c.SH], BF16)
    W1SL_AT = off[0]
    w1sl = sb("w1sl", [P, 2, c.DC, P], BF16)
    W2SL_AT = off[0]
    w2sl = sb("w2sl", [P, 2, c.DC, P], BF16)
    w3sl = sb("w3sl", [P, 4, c.D // c.DQ], BF16)
    sig = sb("sig", [P, min(c.SH, 1024)], F32)
    ffnbf = sb("ffnbf", [P, c.HTT, c.D], BF16)
    y_e = sb("y_e", [P, c.D], F32)
    E_END0 = off[0]
    sq_e = sb("sq_e", [P, c.D], F32, at=W1SL_AT)
    tmp_e = sb("tmp_e", [P, c.D], F32, at=W2SL_AT)
    off[0] = E_END0
    E_END = off[0]
    assert c.D * 4 <= 2 * c.DC * P * 2

    base_x = (max(C_END, LN1TH_AT) + 31) & ~31
    NEXTRA = max(0, min(12, (RTMP_AT - base_x) // (c.GCH * 4)))
    g3x = [sb(f"g3x{j}", [P, c.GCH], F32, at=base_x + j * c.GCH * 4)
           for j in range(NEXTRA)]
    off[0] = E_END

    assert max(E_END, D_END, LB_END) <= SB_LIMIT, (
        "sbuf overflow", A_END, E_END, D_END, LB_END, SB_LIMIT)

    def fence(reads):
        toks = []
        for eng, ap, n in reads:
            base = 0 if eng == "scalar" else 32
            if eng == "scalar":
                toks.append(b.emit("scalar", lambda e, ap=ap, n=n, bo=base: e.copy(
                    fdump[0:1, bo:bo + n], ap)))
            else:
                toks.append(b.emit("vector",
                                   lambda e, ap=ap, n=n, bo=base: e.tensor_copy(
                                       fdump[0:1, bo:bo + n], ap)))
        return toks

    ST_SUM, ST_SQ, ST_MEAN, ST_M2, ST_VAR, ST_LNV, ST_RSTD = 0, 2, 4, 6, 8, 10, 12

    def ln_full(src_ap, dst_ap, sq_buf, width, extra_wait_v=None):
        s = b.ln_i % 2
        b.ln_i += 1
        g = b.ln_guard[s]
        b.wait("vector", g)
        b.wait("scalar", g)
        b.emit("vector", lambda e: e.reduce_sum(
            stA[:, ST_SUM + s:ST_SUM + s + 1], src_ap, axis=AX.X))
        atok = b.emit("scalar", lambda e: e.activation(
            sq_buf[:, 0:width], src_ap, AF.Square,
            accum_out=stA[:, ST_SQ + s:ST_SQ + s + 1]))
        b.wait("vector", atok)
        b.emit("vector", lambda e: e.tensor_scalar_mul(
            stA[:, ST_MEAN + s:ST_MEAN + s + 1],
            stA[:, ST_SUM + s:ST_SUM + s + 1], 1.0 / width))
        b.emit("vector", lambda e: e.tensor_mul(
            stA[:, ST_M2 + s:ST_M2 + s + 1],
            stA[:, ST_MEAN + s:ST_MEAN + s + 1],
            stA[:, ST_MEAN + s:ST_MEAN + s + 1]))
        vtok = b.emit("vector", lambda e: e.scalar_tensor_tensor(
            stA[:, ST_VAR + s:ST_VAR + s + 1], stA[:, ST_SQ + s:ST_SQ + s + 1],
            1.0 / width, stA[:, ST_M2 + s:ST_M2 + s + 1],
            ALU.mult, ALU.subtract))
        b.wait("scalar", vtok)
        b.emit("scalar", lambda e: e.activation(
            stA[:, ST_LNV + s:ST_LNV + s + 1], stA[:, ST_VAR + s:ST_VAR + s + 1],
            AF.Ln, bias=c.EPS))
        atok2 = b.emit("scalar", lambda e: e.activation(
            stA[:, ST_RSTD + s:ST_RSTD + s + 1], stA[:, ST_LNV + s:ST_LNV + s + 1],
            AF.Exp, scale=-0.5))
        b.wait("vector", atok2)
        b.wait("vector", extra_wait_v)
        tok = b.emit("vector", lambda e: e.tensor_scalar(
            dst_ap, src_ap, stA[:, ST_MEAN + s:ST_MEAN + s + 1],
            stA[:, ST_RSTD + s:ST_RSTD + s + 1], ALU.subtract, ALU.mult))
        b.ln_guard[s] = tok
        return tok

    b.dma("sync", ident[:], ins["ident_c"], ch="cst")
    b.dma("sync", maskv_sb[:], ins["maskv_c"].rearrange("a p bq -> p a bq"),
          ch="cst")
    b.dma("gpsimd", cos_sb[:], ins["cos_c"], ch="cos")
    b.dma("gpsimd", ssin_sb[:], ins["ssin_c"], ch="cos")
    b.emit("vector", lambda e: e.memset(ones_f[:], 1.0))
    b.emit("vector", lambda e: e.memset(c_eps[:], c.EPS))
    b.emit("vector", lambda e: e.memset(c_ncs[:], -c.CSHIFT))
    b.emit("vector", lambda e: e.memset(c_zero[:], 0.0))
    b.barrier()

    for ci, (w_in, w_sc, rows) in enumerate((
            (ins["wq"], wq_bf, c.D), (ins["wk"], wk_bf, c.KV),
            (ins["wv"], wv_bf, c.KV), (ins["wo"], wo_bf, c.D))):
        step = 256
        for r0 in range(0, rows, step):
            r1 = min(rows, r0 + step)
            b.dma("gpsimd", w_sc[r0:r1, :], w_in[r0:r1, :], ch=f"cast{ci}")
    qkv_cast = [b.dma_total(f"cast{i}") for i in range(4)]
    wo_cast = b.dma_total("cast3")

    QV = collections.deque()
    QVL = collections.deque()
    QT = collections.deque()
    LEAD = 4
    G3 = [(g3[i], f"g{i}") for i in range(5)]
    pst = {"rr": 0, "pend": collections.deque(), "gt": {}, "qc": 0,
           "sink_tok": [None, None], "gamma_last": None,
           "wq_rd": [None, None, None], "slab_st": None}
    pts_g = [psbf(i, 0) for i in range(2)]
    ptq_g = [psbf(2 + i, 0) for i in range(2)]
    ptq6 = [psbf(2 + i, 0) for i in range(6)]
    pt_tok_g = [None, None]
    ptq_tok = [None, None, None, None, None, None]

    mats = [(ins["w1"], 0, w1T_h, False), (ins["w2"], 1, w2T_h, False),
            (ins["w3"], 2, w3T_h, True)]

    def gamma_load(w_in, r0):
        def fn():
            bid = pst["rr"] % len(G3)
            pst["rr"] += 1
            gb, ch = G3[bid]
            b.wait("gpsimd", pst["gt"].get(bid))
            b.dma("gpsimd", gb[:],
                  w_in.rearrange("a bq -> (a bq)").rearrange(
                      "(n p x) -> n p x", p=P, x=c.GCH)[r0], ch=ch)
            pst["pend"].append(bid)
        return fn

    def use_act():
        pref = pst.get("eng_pref", "A")
        if pref == "ALT":
            pst["alt"] = pst.get("alt", 0) + 1
            return pst["alt"] % 2 == 0
        return pref == "A"

    def gamma_comp(col):
        def fn():
            bid = pst["pend"].popleft()
            gb, ch = G3[bid]
            if use_act():
                b.wait("scalar", b.dma_total(ch))
                b.wait("scalar", pst["sink_tok"][0])
                tok = b.emit("scalar", lambda e, gb=gb, col=col: e.activation(
                    qa_s[:], gb[:], AF.Abs, accum_out=gpart[:, col:col + 1]))
                pst["sink_tok"][0] = tok
                pst["gamma_A"] = tok
            else:
                b.wait("vector", b.dma_total(ch))
                tok = b.emit("vector", lambda e, gb=gb, col=col: e.reduce_sum(
                    gpart[:, col:col + 1], gb[:], axis=AX.X,
                    apply_absolute_value=True))
                pst["gamma_V"] = tok
            pst["gt"][bid] = tok
        return fn

    mat_cols = {}
    gcol = 0
    nchunk = (c.HF * c.D) // (P * c.GCH)
    for (w_in, m, _, is_w3) in mats:
        mat_cols[m] = (gcol, nchunk)
        for t in range(nchunk):
            QV.append((gamma_load(w_in, t), gamma_comp(gcol)))
            gcol += 1
    assert gcol <= ngch

    def thr_item():
        b.wait("vector", pst.get("gamma_A"))
        b.wait("vector", pst.get("gamma_V"))
        for (w_in, m, _, is_w3) in mats:
            c0, nt = mat_cols[m]
            nelem = float(nt * P * c.GCH)
            tok = b.emit("vector", lambda e, c0=c0, nt=nt, m=m: e.reduce_sum(
                gpart[:, ngch + m:ngch + m + 1], gpart[:, c0:c0 + nt], axis=AX.X))
            b.wait("gpsimd", tok)
            tok = b.emit("gpsimd", lambda e, m=m: e.tensor_reduce(
                gsum[:, m:m + 1], gpart[:, ngch + m:ngch + m + 1], axis=AX.C,
                op=ALU.add))
            b.wait("gpsimd", tok)
            b.emit("gpsimd", lambda e, m=m, ne=nelem: e.tensor_scalar(
                gsum[:, 4 + m:5 + m], gsum[:, m:m + 1], 0.5 / ne, 0.5 * c.QEPS,
                ALU.mult, ALU.add))
            b.emit("gpsimd", lambda e, m=m: e.tensor_copy(
                fdump[0:1, 48:49], gsum[0:1, 4 + m:5 + m]))
            b.dma("gpsimd", thr_h[m], gsum[0:1, 4 + m:5 + m], ch="thr")
            ltok = b.dma("gpsimd", thr128[:, m:m + 1],
                         thr_h[m].broadcast_to((P, 1)), ch="thr")
            b.wait("gpsimd", ltok)
            pst["thr_done"] = b.emit("gpsimd", lambda e, m=m: e.tensor_scalar_mul(
                nthr128[:, m:m + 1], thr128[:, m:m + 1], -1.0))
    QV.append((None, thr_item))

    def transp_item(m, hc, wT_out, is_w3, vtok, tpar):
        def fn():
            nrot = 6 if pst.get("drain") else 2
            rot = ptq6 if nrot == 6 else ptq_g
            b.wait("tensor", vtok)
            last_t = None
            for dc in range(c.DC):
                ps = dc % nrot
                b.wait("tensor", ptq_tok[ps])
                src = wqh[:, tpar, dc, :]
                ttok = b.emit("tensor", lambda e, src=src, ps=rot[ps]:
                              e.transpose(ps, src, ident[:]), inc="T")
                last_t = ttok
                eng = "scalar" if dc % 2 == 0 else "vector"
                b.wait(eng, ttok)
                b.wait(eng, pst["slab_st"])
                if eng == "scalar":
                    fn2 = (lambda e, dc=dc, ps=rot[ps]:
                           e.copy(slab[:, dc * P:(dc + 1) * P], ps))
                else:
                    fn2 = (lambda e, dc=dc, ps=rot[ps]:
                           e.tensor_copy(slab[:, dc * P:(dc + 1) * P], ps))
                ptq_tok[ps] = b.emit(eng, fn2)
            pst["wq_rd"][tpar] = last_t
            sl_f = slab[:].rearrange("p (a j) -> p a j", j=P)
            for t_ in fence([("scalar", sl_f[0:1, 0::2, P - 1], c.DC // 2),
                             ("vector", sl_f[0:1, 1::2, P - 1], c.DC // 2)]):
                b.wait("gpsimd", t_)
            if is_w3:
                wdq = c.D // c.DQ
                for dq in range(c.DQ):
                    pst["slab_st"] = b.dma(
                        "gpsimd", wT_out[dq, hc], slab[:, dq * wdq:(dq + 1) * wdq],
                        ch="sl0")
            else:
                pst["slab_st"] = b.dma("gpsimd", wT_out[hc], slab[:], ch="sl0")
        return fn

    def quant_load(w_in, hc, half, is_w3):
        def fn():
            bid = pst["rr"] % len(G3)
            pst["rr"] += 1
            gb, ch = G3[bid]
            b.wait("gpsimd", pst["gt"].get(bid))
            nb = c.GCH // P
            if is_w3:
                w3v = w_in.rearrange("(dc p) (hc j) -> hc p dc j", p=P, j=P)
                srcv = w3v[hc][:, half * nb:(half + 1) * nb, :]
                b.dma("gpsimd", gb[:].rearrange("p (a j) -> p a j", j=P),
                      srcv, ch=ch)
            else:
                b.dma("gpsimd", gb[:],
                      w_in[hc * P:(hc + 1) * P,
                           half * c.GCH:(half + 1) * c.GCH],
                      ch=ch)
            pst["pend"].append(bid)
        return fn

    def quant_comp(m, hc, half, is_w3, wT_out, last_half, tpar):
        def fn():
            bid = pst["pend"].popleft()
            gb, ch = G3[bid]
            nb = c.GCH // P
            dst = wqh[:, tpar, half * nb:(half + 1) * nb, :].rearrange(
                "p a j -> p (a j)")
            if use_act():
                par = pst["qc"] % 2
                pst["qc"] += 1
                sink = qa_s if par == 0 else qb_s
                b.wait("scalar", b.dma_total(ch))
                b.wait("scalar", pst["wq_rd"][tpar])
                b.wait("scalar", pst.get("thr_done"))
                b.wait("scalar", pst["sink_tok"][par])
                b.emit("scalar", lambda e, dst=dst, gb=gb, m=m: e.activation(
                    dst, gb[:], AF.Sign, bias=nthr128[:, m:m + 1]))
                stok = b.emit("scalar", lambda e, sink=sink, gb=gb, m=m:
                              e.activation(sink[:], gb[:], AF.Sign,
                                           bias=thr128[:, m:m + 1]))
                b.wait("vector", stok)
                vtok = b.emit("vector", lambda e, dst=dst, sink=sink:
                              e.tensor_add(dst, dst, sink[:]))
                pst["sink_tok"][par] = vtok
            else:
                b.wait("vector", b.dma_total(ch))
                b.wait("vector", pst["wq_rd"][tpar])
                b.wait("vector", pst.get("thr_done"))
                b.wait("vector", pst["sink_tok"][0])
                b.wait("vector", pst["sink_tok"][1])
                b.emit("vector", lambda e, gb=gb, m=m: e.tensor_scalar(
                    qa_s[:], gb[:], thr128[:, m:m + 1], 2.0,
                    ALU.is_ge, ALU.mult))
                b.emit("vector", lambda e, gb=gb, m=m: e.tensor_scalar(
                    qb_s[:], gb[:], nthr128[:, m:m + 1], 2.0,
                    ALU.is_le, ALU.mult))
                vtok = b.emit("vector", lambda e, dst=dst:
                              e.tensor_sub(dst, qa_s[:], qb_s[:]))
                pst["sink_tok"][0] = vtok
                pst["sink_tok"][1] = vtok
            pst["gt"][bid] = vtok
            if last_half:
                QT.append(transp_item(m, hc, wT_out, is_w3, vtok, tpar))
        return fn

    ti = 0
    for (w_in, m, wT_out, is_w3) in mats:
        width = c.D if not is_w3 else c.DC * P
        nhalf = width // c.GCH
        for hc in range(c.HC):
            for half in range(nhalf):
                QV.append((quant_load(w_in, hc, half, is_w3),
                           quant_comp(m, hc, half, is_w3, wT_out,
                                      half == nhalf - 1, ti % 3)))
            ti += 1

    for it in list(QV):
        QVL.append(it[0])
    for _ in range(LEAD):
        if QVL:
            fn0 = QVL.popleft()
            if fn0 is not None:
                fn0()

    def drip(q, n, qt_ok=True):
        for _ in range(n):
            if q is QV:
                if len(QT) > 2:
                    if not qt_ok:
                        return
                    while len(QT) > 2:
                        QT.popleft()()
                if q:
                    if QVL:
                        fn0 = QVL.popleft()
                        if fn0 is not None:
                            fn0()
                    _, comp = q.popleft()
                    comp()
            elif q:
                q.popleft()()

    x_in = ins["x"]
    NH = c.NH
    for bb in range(c.NB):
        b.emit("vector", lambda e: e.memset(vaug[:, :, :, c.HD:c.HD + 1], 1.0))

        for hf in range(2):
            t0 = hf * c.HTT
            pq = [psf(4 + i, c.CGW) for i in range(3)]
            pq_tok = [None, None, None]
            x_rd = [None]
            lnbf_guard = None
            for i in range(c.HTT):
                tt = t0 + i
                xb = x_t0
                b.wait("gpsimd", x_rd[0])
                xtok = b.dma("gpsimd", xb[:], x_in[bb, tt * P:(tt + 1) * P, :],
                             ch="x0")
                b.wait("vector", xtok)
                b.wait("scalar", xtok)
                vtok = ln_full(xb[:], ln1bf[:], sq_scr, c.D,
                               extra_wait_v=lnbf_guard)
                x_rd[0] = vtok
                b.wait("tensor", vtok)
                last_t = None
                for dc in range(c.DC):
                    ps = dc % 2
                    b.wait("tensor", pt_tok_g[ps])
                    ttok = b.emit("tensor", lambda e, dc=dc, ps=pts_g[ps]:
                                  e.transpose(ps, ln1bf[:, dc * P:(dc + 1) * P],
                                              ident[:]), inc="T")
                    last_t = ttok
                    eng = "scalar" if dc % 2 == 0 else "vector"
                    b.wait(eng, ttok)
                    if eng == "scalar":
                        fn = (lambda e, dc=dc, i=i, ps=pts_g[ps]: e.copy(
                            ln1Th[:, dc, i * P:(i + 1) * P], ps))
                    else:
                        fn = (lambda e, dc=dc, i=i, ps=pts_g[ps]: e.tensor_copy(
                            ln1Th[:, dc, i * P:(i + 1) * P], ps))
                    pt_tok_g[ps] = b.emit(eng, fn)
                lnbf_guard = last_t
                drip(QV, 1)
                drip(QT, 1)
            ln1_done = [("A", b.sem["A"][1]), ("V", b.sem["V"][1])]

            w_bufs = [wslA, wslB]
            w_rd = [None, None]
            for t_ in qkv_cast:
                b.wait("sync", t_)
            pqf = [psf(4, 1024), psf(6, 1024)]
            pqf_tok = [None] * 2
            pgi = 0
            for cg in range(c.CGN):
                wb = w_bufs[cg % 2]
                for t_ in ln1_done:
                    b.wait("sync", t_)
                b.wait("sync", w_rd[cg % 2])
                for dc in range(c.DC):
                    if cg < c.QCG:
                        srcw = wq_bf[cg * c.CGW:(cg + 1) * c.CGW,
                                     dc * P:(dc + 1) * P]
                    elif cg < c.QCG + c.KCG:
                        kk = cg - c.QCG
                        srcw = wk_bf[kk * c.CGW:(kk + 1) * c.CGW,
                                     dc * P:(dc + 1) * P]
                    else:
                        vv = cg - c.QCG - c.KCG
                        srcw = wv_bf[vv * c.CGW:(vv + 1) * c.CGW,
                                     dc * P:(dc + 1) * P]
                    b.dma("sync", wb[:, dc, :], srcw, transpose=True,
                          ch=f"ws{cg % 2}_{dc % 4}")
                ws_toks = [b.dma_total(f"ws{cg % 2}_{j}") for j in range(4)]
                if cg < c.QCG + c.KCG:
                    for blk in range(c.NH):
                        pi = pgi % 2
                        pgi += 1
                        b.wait("tensor", pqf_tok[pi])
                        for t_ in ws_toks:
                            b.wait("tensor", t_)
                        for t_ in ln1_done:
                            b.wait("tensor", t_)
                        mm_tok = None
                        for dc in range(c.DC):
                            for seg in range(c.SH // 512):
                                mm_tok = b.emit(
                                    "tensor",
                                    lambda e, dc=dc, blk=blk, seg=seg, wb=wb, pi=pi:
                                    e.matmul(
                                        pqf[pi][:, seg * 512:(seg + 1) * 512],
                                        wb[:, dc, blk * c.HD:(blk + 1) * c.HD],
                                        ln1Th[:, dc, seg * 512:(seg + 1) * 512],
                                        start=(dc == 0), stop=(dc == c.DC - 1)),
                                    inc="T" if (dc == c.DC - 1 and
                                                seg == c.SH // 512 - 1) else None)
                        w_rd[cg % 2] = mm_tok
                        if cg < c.QCG:
                            dst = qT[:, cg * c.NH + blk,
                                     t0 * P:t0 * P + c.SH]
                        else:
                            kk = cg - c.QCG
                            dst = kT[:, kk * c.NH + blk,
                                     t0 * P:t0 * P + c.SH]
                        cols = slice(t0 * P, t0 * P + c.SH)
                        h2 = c.HD // 2
                        b.wait("vector", mm_tok)
                        b.emit("vector", lambda e, pi=pi, dst=dst, cols=cols:
                               e.tensor_mul(dst, pqf[pi][:, 0:c.SH],
                                            cos_sb[:, cols]))
                        b.emit("vector", lambda e, pi=pi, cols=cols, h2=h2:
                               e.tensor_mul(rtmp[0:h2, 0:c.SH],
                                            pqf[pi][h2:c.HD, 0:c.SH],
                                            ssin_sb[0:h2, cols]))
                        b.emit("vector", lambda e, pi=pi, cols=cols, h2=h2:
                               e.tensor_mul(rtmp[h2:c.HD, 0:c.SH],
                                            pqf[pi][0:h2, 0:c.SH],
                                            ssin_sb[h2:c.HD, cols]))
                        vtok = b.emit("vector", lambda e, dst=dst: e.tensor_add(
                            dst, dst, rtmp[:, 0:c.SH]))
                        pqf_tok[pi] = vtok
                        drip(QV, 3)
                        drip(QT, 1)
                else:
                    vv = cg - c.QCG - c.KCG
                    for i in range(c.HTT):
                        tt = t0 + i
                        pi = pgi % 2
                        pgi += 1
                        b.wait("tensor", pqf_tok[pi])
                        for t_ in ws_toks:
                            b.wait("tensor", t_)
                        for t_ in ln1_done:
                            b.wait("tensor", t_)
                        mm_tok = None
                        for dc in range(c.DC):
                            mm_tok = b.emit(
                                "tensor",
                                lambda e, dc=dc, i=i, wb=wb, pi=pi: e.matmul(
                                    pqf[pi][:, 0:c.CGW],
                                    ln1Th[:, dc, i * P:(i + 1) * P],
                                    wb[:, dc, :], start=(dc == 0),
                                    stop=(dc == c.DC - 1)),
                                inc="T" if dc == c.DC - 1 else None)
                        w_rd[cg % 2] = mm_tok
                        b.wait("scalar", mm_tok)
                        pv = pqf[pi][:, 0:c.CGW].rearrange("p (g j) -> p g j",
                                                           j=c.HD)
                        atok = b.emit("scalar", lambda e, pv=pv, tt=tt, vv=vv:
                                      e.copy(vaug[:, vv * c.NH:(vv + 1) * c.NH, tt,
                                             0:c.HD], pv))
                        pqf_tok[pi] = atok
                        drip(QV, 3)
                        drip(QT, 1)
            b.barrier()

        pst["eng_pref"] = "V"
        if bb == 0 and not pst.get("grown") and NEXTRA > 0:
            pst["grown"] = True
            b.wait("gpsimd", ("T", b.sem["T"][1]))
            for j in range(NEXTRA):
                G3.append((g3x[j], f"gx{j}"))
            for _ in range(NEXTRA):
                if QVL:
                    fn0 = QVL.popleft()
                    if fn0 is not None:
                        fn0()

        pbs = [psf(0, c.QGW), psf(1, c.QGW), psf(7, c.QGW)]
        pos = [psf(3 + i, c.HD + 1) for i in range(2)]
        ptos = [psbf(5 + i, 0) for i in range(2)]
        pb_tok = [None] * 3
        po_tok = [None] * 2
        pto_tok = [None] * 2
        av_done = [None, None]
        groups = [(h, qg) for h in range(c.NQ) for qg in range(c.NG)]
        gstate = {}
        c_pend = collections.deque()

        def c_scores(gi):
            h, qg = groups[gi]
            g = h // c.GRP
            ab = gi % 2
            kts = c.QGT * qg + c.QGT
            exp_tok = None
            for kt in range(kts):
                pbi = (gstate.get("pbc", 0)) % 3
                gstate["pbc"] = gstate.get("pbc", 0) + 1
                b.wait("tensor", pb_tok[pbi])
                mtok = b.emit(
                    "tensor",
                    lambda e, g=g, kt=kt, h=h, qg=qg, pbi=pbi: e.matmul(
                        pbs[pbi], kT[:, g, kt * P:(kt + 1) * P],
                        qT[:, h, qg * c.QGW:(qg + 1) * c.QGW],
                        start=True, stop=True), inc="T")
                dep = mtok
                if kt >= c.QGT * qg:
                    i = kt - c.QGT * qg
                    b.wait("vector", mtok)
                    dep = b.emit("vector", lambda e, pbi=pbi, i=i:
                                 e.tensor_add(pbs[pbi], pbs[pbi],
                                              maskv_sb[:, i, :]))
                b.wait("scalar", dep)
                b.wait("scalar", av_done[ab])
                exp_tok = b.emit("scalar", lambda e, kt=kt, pbi=pbi, ab=ab:
                                 e.activation(attnT[:, ab, kt, :], pbs[pbi],
                                              AF.Exp, scale=c.SCALE,
                                              bias=-c.CSHIFT))
                pb_tok[pbi] = exp_tok
            gstate[gi] = exp_tok

        def c_av(gi):
            h, qg = groups[gi]
            g = h // c.GRP
            ab = gi % 2
            exp_tok = gstate.pop(gi)
            last_av = None
            for qt in range(c.QGT):
                qabs = qg * c.QGT + qt
                poi = (gstate.get("poc", 0)) % 2
                gstate["poc"] = gstate.get("poc", 0) + 1
                b.wait("tensor", po_tok[poi])
                b.wait("tensor", exp_tok)
                nkt = c.QGT * qg + qt + 1
                for kt in range(nkt):
                    last_av = b.emit(
                        "tensor",
                        lambda e, kt=kt, qt=qt, g=g, poi=poi, nkt=nkt, ab=ab:
                        e.matmul(pos[poi],
                                 attnT[:, ab, kt, qt * P:(qt + 1) * P],
                                 vaug[:, g, kt, :], start=(kt == 0),
                                 stop=(kt == nkt - 1)),
                        inc="T" if kt == nkt - 1 else None)
                b.wait("vector", last_av)
                oni = (gstate.get("onc", 0)) % 4
                gstate["onc"] = gstate.get("onc", 0) + 1
                b.wait("vector", gstate.get(("on", oni)))
                b.emit("vector", lambda e, poi=poi: e.reciprocal(
                    recip[:, poi:poi + 1], pos[poi][:, c.HD:c.HD + 1]))
                vtok = b.emit("vector", lambda e, poi=poi, oni=oni:
                              e.tensor_scalar_mul(
                                  o_nrm[:, oni, :], pos[poi][:, 0:c.HD],
                                  recip[:, poi:poi + 1]))
                po_tok[poi] = vtok

                def _otr(h=h, qabs=qabs, oni=oni, vtok=vtok):
                    pti = (gstate.get("ptc", 0)) % 2
                    gstate["ptc"] = gstate.get("ptc", 0) + 1
                    b.wait("tensor", vtok)
                    b.wait("tensor", pto_tok[pti])
                    ttok = b.emit("tensor", lambda e, oni=oni, pti=pti:
                                  e.transpose(ptos[pti], o_nrm[:, oni, :],
                                              ident[:]), inc="T")
                    gstate[("on", oni)] = ttok
                    b.wait("scalar", ttok)
                    pto_tok[pti] = b.emit(
                        "scalar", lambda e, h=h, qabs=qabs, pti=pti: e.copy(
                            oT[:, h, qabs * P:(qabs + 1) * P], ptos[pti]))
                c_pend.append(_otr)
                if len(c_pend) > 3:
                    c_pend.popleft()()
            av_done[ab] = last_av
            drip(QV, 3, qt_ok=False)

        c_scores(0)
        for gi in range(len(groups)):
            if gi + 1 < len(groups):
                c_scores(gi + 1)
            c_av(gi)
        while c_pend:
            c_pend.popleft()()
        b.barrier()
        pst["eng_pref"] = "ALT"
        pst["drain"] = True
        while QV:
            drip(QV, 1)
        while QT:
            QT.popleft()()
        pst["drain"] = False
        b.barrier()

        b.wait("sync", wo_cast)
        for dc in range(c.DC):
            b.dma("sync", woT[:, dc, :], wo_bf[0:c.D, dc * P:(dc + 1) * P],
                  transpose=True, ch=f"wo{dc % 4}")
        wo_loaded = [b.dma_total(f"wo{j}") for j in range(4)]
        pwo = psf(0, c.D)
        ptd = [psbf(4 + i, 0) for i in range(4)]
        pt_tok = [None] * 4
        pwo_tok = None
        x_rd = None
        y_st = None
        l2_st = None
        lnT_guard = None
        d_ao_rd = None
        for tt in range(c.TT):
            b.wait("gpsimd", x_rd)
            xtok = b.dma("gpsimd", x2_t[:], x_in[bb, tt * P:(tt + 1) * P, :],
                         ch="x2")
            for t_ in wo_loaded:
                b.wait("tensor", t_)
            b.wait("tensor", pwo_tok)
            mm_tok = None
            for dc in range(c.DC):
                for seg in range(c.D // 512):
                    last = (dc == c.DC - 1 and seg == c.D // 512 - 1)
                    mm_tok = b.emit(
                        "tensor", lambda e, dc=dc, seg=seg, tt=tt: e.matmul(
                            pwo[:, seg * 512:(seg + 1) * 512],
                            oT[:, dc, tt * P:(tt + 1) * P],
                            woT[:, dc, seg * 512:(seg + 1) * 512],
                            start=(dc == 0), stop=(dc == c.DC - 1)),
                        inc="T" if last else None)
            b.wait("scalar", mm_tok)
            b.wait("scalar", d_ao_rd)
            atok = b.emit("scalar", lambda e: e.copy(atn_o[:], pwo))
            pwo_tok = atok
            b.wait("vector", atok)
            b.wait("scalar", atok)
            d_ao_rd = ln_full(atn_o[:], tmp_d[:], sq_d, c.D)
            b.wait("vector", xtok)
            b.wait("vector", y_st)
            vtok = b.emit("vector", lambda e: e.tensor_add(
                y_t[:], tmp_d[:], x2_t[:]))
            x_rd = vtok
            for t_ in fence([("vector", y_t[0:1, c.D - 16:c.D], 16)]):
                b.wait("gpsimd", t_)
            y_st = b.dma("gpsimd", y_h[bb, tt * P:(tt + 1) * P, :], y_t[:],
                         ch="yst")
            b.wait("scalar", vtok)
            v2 = ln_full(y_t[:], ln2bf[:], sq_d, c.D, extra_wait_v=lnT_guard)
            b.wait("tensor", v2)
            last_t = None
            for dc in range(c.DC):
                ps = dc % 4
                b.wait("tensor", pt_tok[ps])
                b.wait("tensor", l2_st)
                ttok = b.emit("tensor", lambda e, dc=dc, ps=ptd[ps]: e.transpose(
                    ps, ln2bf[:, dc * P:(dc + 1) * P], ident[:]), inc="T")
                last_t = ttok
                eng = "scalar" if dc % 2 == 0 else "vector"
                b.wait(eng, ttok)
                if eng == "scalar":
                    fn = (lambda e, dc=dc, ps=ptd[ps]: e.copy(
                        ln2T_t[:, dc, :], ps))
                else:
                    fn = (lambda e, dc=dc, ps=ptd[ps]: e.tensor_copy(
                        ln2T_t[:, dc, :], ps))
                pt_tok[ps] = b.emit(eng, fn)
            lnT_guard = last_t
            for t_ in fence([("scalar", ln2T_t[0:1, 0::2, P - 1], c.DC // 2),
                             ("vector", ln2T_t[0:1, 1::2, P - 1], c.DC // 2)]):
                b.wait("gpsimd", t_)
            l2_st = b.dma("gpsimd", ln2T_hp[:, :, tt * P:(tt + 1) * P],
                          ln2T_t[:], ch="l2st")
        b.barrier()

        for hf in range(2):
            t0 = hf * c.HTT
            lch = ["wab0", "wab1", "w2l0", "w2l1"]
            ltoks = [b.dma("sync", ln2Th[:, q * 4:(q + 1) * 4, :],
                           ln2T_hp[:, q * 4:(q + 1) * 4,
                                   hf * c.SH:(hf + 1) * c.SH], ch=lch[q])
                     for q in range(4)]
            segs = c.SH // 512
            px1 = [psf(0 * segs, c.SH), psf(1 * segs, c.SH)]
            px2 = [psf(2 * segs, c.SH), psf(3 * segs, c.SH)]
            px1_tok = [None, None]
            px2_tok = [None, None]
            w_rd = [None, None]
            sig_guard = None
            for hc in range(c.HC):
                wi = hc % 2
                b.wait("sync", w_rd[wi])
                l1 = b.dma("sync", w1sl[:, wi, :, :].rearrange("p a bq -> p (a bq)"),
                           w1T_h[hc], ch=f"wab{wi}")
                l2 = b.dma("sync", w2sl[:, wi, :, :].rearrange("p a bq -> p (a bq)"),
                           w2T_h[hc], ch=f"w2l{wi}")
                b.wait("tensor", l1)
                b.wait("tensor", l2)
                for t_ in ltoks:
                    b.wait("tensor", t_)
                b.wait("tensor", px1_tok[wi])
                b.wait("tensor", px2_tok[wi])
                mm_tok = None
                for dc in range(c.DC):
                    for seg in range(segs):
                        b.emit("tensor", lambda e, dc=dc, seg=seg, wi=wi:
                               e.matmul(px1[wi][:, seg * 512:(seg + 1) * 512],
                                        w1sl[:, wi, dc, :],
                                        ln2Th[:, dc, seg * 512:(seg + 1) * 512],
                                        start=(dc == 0), stop=(dc == c.DC - 1)))
                    for seg in range(segs):
                        last = (dc == c.DC - 1 and seg == segs - 1)
                        mm_tok = b.emit(
                            "tensor", lambda e, dc=dc, seg=seg, wi=wi: e.matmul(
                                px2[wi][:, seg * 512:(seg + 1) * 512],
                                w2sl[:, wi, dc, :],
                                ln2Th[:, dc, seg * 512:(seg + 1) * 512],
                                start=(dc == 0), stop=(dc == c.DC - 1)),
                            inc="T" if last else None)
                w_rd[wi] = mm_tok
                b.wait("scalar", mm_tok)
                b.wait("scalar", sig_guard)
                atok = b.emit("scalar", lambda e, wi=wi: e.activation(
                    sig[:, 0:c.SH], px1[wi], AF.Sigmoid, scale=0.5))
                px1_tok[wi] = atok
                b.wait("vector", atok)
                vtok = b.emit("vector", lambda e, hc=hc, wi=wi: e.tensor_mul(
                    hidT[:, hc, :], sig[:, 0:c.SH], px2[wi]))
                px2_tok[wi] = vtok
                sig_guard = vtok
            hid_done = ("V", b.sem["V"][1])
            e_guards = [w_rd[0], w_rd[1], b.dma_total("wab0"),
                        b.dma_total("wab1"), b.dma_total("w2l0"),
                        b.dma_total("w2l1")]
            wdq = c.D // c.DQ
            pfs = [psf(i, wdq) for i in range(c.HTT)]
            pf_tok = [None] * c.HTT
            w3_rd = [None, None, None, None]
            for dq in range(c.DQ):
                for hc in range(c.HC):
                    wi = hc % 4
                    b.wait("sync", w3_rd[wi])
                    l3 = b.dma("sync", w3sl[:, wi, :], w3T_h[dq, hc],
                               ch=f"w3{wi}")
                    b.wait("tensor", l3)
                    b.wait("tensor", hid_done)
                    mm_tok = None
                    for i in range(c.HTT):
                        if hc == 0:
                            b.wait("tensor", pf_tok[i])
                        mm_tok = b.emit(
                            "tensor", lambda e, i=i, hc=hc, wi=wi: e.matmul(
                                pfs[i], hidT[:, hc, i * P:(i + 1) * P],
                                w3sl[:, wi, :], start=(hc == 0),
                                stop=(hc == c.HC - 1)),
                            inc="T" if i == c.HTT - 1 else None)
                    w3_rd[wi] = mm_tok
                    if hc == c.HC - 1:
                        b.wait("scalar", mm_tok)
                        for i in range(c.HTT):
                            pf_tok[i] = b.emit(
                                "scalar", lambda e, i=i, dq=dq: e.activation(
                                    ffnbf[:, i, dq * wdq:(dq + 1) * wdq],
                                    pfs[i], AF.Copy, scale=0.25))
            ffn_done = ("A", b.sem["A"][1])
            y_rd = None
            for i in range(c.HTT):
                tt = t0 + i
                b.wait("gpsimd", y_rd)
                b.wait("gpsimd", b.dma_total("ost0"))
                b.wait("gpsimd", b.dma_total("ost1"))
                ytok = b.dma("gpsimd", y_e[:], y_h[bb, tt * P:(tt + 1) * P, :],
                             ch="yld")
                for e_ in ("vector", "scalar"):
                    b.wait(e_, ffn_done)
                    b.wait(e_, ytok)
                    for t_ in e_guards:
                        b.wait(e_, t_)
                ln_full(ffnbf[:, i, :], tmp_e[:], sq_e, c.D)
                vtok = b.emit("vector", lambda e: e.tensor_add(
                    y_e[:], tmp_e[:], y_e[:]))
                y_rd = vtok
                for t_ in fence([("vector", y_e[0:1, c.D - 16:c.D], 16)]):
                    b.wait("gpsimd", t_)
                b.dma("gpsimd", out[bb, tt * P:(tt + 1) * P, :], y_e[:],
                      ch=f"ost{i % 2}")
            b.barrier()
        b.barrier()

    b.final_dma_waits("gpsimd")
    b.final_dma_waits("sync")

    with nc.Block() as block:
        b.run(block)
    stack.close()
    return nc


def build_nc(cfg):
    nc = bass.Bass("TRN2", target_bir_lowering=False, debug=False)
    c = cfg
    ins = {
        "x": nc.dram_tensor("x", [c.NB, c.S, c.D], F32, kind="ExternalInput").ap(),
        "wq": nc.dram_tensor("wq", [c.D, c.D], F32, kind="ExternalInput").ap(),
        "wk": nc.dram_tensor("wk", [c.KV, c.D], F32, kind="ExternalInput").ap(),
        "wv": nc.dram_tensor("wv", [c.KV, c.D], F32, kind="ExternalInput").ap(),
        "wo": nc.dram_tensor("wo", [c.D, c.D], F32, kind="ExternalInput").ap(),
        "w1": nc.dram_tensor("w1", [c.HF, c.D], F32, kind="ExternalInput").ap(),
        "w2": nc.dram_tensor("w2", [c.HF, c.D], F32, kind="ExternalInput").ap(),
        "w3": nc.dram_tensor("w3", [c.D, c.HF], F32, kind="ExternalInput").ap(),
        "cos_c": nc.dram_tensor("cos_c", [c.HD, c.S], F32, kind="ExternalInput").ap(),
        "ssin_c": nc.dram_tensor("ssin_c", [c.HD, c.S], F32, kind="ExternalInput").ap(),
        "ident_c": nc.dram_tensor("ident_c", [P, P], BF16, kind="ExternalInput").ap(),
        "maskv_c": nc.dram_tensor("maskv_c", [c.QGT, P, c.QGT * P], BF16,
                                  kind="ExternalInput").ap(),
    }
    out = nc.dram_tensor("out", [c.NB, c.S, c.D], F32, kind="ExternalOutput").ap()
    build(nc, ins, out, cfg)
    return nc


def kernel(L0_attn_x, L0_attn_Wq, L0_attn_Wk, L0_attn_Wv, L0_attn_Wo,
           L0_ffn_w1, L0_ffn_w2, L0_ffn_w3):
    cfg = Cfg()
    consts = host_constants(cfg)
    x = np.ascontiguousarray(np.asarray(L0_attn_x, dtype=np.float32))
    shards = np.split(x, N_CORES, axis=0)
    common = {
        "wq": np.ascontiguousarray(np.asarray(L0_attn_Wq, np.float32)),
        "wk": np.ascontiguousarray(np.asarray(L0_attn_Wk, np.float32)),
        "wv": np.ascontiguousarray(np.asarray(L0_attn_Wv, np.float32)),
        "wo": np.ascontiguousarray(np.asarray(L0_attn_Wo, np.float32)),
        "w1": np.ascontiguousarray(np.asarray(L0_ffn_w1, np.float32)),
        "w2": np.ascontiguousarray(np.asarray(L0_ffn_w2, np.float32)),
        "w3": np.ascontiguousarray(np.asarray(L0_ffn_w3, np.float32)),
        **consts,
    }
    nc = build_nc(cfg)
    in_maps = [{"x": s, **common} for s in shards]
    res = run_bass_kernel_spmd(nc, in_maps, core_ids=list(range(N_CORES)))
    return np.concatenate([np.asarray(r["out"], np.float32) for r in res.results],
                          axis=0)
